# version 17
# baseline (speedup 1.0000x reference)
from contextlib import ExitStack

import numpy as np

import concourse.bass as bass
import concourse.mybir as mybir
from concourse.bass_utils import run_bass_kernel_spmd

F32 = mybir.dt.float32
BF16 = mybir.dt.bfloat16
ALU = mybir.AluOpType
ACT = mybir.ActivationFunctionType
AX = mybir.AxisListType

MAGIC = float(np.float32(1.5 * 2**23))
QMAX = 32767.0
EPS = 1e-5

D, DI, NS, RK, KC, HI = 1024, 2048, 16, 64, 4, 3072
NKC, NFC, NIC = DI // 128, D // 128, HI // 128

ENGINES = ("sync", "vector", "scalar", "tensor", "gpsimd")
DBG = {"no_collective": False, "no_r2s": False, "no_bcast": False}


class Ctx:

    def __init__(self, nc, stack):
        self.nc = nc
        self.q = {e: [] for e in ENGINES}
        self.sem = {}
        self.cnt = {}
        self.dma_lanes = ("dh", "dx0", "dx1", "dw0", "dw1", "db0", "db1",
                          "ds0", "ds1", "du0", "du1", "dr0", "dr1",
                          "dt0", "dt1", "dst", "dp0", "dp1", "dg")
        for name in self.dma_lanes + ("sv", "ss", "stn", "sg", "cc"):
            self.sem[name] = stack.enter_context(nc.semaphore(name))
            self.cnt[name] = 0
        self.guards = {}
        self.eng_sem = {"vector": "sv", "scalar": "ss", "tensor": "stn",
                        "gpsimd": "sg"}

    def wait(self, eng, name, val=None):
        if val is None:
            val = self.cnt[name]
        if val <= 0:
            return
        h = self.sem[name]
        self.q[eng].append(lambda e, h=h, v=val: e.wait_ge(h, v))

    def run(self, eng, fn, inc=None):
        if inc is None:
            self.q[eng].append(fn)
            return None
        step = 16 if inc in self.dma_lanes else 1
        self.cnt[inc] += step
        h = self.sem[inc]
        self.q[eng].append(lambda e, fn=fn, h=h, s=step: fn(e).then_inc(h, s))
        return self.cnt[inc]

    def v(self, fn):
        return self.run("vector", fn, "sv")

    def s(self, fn):
        return self.run("scalar", fn, "ss")

    def t(self, fn):
        return self.run("tensor", fn, "stn")

    def g(self, fn):
        return self.run("gpsimd", fn, "sg")

    def dma(self, out, in_, eng="sync", slow=False, lane=None):
        sem = lane or ("dg" if eng == "gpsimd" else "dh")
        return self.run(eng, lambda e, o=out, i=in_, s=slow: e.dma_start(
            out=o, in_=i, allow_slow_non_contiguous=s), sem)

    def mark(self, *names):
        if not names:
            names = ("sv", "ss", "stn", "sg", "dh", "dg")
        return {n: self.cnt[n] for n in names}

    def wait_marks(self, eng, marks):
        for name, val in marks.items():
            if val > 0:
                self.wait(eng, name, val)

    def use(self, eng, key):
        m = self.guards.get(key)
        if m:
            self.wait_marks(eng, m)

    def setg(self, key, *names):
        self.guards[key] = self.mark(*names)

    def selfw(self, eng):
        self.wait(eng, self.eng_sem[eng])

    def lanewait(self, eng, *lanes):
        for ln in lanes:
            self.wait(eng, ln)

    def barrier(self):
        for eng, sname in (("vector", "sv"), ("scalar", "ss"),
                           ("tensor", "stn"), ("gpsimd", "sg")):
            self.run(eng, lambda e: e.nop(), sname)
        m = {n: self.cnt[n] for n in
             ("sv", "ss", "stn", "sg") + self.dma_lanes}
        for eng in ENGINES:
            self.wait_marks(eng, m)


def build_nc(L=2048, n_cores=8, taps=False, cut=99):
    TH = L // 2
    NTC = TH // 128
    TTL = min(512, TH)
    NTT = TH // TTL
    TL2 = min(512, L)
    NT2 = L // TL2
    YB = max(1, TH // 512)
    LNB = min(2, NTC)

    nc = bass.Bass("TRN2", target_bir_lowering=False,
                   detect_race_conditions=False)

    xT = nc.dram_tensor("x", [L, D], F32, kind="ExternalInput").ap()
    g_ln = nc.dram_tensor("ln_g", [D], F32, kind="ExternalInput").ap()
    b_ln = nc.dram_tensor("ln_b", [D], F32, kind="ExternalInput").ap()
    w_in = nc.dram_tensor("w_in", [D, 2 * DI], F32, kind="ExternalInput").ap()
    b_in = nc.dram_tensor("b_in", [2 * DI], F32, kind="ExternalInput").ap()
    conv_w = nc.dram_tensor("conv_w", [KC, DI], F32, kind="ExternalInput").ap()
    conv_b = nc.dram_tensor("conv_b", [DI], F32, kind="ExternalInput").ap()
    w_x = nc.dram_tensor("w_x", [DI, RK + 2 * NS], F32, kind="ExternalInput").ap()
    w_dt = nc.dram_tensor("w_dt", [RK, DI], F32, kind="ExternalInput").ap()
    b_dt = nc.dram_tensor("b_dt", [DI], F32, kind="ExternalInput").ap()
    a_log = nc.dram_tensor("a_log", [DI, NS], F32, kind="ExternalInput").ap()
    d_vec = nc.dram_tensor("d_vec", [DI], F32, kind="ExternalInput").ap()
    w_out = nc.dram_tensor("w_out", [DI, D], F32, kind="ExternalInput").ap()
    b_out = nc.dram_tensor("b_out", [D], F32, kind="ExternalInput").ap()
    w_gate = nc.dram_tensor("w_gate", [HI, D], F32, kind="ExternalInput").ap()
    w_up = nc.dram_tensor("w_up", [HI, D], F32, kind="ExternalInput").ap()
    w_down = nc.dram_tensor("w_down", [D, HI], F32, kind="ExternalInput").ap()
    outT = nc.dram_tensor("out", [L, D], F32, kind="ExternalOutput").ap()

    wkind = "ExternalOutput" if taps else "Internal"
    win_h = nc.dram_tensor("win_h", [D, 2 * DI], BF16).ap()
    wout_h = nc.dram_tensor("wout_h", [DI, D], BF16).ap()
    wupT_h = nc.dram_tensor("wupT_h", [D, HI], BF16, kind=wkind).ap()
    wgtT_h = nc.dram_tensor("wgtT_h", [D, HI], BF16, kind=wkind).ap()
    wdnT_h = nc.dram_tensor("wdnT_h", [HI, D], BF16, kind=wkind).ap()
    sres_h = nc.dram_tensor("sres_h", [DI, L], BF16).ap()
    spkind = "ExternalOutput" if taps else "Internal"
    sp_up = nc.dram_tensor("sp_up", [HI, L], F32, kind=spkind).ap()
    sp_gate = nc.dram_tensor("sp_gate", [HI, L], F32, kind=spkind).ap()
    sp_g2 = nc.dram_tensor("sp_g2", [HI, L], F32, kind=spkind).ap()
    sp_o = nc.dram_tensor("sp_o", [HI, L], F32, kind=spkind).ap()
    x1_h = nc.dram_tensor("x1_h", [L, D], F32).ap()
    bc_h = nc.dram_tensor("bc_h", [2 * NS, L], BF16).ap()
    sc_h = nc.dram_tensor("sc_h", [4, 32], F32,
                          kind="ExternalOutput" if taps else "Internal").ap()
    r2s_h = nc.dram_tensor("r2s_h", [16, 128], F32,
                           kind="ExternalOutput" if taps else "Internal").ap()
    cc_bufs = [(nc.dram_tensor("cci%d" % i, [1, 8], F32).ap(),
                nc.dram_tensor("cco%d" % i, [1, 8], F32).ap())
               for i in range(4)]
    scb_h = (nc.dram_tensor("scb_h", [128, 12], F32, kind="ExternalOutput").ap()
             if taps else None)
    xq_h = (nc.dram_tensor("xq_h", [128, NFC * L], BF16,
                           kind="ExternalOutput").ap() if taps else None)
    tap = {}
    if taps:
        for nm, shp in (("tap_u", [DI, L]), ("tap_delta", [DI, L]),
                        ("tap_ymod", [DI, L]), ("tap_x1", [L, D])):
            tap[nm] = nc.dram_tensor(nm, shp, F32, kind="ExternalOutput").ap()

    stack = ExitStack()
    with stack:
        c = Ctx(nc, stack)

        def sb(name, shape, dt=F32):
            return stack.enter_context(nc.sbuf_tensor(name, shape, dt))

        pt = [stack.enter_context(nc.psum_tensor("ps%d" % i, [128, 512], F32))
              for i in range(8)]

        A_sb = sb("A_sb", [128, NKC * NS])
        cw_sb = sb("cw_sb", [128, KC * NKC])
        cb_sb = sb("cb_sb", [128, NKC])
        dv_sb = sb("dv_sb", [128, NKC])
        bdt_sb = sb("bdt_sb", [128, NKC])
        bin_sb = sb("bin_sb", [128, 2 * NKC])
        gam_sb = sb("gam_sb", [128, NFC])
        bet_sb = sb("bet_sb", [128, NFC])
        bout_b = sb("bout_b", [128, D], BF16)
        I_bf = sb("I_bf", [128, 128], BF16)
        I_f32 = sb("I_f32", [128, 128])
        state = sb("state", [128, NKC * NS])
        scal = sb("scal", [1, 32])
        scb = sb("scb", [128, 12])
        sca = sb("sca", [128, 8])
        scg = sb("scg", [128, 8])
        zero8 = sb("zero8", [1, 8])
        wx_bf = sb("wx_bf", [128, NKC * 128], BF16)
        wdt_bf = sb("wdt_bf", [64, DI], BF16)
        stat = sb("stat", [128, 64])
        sqd = sb("sqd", [128, D])
        mxu = sb("mxu", [128, max(16, NIC * NT2)])
        mxg = sb("mxg", [128, NIC * NT2])
        mx4 = sb("mx4", [128, NIC * NT2])
        mx5 = sb("mx5", [128, NIC * NT2])

        c.dma(A_sb[:, :].rearrange("p (k n) -> p k n", n=NS),
              a_log.rearrange("(k p) n -> p k n", p=128))
        c.dma(cw_sb[:, :].rearrange("p (i k) -> p i k", i=KC),
              conv_w.rearrange("i (k p) -> p i k", p=128), slow=True)
        c.dma(cb_sb[:, :], conv_b.rearrange("(k p) -> p k", p=128), slow=True)
        c.dma(dv_sb[:, :], d_vec.rearrange("(k p) -> p k", p=128), slow=True)
        c.dma(bdt_sb[:, :], b_dt.rearrange("(k p) -> p k", p=128), slow=True)
        c.dma(bin_sb[:, :], b_in.rearrange("(k p) -> p k", p=128), slow=True)
        c.dma(gam_sb[:, :], g_ln.rearrange("(k p) -> p k", p=128), slow=True)
        c.dma(bet_sb[:, :], b_ln.rearrange("(k p) -> p k", p=128), slow=True)
        c.dma(bout_b[:, :],
              b_out.rearrange("(o f) -> o f", o=1).broadcast_to((128, D)),
              eng="gpsimd")
        c.g(lambda e: e.memset(wx_bf[:, :], 0.0))
        c.wait("gpsimd", "sg")
        c.dma(wx_bf[:, :].rearrange("p (k j) -> p k j", j=128)[:, :, 0:RK],
              w_x.rearrange("(k p) j -> p k j", p=128)[:, :, 0:RK],
              eng="gpsimd")
        c.dma(wx_bf[:, :].rearrange("p (k j) -> p k j", j=128)[:, :, 64:64 + NS],
              w_x.rearrange("(k p) j -> p k j", p=128)[:, :, RK:RK + NS],
              eng="gpsimd")
        c.dma(wx_bf[:, :].rearrange("p (k j) -> p k j", j=128)[:, :, 96:96 + NS],
              w_x.rearrange("(k p) j -> p k j", p=128)[:, :, RK + NS:RK + 2 * NS],
              eng="gpsimd")
        c.dma(wdt_bf[0:64, :], w_dt[:, :], eng="gpsimd")
        c.wait("scalar", "dh")
        c.s(lambda e: e.activation(A_sb[:, :], A_sb[:, :], ACT.Exp))
        c.wait("vector", "ss")
        c.v(lambda e: e.tensor_scalar_mul(bdt_sb[:, :], bdt_sb[:, :], -1.0))
        c.g(lambda e: e.iota(I_f32[:, :], [[1, 128]], base=0,
                             channel_multiplier=0,
                             allow_small_or_imprecise_dtypes=True))
        c.g(lambda e: e.iota(scb[:, 9:10], [[0, 1]], base=0,
                             channel_multiplier=1,
                             allow_small_or_imprecise_dtypes=True))
        mg = c.mark("sg")
        c.wait_marks("vector", mg)
        c.v(lambda e: e.tensor_tensor(I_f32[:, :], I_f32[:, :],
                                      scb[:, 9:10].broadcast_to((128, 128)),
                                      ALU.is_equal))
        c.selfw("vector")
        c.v(lambda e: e.tensor_copy(I_bf[:, :], I_f32[:, :]))
        c.g(lambda e: e.memset(state[:, :], 0.0))
        c.g(lambda e: e.memset(zero8[:, :], 0.0))
        c.g(lambda e: e.memset(scal[:, :], 0.0))
        c.g(lambda e: e.memset(scb[:, 10:12], 0.0))
        c.g(lambda e: e.memset(sca[:, :], 0.0))
        c.g(lambda e: e.memset(scg[:, :], 0.0))
        c.barrier()

        r2s_idx = [0]

        def reduce_to_all(src_ap, dst_col):
            idx = r2s_idx[0] % 16
            r2s_idx[0] += 1
            c.selfw("vector")
            c.use("vector", "stat63")
            c.v(lambda e: e.tensor_reduce(stat[:, 63:64], src_ap, AX.X,
                                          ALU.max, apply_absolute_value=True))
            mv = c.mark("sv")
            c.wait_marks("sync", mv)
            c.dma(r2s_h[idx, :].rearrange("(p o) -> p o", o=1),
                  stat[:, 63:64])
            c.setg("stat63", "dh")
            c.lanewait("sync", "dh")
            c.dma(sqd[:, 0:128],
                  r2s_h[idx, :].rearrange("(o f) -> o f", o=1)
                  .broadcast_to((128, 128)))
            c.lanewait("vector", "dh")
            c.v(lambda e, d=dst_col: e.tensor_reduce(
                sca[:, d:d + 1], sqd[:, 0:128], AX.X, ALU.max,
                apply_absolute_value=True))
            c.selfw("vector")

        def collective(idx, n_in):
            cci, cco = cc_bufs[idx]
            if DBG["no_collective"]:
                c.v(lambda e, n=n_in: e.tensor_copy(scg[:, 0:n],
                                                    sca[:, 0:n]))
                c.selfw("vector")
                return
            mv = c.mark("sv")
            c.wait_marks("sync", mv)
            c.dma(cci[0:1, 0:8], zero8[0:1, 0:8])
            c.lanewait("sync", "dh")
            c.dma(cci[0:1, 0:n_in], sca[0:1, 0:n_in])
            c.lanewait("gpsimd", "dh")
            c.run("gpsimd", lambda e: e.collective_compute(
                "AllReduce", ALU.max,
                replica_groups=[list(range(n_cores))],
                ins=[cci.opt()], outs=[cco.opt()]), "cc")
            c.wait("sync", "cc")
            c.dma(scg[:, 0:n_in],
                  cco[0:1, 0:n_in].broadcast_to((128, n_in)))
            c.lanewait("vector", "dh")

        def ln_rstd(n):
            c.selfw("scalar")
            c.s(lambda e: e.nop())
            c.wait_marks("vector", c.mark("ss"))
            c.selfw("vector")
            c.v(lambda e: e.tensor_scalar_mul(stat[:, 16:16 + n],
                                              stat[:, 0:n], 1.0 / D))
            c.v(lambda e: e.tensor_scalar_mul(stat[:, 32:32 + n],
                                              stat[:, 8:8 + n], 1.0 / D))
            c.selfw("vector")
            c.v(lambda e: e.tensor_tensor(stat[:, 40:40 + n],
                                          stat[:, 16:16 + n],
                                          stat[:, 16:16 + n], ALU.mult))
            c.selfw("vector")
            c.v(lambda e: e.tensor_tensor(stat[:, 40:40 + n],
                                          stat[:, 32:32 + n],
                                          stat[:, 40:40 + n], ALU.subtract))
            c.selfw("vector")
            c.v(lambda e: e.tensor_scalar_add(stat[:, 40:40 + n],
                                              stat[:, 40:40 + n], EPS))
            mv = c.mark("sv")
            c.wait_marks("scalar", mv)
            c.s(lambda e: e.activation(stat[:, 48:48 + n],
                                       stat[:, 40:40 + n], ACT.Sqrt))
            c.selfw("scalar")
            c.s(lambda e: e.nop())
            ms = c.mark("ss")
            c.wait_marks("vector", ms)
            c.v(lambda e: e.reciprocal(stat[:, 24:24 + n], stat[:, 48:48 + n]))
            c.selfw("vector")
            c.v(lambda e: e.tensor_tensor(stat[:, 56:56 + n],
                                          stat[:, 40:40 + n],
                                          stat[:, 24:24 + n], ALU.mult))
            c.selfw("vector")
            c.v(lambda e: e.tensor_tensor(stat[:, 56:56 + n],
                                          stat[:, 56:56 + n],
                                          stat[:, 24:24 + n], ALU.mult))
            c.selfw("vector")
            c.v(lambda e: e.tensor_scalar(stat[:, 56:56 + n],
                                          stat[:, 56:56 + n], -0.5, 1.5,
                                          ALU.mult, ALU.add))
            c.selfw("vector")
            c.v(lambda e: e.tensor_tensor(stat[:, 24:24 + n],
                                          stat[:, 24:24 + n],
                                          stat[:, 56:56 + n], ALU.mult))
            c.selfw("vector")

        class _Cut(Exception):
            pass

        try:
            p0scope = ExitStack()
            with p0scope:
                _e = p0scope.enter_context
                raw = _e(nc.sbuf_tensor("raw", [128, 4 * 4096], F32))
                dqs = _e(nc.sbuf_tensor("dqs", [128, 4 * 128], F32))
                dqr = _e(nc.sbuf_tensor("dqr", [128, 4 * 128], F32))
                trb = _e(nc.sbuf_tensor("trb", [128, 8 * 128], BF16))
                cstb = _e(nc.sbuf_tensor("cstb", [128, 2 * 4096], BF16))
                p0lanes = ("dw0", "dw1", "dx0", "dx1",
                           "db0", "db1", "ds0", "ds1")
                casts = [("c", (w_in, win_h, rc, 2 * DI)) for rc in range(NFC)] + \
                        [("c", (w_out, wout_h, rc, D)) for rc in range(NKC)]
                groups = [("g", (w_up, wupT_h, 4 * g, 4, D)) for g in range(6)] + \
                         [("g", (w_gate, wgtT_h, 4 * g, 4, D)) for g in range(6)] + \
                         [("g", (w_down, wdnT_h, oc, 1, HI)) for oc in range(8)]
                seq = []
                while casts or groups:
                    if casts:
                        seq.append(casts.pop(0))
                    if groups:
                        seq.append(groups.pop(0))
                tri = 0
                for idx, (kind, item) in enumerate(seq):
                    s4 = idx % 4
                    ldl = p0lanes[idx % 8]
                    if kind == "c":
                        src, dst, rc, NCOL = item
                        sl = idx % 2
                        c.use("sync", ("p0raw", s4))
                        c.dma(raw[:, s4 * 4096: s4 * 4096 + NCOL],
                              src[rc * 128:(rc + 1) * 128, :], lane=ldl)
                        c.lanewait("vector", ldl)
                        c.use("vector", ("p0cst", sl))
                        c.v(lambda e, s4=s4, sl=sl, NCOL=NCOL: e.tensor_copy(
                            cstb[:, sl * 4096: sl * 4096 + NCOL],
                            raw[:, s4 * 4096: s4 * 4096 + NCOL]))
                        c.setg(("p0raw", s4), "sv")
                        mv = c.mark("sv")
                        c.wait_marks("sync", mv)
                        ln = "dp%d" % sl
                        c.dma(dst[rc * 128:(rc + 1) * 128, :],
                              cstb[:, sl * 4096: sl * 4096 + NCOL], lane=ln)
                        c.guards[("p0cst", sl)] = {ln: c.cnt[ln]}
                        continue
                    wsrc, wdst, oc0, GRP, I = item
                    gw = GRP * I
                    NB = gw // 32
                    rawt = raw[:, s4 * 4096: s4 * 4096 + gw]
                    sct = dqs[:, s4 * 128: s4 * 128 + NB]
                    rct = dqr[:, s4 * 128: s4 * 128 + NB]
                    c.use("sync", ("p0raw", s4))
                    if GRP == 1:
                        c.dma(rawt, wsrc[oc0 * 128:(oc0 + 1) * 128, :],
                              lane=ldl)
                    else:
                        c.dma(rawt.rearrange("p (o i) -> p o i", o=GRP),
                              wsrc[oc0 * 128:(oc0 + GRP) * 128, :]
                              .rearrange("(o p) i -> p o i", p=128),
                              lane=ldl)
                    c.lanewait("vector", ldl)
                    c.use("vector", ("p0dq", s4))
                    c.v(lambda e, r=rawt, s=sct: e.tensor_reduce(
                        s, r.rearrange("p (b k) -> p b k", k=32), AX.X,
                        ALU.max, apply_absolute_value=True))
                    c.selfw("vector")
                    c.v(lambda e, s=sct: e.tensor_scalar(
                        s, s, 1.0 / 7.0, 1e-12, ALU.mult, ALU.max))
                    c.selfw("vector")
                    c.v(lambda e, s=sct, r=rct: e.reciprocal(r, s))
                    c.selfw("vector")
                    c.v(lambda e, r=rawt, rc_=rct, NB=NB: e.tensor_tensor(
                        r.rearrange("p (b k) -> p b k", k=32),
                        r.rearrange("p (b k) -> p b k", k=32),
                        rc_.rearrange("p (b o) -> p b o", o=1)
                        .broadcast_to((128, NB, 32)), ALU.mult))
                    c.v(lambda e, r=rawt: e.tensor_scalar(
                        r, r, MAGIC, MAGIC, ALU.add, ALU.subtract))
                    c.v(lambda e, r=rawt, s=sct, NB=NB: e.tensor_tensor(
                        r.rearrange("p (b k) -> p b k", k=32),
                        r.rearrange("p (b k) -> p b k", k=32),
                        s.rearrange("p (b o) -> p b o", o=1)
                        .broadcast_to((128, NB, 32)), ALU.mult))
                    mwq = c.mark("sv")
                    c.setg(("p0dq", s4), "sv")
                    NBT = I // 128
                    for b in range(gw // 128):
                        ocl, bb = divmod(b, NBT)
                        psl = tri % 8
                        tsl = tri % 8
                        c.wait_marks("tensor", mwq)
                        c.use("tensor", ("ps", psl))
                        c.t(lambda e, b=b, w=rawt, psl=psl: e.transpose(
                            pt[psl][:, 0:128], w[:, b * 128:(b + 1) * 128],
                            I_f32[:, :]))
                        mt = c.mark("stn")
                        c.wait_marks("scalar", mt)
                        c.use("scalar", ("p0trb", tsl))
                        c.s(lambda e, psl=psl, tsl=tsl: e.activation(
                            trb[:, tsl * 128:(tsl + 1) * 128],
                            pt[psl][:, 0:128], ACT.Copy))
                        c.setg(("ps", psl), "ss")
                        ms = c.mark("ss")
                        c.wait_marks("sync", ms)
                        ln = ("dt0", "dt1", "dst", "dh")[tsl % 4]
                        c.dma(wdst[bb * 128:(bb + 1) * 128,
                                   (oc0 + ocl) * 128:(oc0 + ocl + 1) * 128],
                              trb[:, tsl * 128:(tsl + 1) * 128], lane=ln)
                        c.guards[("p0trb", tsl)] = {ln: c.cnt[ln]}
                        tri += 1
                    c.setg(("p0raw", s4), "stn")
            c.barrier()
            if cut == 2:
                raise _Cut()

            mscope = ExitStack()
            with mscope:
                _e = mscope.enter_context
                xbuf = _e(nc.sbuf_tensor("xbuf", [128, LNB * D], F32))
                xnp = _e(nc.sbuf_tensor("xnp", [128, D], F32))
                xn_fm = _e(nc.sbuf_tensor("xn_fm", [128, NFC * TH], BF16))
                u0 = _e(nc.sbuf_tensor("u0", [128, NKC * (TH + 3)], BF16))
                delta = _e(nc.sbuf_tensor("delta", [128, NKC * TH], BF16))
                dt_sb = _e(nc.sbuf_tensor("dt_sb", [64, TH], BF16))
                bcB_sb = _e(nc.sbuf_tensor("bcB_sb", [16, TH], BF16))
                bcC_sb = _e(nc.sbuf_tensor("bcC_sb", [16, TH], BF16))
                wint = _e(nc.sbuf_tensor("wint", [128, 2 * 1024], BF16))
                srstg = _e(nc.sbuf_tensor("srstg", [128, 2 * TTL], BF16))
                acc2 = _e(nc.sbuf_tensor("acc2", [128, TH], BF16))
                tails = _e(nc.sbuf_tensor("tails", [128, NKC * 3], BF16))
                gkt = _e(nc.sbuf_tensor("gkt", [128, 4 * TH], BF16))
                srkg = _e(nc.sbuf_tensor("srkg", [128, 2 * TH], BF16))
                Bb = _e(nc.sbuf_tensor("Bb", [128, 3 * TH], BF16))
                Cb = _e(nc.sbuf_tensor("Cb", [128, 3 * TH], BF16))
                ab = _e(nc.sbuf_tensor("ab", [128, 3 * TH], F32))
                bb = _e(nc.sbuf_tensor("bb", [128, 3 * TH], BF16))
                hb = _e(nc.sbuf_tensor("hb", [128, 2 * TH], BF16))
                wout_bf = _e(nc.sbuf_tensor("wout_bf", [128, NKC * D], BF16))
                xb2 = _e(nc.sbuf_tensor("xb2", [128, D], F32))

                def u0k(kc):
                    return u0[:, kc * (TH + 3): kc * (TH + 3) + TH + 3]

                for h in range(2):
                    t0 = h * TH
                    for tcg in range(NTC // LNB):
                        for tc2 in range(LNB):
                            tc = tcg * LNB + tc2
                            c.use("sync", ("xbuf", tc2))
                            c.dma(xbuf[:, tc2 * D:(tc2 + 1) * D],
                                  xT[t0 + tc * 128: t0 + (tc + 1) * 128, :],
                                  lane="dx%d" % tc2)
                            c.lanewait("vector", "dx%d" % tc2)
                            c.v(lambda e, tc2=tc2: e.tensor_reduce(
                                stat[:, tc2:tc2 + 1],
                                xbuf[:, tc2 * D:(tc2 + 1) * D], AX.X, ALU.add))
                            c.lanewait("scalar", "dx%d" % tc2)
                            c.s(lambda e, tc2=tc2: e.activation(
                                sqd[:, :], xbuf[:, tc2 * D:(tc2 + 1) * D],
                                ACT.Square, accum_out=stat[:, 8 + tc2: 9 + tc2]))
                        ms = c.mark("ss")
                        c.wait_marks("vector", ms)
                        ln_rstd(LNB)
                        for tc2 in range(LNB):
                            tc = tcg * LNB + tc2
                            c.use("vector", ("xnp", 0))
                            c.v(lambda e, tc2=tc2: e.tensor_scalar(
                                xnp[:, :],
                                xbuf[:, tc2 * D:(tc2 + 1) * D],
                                stat[:, 16 + tc2:17 + tc2],
                                stat[:, 24 + tc2:25 + tc2],
                                ALU.subtract, ALU.mult))
                            c.setg(("xbuf", tc2), "sv")
                            mv = c.mark("sv")
                            for fc in range(NFC):
                                psl = 6 + fc % 2
                                c.wait_marks("tensor", mv)
                                c.use("tensor", ("ps", psl))
                                c.t(lambda e, fc=fc, psl=psl: e.transpose(
                                    pt[psl][:, 0:128],
                                    xnp[:, fc * 128:(fc + 1) * 128],
                                    I_f32[:, :]))
                                mt = c.mark("stn")
                                c.wait_marks("vector", mt)
                                c.v(lambda e, fc=fc, tc=tc, psl=psl: e.tensor_scalar(
                                    xn_fm[:, fc * TH + tc * 128:
                                          fc * TH + (tc + 1) * 128],
                                    pt[psl][:, 0:128], gam_sb[:, fc:fc + 1],
                                    bet_sb[:, fc:fc + 1], ALU.mult, ALU.add))
                                c.setg(("ps", psl), "sv")
                            c.setg(("xnp", 0), "stn")
                    mxn = c.mark("sv")
                    for jc in range(2 * NKC):
                        wsl = jc % 2
                        c.use("sync", ("wint", wsl))
                        c.dma(wint[:, wsl * 1024:(wsl + 1) * 1024]
                              .rearrange("p (k j) -> p k j", k=NFC),
                              win_h[:, jc * 128:(jc + 1) * 128]
                              .rearrange("(k p) j -> p k j", p=128),
                              lane="dw%d" % wsl)
                        for tt in range(NTT):
                            psl = (jc * NTT + tt) % 4
                            c.use("tensor", ("ps", psl))
                            c.lanewait("tensor", "dw%d" % wsl)
                            c.wait_marks("tensor", mxn)
                            for kc in range(NFC):
                                c.t(lambda e, kc=kc, tt=tt, wsl=wsl, psl=psl:
                                    e.matmul(pt[psl][:, 0:TTL],
                                             wint[:, wsl * 1024 + kc * 128:
                                                  wsl * 1024 + (kc + 1) * 128],
                                             xn_fm[:, kc * TH + tt * TTL:
                                                   kc * TH + (tt + 1) * TTL],
                                             start=(kc == 0),
                                             stop=(kc == NFC - 1)))
                            mt = c.mark("stn")
                            c.wait_marks("scalar", mt)
                            if jc < NKC:
                                c.s(lambda e, jc=jc, tt=tt, psl=psl: e.activation(
                                    u0[:, jc * (TH + 3) + 3 + tt * TTL:
                                       jc * (TH + 3) + 3 + (tt + 1) * TTL],
                                    pt[psl][:, 0:TTL], ACT.Identity,
                                    bias=bin_sb[:, jc:jc + 1]))
                                c.setg(("ps", psl), "ss")
                            else:
                                ssl = (jc * NTT + tt) % 2
                                c.use("scalar", ("srstg", ssl))
                                c.s(lambda e, jc=jc, psl=psl, ssl=ssl: e.activation(
                                    srstg[:, ssl * TTL:(ssl + 1) * TTL],
                                    pt[psl][:, 0:TTL], ACT.Silu,
                                    bias=bin_sb[:, jc:jc + 1]))
                                c.setg(("ps", psl), "ss")
                                msl = c.mark("ss")
                                c.wait_marks("sync", msl)
                                lnr = "dr%d" % ssl
                                c.dma(sres_h[(jc - NKC) * 128:
                                             (jc - NKC + 1) * 128,
                                             t0 + tt * TTL:
                                             t0 + (tt + 1) * TTL],
                                      srstg[:, ssl * TTL:(ssl + 1) * TTL],
                                      lane=lnr)
                                c.guards[("srstg", ssl)] = {lnr: c.cnt[lnr]}
                        c.setg(("wint", wsl), "stn")
                    msu = c.mark("ss")
                    c.wait_marks("vector", {"ss": msu["ss"]})
                    if h == 0:
                        for kc in range(NKC):
                            c.v(lambda e, kc=kc: e.memset(u0k(kc)[:, 0:3], 0.0))
                    else:
                        for kc in range(NKC):
                            c.v(lambda e, kc=kc: e.tensor_copy(
                                u0k(kc)[:, 0:3], tails[:, kc * 3:kc * 3 + 3]))
                    c.selfw("vector")
                    for kc in range(NKC):
                        if h == 0:
                            c.v(lambda e, kc=kc: e.tensor_copy(
                                tails[:, kc * 3:kc * 3 + 3],
                                u0k(kc)[:, TH:TH + 3]))
                        c.v(lambda e, kc=kc: e.tensor_scalar(
                            acc2[:, 0:TH], u0k(kc)[:, 0:TH],
                            cw_sb[:, 0 * NKC + kc:0 * NKC + kc + 1],
                            cb_sb[:, kc:kc + 1], ALU.mult, ALU.add))
                        c.v(lambda e, kc=kc: e.scalar_tensor_tensor(
                            acc2[:, 0:TH], u0k(kc)[:, 1:TH + 1],
                            cw_sb[:, 1 * NKC + kc:1 * NKC + kc + 1],
                            acc2[:, 0:TH], ALU.mult, ALU.add))
                        c.v(lambda e, kc=kc: e.scalar_tensor_tensor(
                            acc2[:, 0:TH], u0k(kc)[:, 2:TH + 2],
                            cw_sb[:, 2 * NKC + kc:2 * NKC + kc + 1],
                            acc2[:, 0:TH], ALU.mult, ALU.add))
                        c.v(lambda e, kc=kc: e.scalar_tensor_tensor(
                            u0k(kc)[:, 3:TH + 3], u0k(kc)[:, 3:TH + 3],
                            cw_sb[:, 3 * NKC + kc:3 * NKC + kc + 1],
                            acc2[:, 0:TH], ALU.mult, ALU.add))
                        mv = c.mark("sv")
                        c.wait_marks("scalar", mv)
                        c.s(lambda e, kc=kc: e.activation(
                            u0k(kc)[:, 3:TH + 3], u0k(kc)[:, 3:TH + 3], ACT.Silu))
                    if taps:
                        c.wait_marks("gpsimd", c.mark("ss"))
                        for kc in range(NKC):
                            c.dma(tap["tap_u"][kc * 128:(kc + 1) * 128,
                                               t0:t0 + TH], u0k(kc)[:, 3:TH + 3],
                                  eng="gpsimd")
                    msu = c.mark("ss")
                    c.wait_marks("tensor", msu)
                    for tt in range(NTT):
                        psl = 4 + tt % 2
                        c.use("tensor", ("ps", psl))
                        for kc in range(NKC):
                            c.t(lambda e, kc=kc, tt=tt, psl=psl: e.matmul(
                                pt[psl][:, 0:TTL],
                                wx_bf[:, kc * 128:(kc + 1) * 128],
                                u0k(kc)[:, 3 + tt * TTL: 3 + (tt + 1) * TTL],
                                start=(kc == 0), stop=(kc == NKC - 1)))
                        mt = c.mark("stn")
                        c.wait_marks("vector", mt)
                        c.v(lambda e, tt=tt, psl=psl: e.tensor_copy(
                            dt_sb[0:64, tt * TTL:(tt + 1) * TTL],
                            pt[psl][0:64, 0:TTL]))
                        c.v(lambda e, tt=tt, psl=psl: e.tensor_scalar_mul(
                            bcB_sb[0:NS, tt * TTL:(tt + 1) * TTL],
                            pt[psl][64:64 + NS, 0:TTL], -1.0))
                        c.v(lambda e, tt=tt, psl=psl: e.tensor_copy(
                            bcC_sb[0:NS, tt * TTL:(tt + 1) * TTL],
                            pt[psl][96:96 + NS, 0:TTL]))
                        c.setg(("ps", psl), "sv")
                    c.wait_marks("sync", c.mark("sv"))
                    c.dma(bc_h[0:NS, t0:t0 + TH], bcB_sb[0:NS, :])
                    c.dma(bc_h[NS:2 * NS, t0:t0 + TH], bcC_sb[0:NS, :])
                    mdt = c.mark("sv")
                    c.wait_marks("tensor", mdt)
                    for jc in range(NKC):
                        for tt in range(NTT):
                            psl = (jc * NTT + tt) % 4
                            c.use("tensor", ("ps", psl))
                            c.t(lambda e, jc=jc, tt=tt, psl=psl: e.matmul(
                                pt[psl][:, 0:TTL],
                                wdt_bf[0:64, jc * 128:(jc + 1) * 128],
                                dt_sb[0:64, tt * TTL:(tt + 1) * TTL],
                                start=True, stop=True))
                            mt = c.mark("stn")
                            c.wait_marks("scalar", mt)
                            c.s(lambda e, jc=jc, tt=tt, psl=psl: e.activation(
                                delta[:, jc * TH + tt * TTL:
                                      jc * TH + (tt + 1) * TTL],
                                pt[psl][:, 0:TTL], ACT.Sigmoid,
                                bias=bdt_sb[:, jc:jc + 1], scale=-1.0))
                            c.setg(("ps", psl), "ss")
                    for jc in range(NKC):
                        c.s(lambda e, jc=jc: e.activation(
                            delta[:, jc * TH:(jc + 1) * TH],
                            delta[:, jc * TH:(jc + 1) * TH], ACT.Ln))
                    c.selfw("scalar")
                    c.s(lambda e: e.nop())
                    if taps:
                        c.wait_marks("gpsimd", c.mark("ss"))
                        for kc in range(NKC):
                            c.dma(tap["tap_delta"][kc * 128:(kc + 1) * 128,
                                                   t0:t0 + TH],
                                  delta[:, kc * TH:(kc + 1) * TH], eng="gpsimd")
                    if cut == 3:
                        mscope.close()
                        raise _Cut()
                    for kc in range(NKC):
                        c.use("sync", ("woutld", kc))
                        c.dma(wout_bf[:, kc * D:(kc + 1) * D],
                              wout_h[kc * 128:(kc + 1) * 128, :])
                    c.lanewait("sync", "dh", "dr0", "dr1")
                    mdl = c.mark("ss")
                    c.wait_marks("vector", mdl)
                    ibb = [0]
                    for kg in range(NKC // 4):
                        ks = [kg * 4 + j for j in range(4)]
                        for j, k in enumerate(ks[:2]):
                            c.use("sync", ("srkg", j % 2))
                            c.dma(srkg[:, (j % 2) * TH:(j % 2 + 1) * TH],
                                  sres_h[k * 128:(k + 1) * 128, t0:t0 + TH])
                        for j, k in enumerate(ks):
                            c.use("vector", ("ypsum", j))
                            c.v(lambda e, j=j, k=k: e.tensor_tensor(
                                gkt[:, j * TH:(j + 1) * TH],
                                delta[:, k * TH:(k + 1) * TH],
                                u0k(k)[:, 3:TH + 3], ALU.mult))
                            b0 = ibb[0] % 3
                            ibb[0] += 1
                            c.use("vector", ("bb", b0))
                            c.v(lambda e, k=k, b0=b0: e.tensor_scalar_mul(
                                bb[:, b0 * TH:(b0 + 1) * TH],
                                u0k(k)[:, 3:TH + 3], dv_sb[:, k:k + 1]))
                            mv = c.mark("sv")
                            c.wait_marks("tensor", mv)
                            for tb in range(YB):
                                c.t(lambda e, j=j, tb=tb, b0=b0: e.matmul(
                                    pt[2 * j + tb][:, 0:min(512, TH)], I_bf[:, :],
                                    bb[:, b0 * TH + tb * 512:
                                       b0 * TH + tb * 512 + min(512, TH)],
                                    start=True, stop=False))
                            c.setg(("bb", b0), "stn")
                        for n in range(NS):
                            bsl = n % 3
                            bln = ("db0", "db1", "ds0")[bsl]
                            c.use("sync", ("bcb", bsl))
                            c.dma(Bb[:, bsl * TH:(bsl + 1) * TH],
                                  bc_h[n:n + 1, t0:t0 + TH]
                                  .broadcast_to((128, TH)), lane=bln)
                            c.dma(Cb[:, bsl * TH:(bsl + 1) * TH],
                                  bc_h[NS + n:NS + n + 1, t0:t0 + TH]
                                  .broadcast_to((128, TH)), lane=bln)
                            for j, k in enumerate(ks):
                                asl = (n * 4 + j) % 3
                                hsl = (n * 4 + j) % 2
                                kn = k * NS + n
                                c.use("scalar", ("ab", asl))
                                c.s(lambda e, k=k, kn=kn, asl=asl: e.activation(
                                    ab[:, asl * TH:(asl + 1) * TH],
                                    delta[:, k * TH:(k + 1) * TH], ACT.Exp,
                                    scale=A_sb[:, kn:kn + 1]))
                                msa = c.mark("ss")
                                b1 = ibb[0] % 3
                                ibb[0] += 1
                                c.use("vector", ("bb", b1))
                                c.lanewait("vector", bln)
                                c.v(lambda e, j=j, bsl=bsl, b1=b1:
                                    e.tensor_tensor(
                                        bb[:, b1 * TH:(b1 + 1) * TH],
                                        gkt[:, j * TH:(j + 1) * TH],
                                        Bb[:, bsl * TH:(bsl + 1) * TH],
                                        ALU.mult))
                                c.wait_marks("vector", msa)
                                c.use("vector", ("hb", hsl))
                                c.v(lambda e, kn=kn, asl=asl, hsl=hsl, b1=b1:
                                    e.tensor_tensor_scan(
                                        hb[:, hsl * TH:(hsl + 1) * TH],
                                        ab[:, asl * TH:(asl + 1) * TH],
                                        bb[:, b1 * TH:(b1 + 1) * TH],
                                        state[:, kn:kn + 1], ALU.mult,
                                        ALU.add))
                                c.setg(("ab", asl), "sv")
                                c.setg(("bb", b1), "sv")
                                b2 = ibb[0] % 3
                                ibb[0] += 1
                                c.use("vector", ("bb", b2))
                                c.v(lambda e, hsl=hsl, bsl=bsl, b2=b2:
                                    e.tensor_tensor(
                                        bb[:, b2 * TH:(b2 + 1) * TH],
                                        hb[:, hsl * TH:(hsl + 1) * TH],
                                        Cb[:, bsl * TH:(bsl + 1) * TH],
                                        ALU.mult))
                                if h == 0:
                                    mv = c.mark("sv")
                                    c.wait_marks("gpsimd", mv)
                                    c.g(lambda e, kn=kn, hsl=hsl: e.tensor_copy(
                                        state[:, kn:kn + 1],
                                        hb[:, hsl * TH + TH - 1: hsl * TH + TH]))
                                    c.setg(("hb", hsl), "sv", "sg")
                                else:
                                    c.setg(("hb", hsl), "sv")
                                mv = c.mark("sv")
                                c.wait_marks("tensor", mv)
                                for tb in range(YB):
                                    c.t(lambda e, j=j, tb=tb, n=n, b2=b2:
                                        e.matmul(
                                            pt[2 * j + tb][:, 0:min(512, TH)],
                                            I_bf[:, :],
                                            bb[:, b2 * TH + tb * 512:
                                               b2 * TH + tb * 512 +
                                               min(512, TH)],
                                            start=False, stop=(n == NS - 1)))
                                c.setg(("bb", b2), "stn")
                            c.setg(("bcb", bsl), "sv")
                        mt = c.mark("stn")
                        c.wait_marks("vector", mt)
                        for j, k in enumerate(ks):
                            c.wait("vector", "dh")
                            for tb in range(YB):
                                w = min(512, TH)
                                c.v(lambda e, j=j, k=k, tb=tb, w=w: e.tensor_tensor(
                                    delta[:, k * TH + tb * 512:
                                          k * TH + tb * 512 + w],
                                    pt[2 * j + tb][:, 0:w],
                                    srkg[:, (j % 2) * TH + tb * 512:
                                         (j % 2) * TH + tb * 512 + w], ALU.mult))
                            c.setg(("ypsum", j), "sv")
                            c.setg(("srkg", j % 2), "sv")
                            if j + 2 < 4:
                                c.use("sync", ("srkg", j % 2))
                                c.dma(srkg[:, (j % 2) * TH:(j % 2 + 1) * TH],
                                      sres_h[ks[j + 2] * 128:
                                             (ks[j + 2] + 1) * 128, t0:t0 + TH])
                    if taps:
                        c.wait_marks("gpsimd", c.mark("sv"))
                        for kc in range(NKC):
                            c.dma(tap["tap_ymod"][kc * 128:(kc + 1) * 128,
                                                  t0:t0 + TH],
                                  delta[:, kc * TH:(kc + 1) * TH], eng="gpsimd")
                    if cut == 4:
                        mscope.close()
                        raise _Cut()
                    mym = c.mark("sv")
                    c.wait_marks("tensor", mym)
                    c.lanewait("tensor", "dh")
                    for kc in range(NKC):
                        c.setg(("woutld", kc), "stn")
                    for tcg in range(NTC // LNB):
                        for tc2 in range(LNB):
                            tc = tcg * LNB + tc2
                            c.use("sync", ("xbuf", tc2))
                            c.dma(xbuf[:, tc2 * D:(tc2 + 1) * D],
                                  xT[t0 + tc * 128: t0 + (tc + 1) * 128, :],
                                  lane="dx%d" % tc2)
                            c.lanewait("vector", "dx%d" % tc2)
                            c.use("vector", "xb2")
                            c.v(lambda e, tc2=tc2: e.scalar_tensor_tensor(
                                xb2[:, :], xbuf[:, tc2 * D:(tc2 + 1) * D],
                                2.0, bout_b[:, :], ALU.mult, ALU.add))
                            for hc in range(2):
                                psl = (tc * 2 + hc) % 4
                                c.use("tensor", ("ps", psl))
                                for kc in range(NKC):
                                    c.t(lambda e, kc=kc, tc=tc, hc=hc, psl=psl:
                                        e.matmul(
                                            pt[psl][:, 0:512],
                                            delta[:, kc * TH + tc * 128:
                                                  kc * TH + (tc + 1) * 128],
                                            wout_bf[:, kc * D + hc * 512:
                                                    kc * D + (hc + 1) * 512],
                                            start=(kc == 0),
                                            stop=(kc == NKC - 1)))
                                mt = c.mark("stn")
                                c.wait_marks("vector", mt)
                                c.v(lambda e, tc2=tc2, hc=hc, psl=psl:
                                    e.tensor_tensor(
                                        xbuf[:, tc2 * D + hc * 512:
                                             tc2 * D + hc * 512 + 512],
                                        pt[psl][:, 0:512],
                                        xb2[:, hc * 512: hc * 512 + 512],
                                        ALU.add))
                                c.setg(("ps", psl), "sv")
                            c.setg("xb2", "sv")
                            c.selfw("vector")
                            c.v(lambda e, tc2=tc2: e.tensor_reduce(
                                stat[:, tc2:tc2 + 1],
                                xbuf[:, tc2 * D:(tc2 + 1) * D], AX.X, ALU.add))
                            mv = c.mark("sv")
                            c.wait_marks("scalar", mv)
                            c.s(lambda e, tc2=tc2: e.activation(
                                sqd[:, :], xbuf[:, tc2 * D:(tc2 + 1) * D],
                                ACT.Square, accum_out=stat[:, 8 + tc2: 9 + tc2]))
                        ms = c.mark("ss")
                        c.wait_marks("vector", ms)
                        ln_rstd(LNB)
                        for tc2 in range(LNB):
                            tc = tcg * LNB + tc2
                            c.use("vector", "x1t")
                            c.use("vector", ("xnp", 0))
                            c.v(lambda e, tc2=tc2: e.tensor_scalar(
                                xnp[:, :], xbuf[:, tc2 * D:(tc2 + 1) * D],
                                stat[:, 16 + tc2:17 + tc2],
                                stat[:, 24 + tc2:25 + tc2],
                                ALU.subtract, ALU.mult))
                            c.setg(("xbuf", tc2), "sv")
                            c.v(lambda e, tc=tc, h=h: e.tensor_reduce(
                                mxu[:, h * NTC + tc: h * NTC + tc + 1],
                                xnp[:, :], AX.X, ALU.max,
                                apply_absolute_value=True))
                            mv = c.mark("sv")
                            c.wait_marks("sync", mv)
                            c.dma(x1_h[t0 + tc * 128: t0 + (tc + 1) * 128, :],
                                  xnp[:, :], lane="dst")
                            if taps:
                                c.dma(tap["tap_x1"][t0 + tc * 128:
                                                    t0 + (tc + 1) * 128, :],
                                      xnp[:, :], lane="dst")
                            c.guards["x1t"] = {"dst": c.cnt["dst"]}
                    c.barrier()

                if cut == 5:
                    mscope.close()
                    raise _Cut()
                reduce_to_all(mxu[:, 0:2 * NTC], 0)
                collective(0, 1)
                c.v(lambda e: e.tensor_scalar(scb[:, 0:1], scg[:, 0:1],
                                              1.0 / QMAX, 1e-12, ALU.mult,
                                              ALU.max))
                c.selfw("vector")
                c.v(lambda e: e.reciprocal(scb[:, 1:2], scb[:, 0:1]))
                c.selfw("vector")
            c.barrier()
            if cut == 55:
                raise _Cut()

            fscope = ExitStack()
            with fscope:
                _e = fscope.enter_context
                xq = _e(nc.sbuf_tensor("xq", [128, NFC * L], BF16))
                xr = _e(nc.sbuf_tensor("xr", [128, 2 * D], F32))
                xqp = _e(nc.sbuf_tensor("xqp", [128, 2 * D], F32))
                fwt = _e(nc.sbuf_tensor("fwt", [128, 4 * 1024], BF16))
                stg = _e(nc.sbuf_tensor("stg", [128, 8 * 512], F32))

                for tc in range(L // 128):
                    xsl = tc % 2
                    c.use("sync", ("xr", xsl))
                    c.dma(xr[:, xsl * D:(xsl + 1) * D],
                          x1_h[tc * 128:(tc + 1) * 128, :],
                          lane="dx%d" % xsl)
                    c.lanewait("vector", "dx%d" % xsl)
                    c.use("vector", ("xqp", xsl))
                    c.v(lambda e, xsl=xsl: e.tensor_scalar(
                        xr[:, xsl * D:(xsl + 1) * D],
                        xr[:, xsl * D:(xsl + 1) * D],
                        scb[:, 1:2], MAGIC, ALU.mult, ALU.add))
                    c.selfw("vector")
                    c.v(lambda e, xsl=xsl: e.tensor_scalar(
                        xqp[:, xsl * D:(xsl + 1) * D],
                        xr[:, xsl * D:(xsl + 1) * D],
                        MAGIC, scb[:, 0:1], ALU.subtract, ALU.mult))
                    c.setg(("xr", xsl), "sv")
                    mv = c.mark("sv")
                    for fc in range(NFC):
                        psl = 6 + fc % 2
                        c.wait_marks("tensor", mv)
                        c.use("tensor", ("ps", psl))
                        c.t(lambda e, fc=fc, xsl=xsl, psl=psl: e.transpose(
                            pt[psl][:, 0:128],
                            xqp[:, xsl * D + fc * 128: xsl * D + (fc + 1) * 128],
                            I_f32[:, :]))
                        mt = c.mark("stn")
                        c.wait_marks("scalar", mt)
                        c.s(lambda e, fc=fc, tc=tc, psl=psl: e.activation(
                            xq[:, fc * L + tc * 128: fc * L + (tc + 1) * 128],
                            pt[psl][:, 0:128], ACT.Copy))
                        c.setg(("ps", psl), "ss")
                    c.setg(("xqp", xsl), "stn")

                if taps:
                    c.wait_marks("gpsimd", c.mark("ss"))
                    c.dma(xq_h[:, :], xq[:, :], eng="gpsimd")
                if cut == 6:
                    fscope.close()
                    raise _Cut()
                mxq = c.mark("ss")
                spc = [0]
                for ic in range(NIC):
                    wsl = ic % 2
                    c.use("sync", ("fwt", wsl))
                    c.dma(fwt[:, wsl * 1024: wsl * 1024 + 1024]
                          .rearrange("p (fc i) -> p fc i", fc=NFC),
                          wupT_h[:, ic * 128:(ic + 1) * 128]
                          .rearrange("(fc p) i -> p fc i", p=128),
                          lane="dw%d" % wsl)
                    c.dma(fwt[:, (2 + wsl) * 1024:
                              (2 + wsl) * 1024 + 1024]
                          .rearrange("p (fc i) -> p fc i", fc=NFC),
                          wgtT_h[:, ic * 128:(ic + 1) * 128]
                          .rearrange("(fc p) i -> p fc i", p=128),
                          lane="dw%d" % wsl)
                    for tt in range(NT2):
                        for gi, (spill, mx) in enumerate(
                                ((sp_up, mxu), (sp_gate, mxg))):
                            psl = (tt * 2 + gi) % 4
                            c.use("tensor", ("ps", psl))
                            c.lanewait("tensor", "dw%d" % wsl)
                            c.wait_marks("tensor", mxq)
                            for fc in range(NFC):
                                c.t(lambda e, fc=fc, tt=tt, gi=gi, wsl=wsl,
                                    psl=psl: e.matmul(
                                        pt[psl][:, 0:TL2],
                                        fwt[:, (2 * gi + wsl) * 1024 + fc * 128:
                                            (2 * gi + wsl) * 1024 + (fc + 1) * 128],
                                        xq[:, fc * L + tt * TL2:
                                           fc * L + (tt + 1) * TL2],
                                        start=(fc == 0), stop=(fc == NFC - 1)))
                            mt = c.mark("stn")
                            ssl = (ic * NT2 * 2 + tt * 2 + gi) % 4
                            c.wait_marks("scalar", mt)
                            c.use("scalar", ("stg", ssl))
                            c.s(lambda e, psl=psl, ssl=ssl: e.activation(
                                stg[:, ssl * 512: ssl * 512 + TL2],
                                pt[psl][:, 0:TL2], ACT.Copy))
                            c.setg(("ps", psl), "ss")
                            c.selfw("scalar")
                            c.s(lambda e: e.nop())
                            ms = c.mark("ss")
                            c.wait_marks("vector", ms)
                            c.v(lambda e, ic=ic, tt=tt, ssl=ssl, mx=mx:
                                e.tensor_reduce(
                                    mx[:, ic * NT2 + tt: ic * NT2 + tt + 1],
                                    stg[:, ssl * 512: ssl * 512 + TL2], AX.X,
                                    ALU.max, apply_absolute_value=True))
                            c.wait_marks("sync", ms)
                            lnp = ("dp0", "dp1", "dt0", "dt1", "ds0", "ds1",
                                   "du0", "du1")[spc[0] % 8]
                            spc[0] += 1
                            c.dma(spill[ic * 128:(ic + 1) * 128,
                                        tt * TL2:(tt + 1) * TL2],
                                  stg[:, ssl * 512: ssl * 512 + TL2],
                                  lane=lnp)
                            c.guards[("stg", ssl)] = {lnp: c.cnt[lnp],
                                                      "sv": c.cnt["sv"]}
                    c.setg(("fwt", wsl), "stn")

                reduce_to_all(mxu[:, :], 0)
                reduce_to_all(mxg[:, :], 1)
                collective(1, 2)
                c.v(lambda e: e.tensor_scalar(scb[:, 2:3], scg[:, 0:1],
                                              1.0 / QMAX, 1e-12, ALU.mult,
                                              ALU.max))
                c.v(lambda e: e.tensor_scalar(scb[:, 4:5], scg[:, 1:2],
                                              1.0 / QMAX, 1e-12, ALU.mult,
                                              ALU.max))
                c.selfw("vector")
                c.v(lambda e: e.reciprocal(scb[:, 3:4], scb[:, 2:3]))
                c.v(lambda e: e.reciprocal(scb[:, 5:6], scb[:, 4:5]))
                c.selfw("vector")
                mv = c.mark("sv")
                c.wait_marks("scalar", mv)
                c.wait_marks("gpsimd", mv)

                fscope.close()
                f34 = ExitStack()
                _e34 = f34.enter_context
                gld = _e34(nc.sbuf_tensor("gld", [128, 4 * L], F32))
                uld = _e34(nc.sbuf_tensor("uld", [128, 4 * L], F32))
                g2o = _e34(nc.sbuf_tensor("g2o", [128, 4 * L], F32))
                gint = _e34(nc.sbuf_tensor("gint", [128, 2 * L], F32))
                sig = _e34(nc.sbuf_tensor("sig", [128, 2 * L], F32))
                ldlanes = ("ds0", "ds1", "dr0", "dr1")
                uplanes = ("du0", "du1", "db0", "db1")
                splanes = ("dp0", "dp1", "dt0", "dt1")

                c.lanewait("sync", "dp0", "dp1", "dt0", "dt1",
                           "ds0", "ds1", "du0", "du1")
                for ic in range(NIC):
                    s4 = ic % 4
                    s2_ = ic % 2
                    c.use("sync", ("gld", s4))
                    c.dma(gld[:, s4 * L:(s4 + 1) * L],
                          sp_gate[ic * 128:(ic + 1) * 128, :],
                          lane=ldlanes[s4])
                    c.lanewait("vector", ldlanes[s4])
                    c.use("vector", ("gint", s2_))
                    c.v(lambda e, s4=s4, s2_=s2_: e.tensor_scalar(
                        gint[:, s2_ * L:(s2_ + 1) * L],
                        gld[:, s4 * L:(s4 + 1) * L],
                        scb[:, 5:6], MAGIC, ALU.mult, ALU.add))
                    c.setg(("gld", s4), "sv")
                    c.selfw("vector")
                    c.v(lambda e, s2_=s2_: e.tensor_single_scalar(
                        gint[:, s2_ * L:(s2_ + 1) * L],
                        gint[:, s2_ * L:(s2_ + 1) * L], MAGIC,
                        ALU.subtract))
                    mv = c.mark("sv")
                    c.wait_marks("scalar", mv)
                    c.use("scalar", ("sig", s2_))
                    c.s(lambda e, s2_=s2_: e.activation(
                        sig[:, s2_ * L:(s2_ + 1) * L],
                        gint[:, s2_ * L:(s2_ + 1) * L], ACT.Sigmoid,
                        scale=scb[:, 4:5]))
                    c.selfw("scalar")
                    c.s(lambda e: e.nop())
                    ms = c.mark("ss")
                    c.setg(("gint", s2_), "ss")
                    c.wait_marks("vector", ms)
                    c.v(lambda e, s2_=s2_: e.tensor_scalar(
                        sig[:, s2_ * L:(s2_ + 1) * L],
                        sig[:, s2_ * L:(s2_ + 1) * L],
                        65536.0, MAGIC, ALU.mult, ALU.add))
                    c.selfw("vector")
                    c.v(lambda e, s2_=s2_: e.tensor_scalar(
                        sig[:, s2_ * L:(s2_ + 1) * L],
                        sig[:, s2_ * L:(s2_ + 1) * L],
                        MAGIC, 65535.0, ALU.subtract, ALU.min))
                    c.selfw("vector")
                    c.use("vector", ("g2o", s4))
                    c.v(lambda e, s4=s4, s2_=s2_: e.tensor_tensor(
                        g2o[:, s4 * L:(s4 + 1) * L],
                        gint[:, s2_ * L:(s2_ + 1) * L],
                        sig[:, s2_ * L:(s2_ + 1) * L], ALU.mult))
                    c.setg(("sig", s2_), "sv")
                    c.selfw("vector")
                    c.v(lambda e, ic=ic, s4=s4: e.tensor_reduce(
                        mx4[:, ic:ic + 1],
                        g2o[:, s4 * L:(s4 + 1) * L], AX.X,
                        ALU.max, apply_absolute_value=True))
                    mv = c.mark("sv")
                    c.wait_marks("sync", mv)
                    c.dma(sp_g2[ic * 128:(ic + 1) * 128, :],
                          g2o[:, s4 * L:(s4 + 1) * L], lane=splanes[s4])
                    c.guards[("g2o", s4)] = {splanes[s4]: c.cnt[splanes[s4]],
                                             "sv": c.cnt["sv"]}

                c.wait_marks("vector", c.mark("sg"))
                reduce_to_all(mx4[:, 0:NIC], 0)
                c.v(lambda e: e.tensor_tensor(sca[:, 0:1], sca[:, 0:1],
                                              scb[:, 4:5], ALU.mult))
                c.selfw("vector")
                c.v(lambda e: e.tensor_scalar_mul(sca[:, 0:1], sca[:, 0:1],
                                                  1.0 / 65536.0))
                c.selfw("vector")
                collective(2, 1)
                c.v(lambda e: e.tensor_scalar(sca[:, 2:3], scg[:, 0:1],
                                              1.0 / QMAX, 1e-12, ALU.mult,
                                              ALU.max))
                c.selfw("vector")
                c.v(lambda e: e.reciprocal(sca[:, 3:4], sca[:, 2:3]))
                c.selfw("vector")
                c.v(lambda e: e.tensor_tensor(scb[:, 6:7], scb[:, 4:5],
                                              sca[:, 3:4], ALU.mult))
                c.selfw("vector")
                c.v(lambda e: e.tensor_scalar_mul(scb[:, 6:7], scb[:, 6:7],
                                                  1.0 / 65536.0))
                c.selfw("vector")
                mv = c.mark("sv")
                c.wait_marks("gpsimd", mv)

                c.lanewait("sync", "dp0", "dp1", "dt0", "dt1")
                for ic in range(NIC):
                    s4 = ic % 4
                    s2_ = ic % 2
                    c.use("sync", ("gld", s4))
                    c.dma(gld[:, s4 * L:(s4 + 1) * L],
                          sp_g2[ic * 128:(ic + 1) * 128, :],
                          lane=ldlanes[s4])
                    c.use("sync", ("uld", s4))
                    c.dma(uld[:, s4 * L:(s4 + 1) * L],
                          sp_up[ic * 128:(ic + 1) * 128, :],
                          lane=uplanes[s4])
                    c.lanewait("vector", ldlanes[s4])
                    c.use("vector", ("gint", s2_))
                    c.v(lambda e, s4=s4, s2_=s2_: e.tensor_scalar(
                        gint[:, s2_ * L:(s2_ + 1) * L],
                        gld[:, s4 * L:(s4 + 1) * L],
                        scb[:, 6:7], MAGIC, ALU.mult, ALU.add))
                    c.setg(("gld", s4), "sv")
                    c.lanewait("vector", uplanes[s4])
                    c.use("vector", ("sig", s2_))
                    c.v(lambda e, s4=s4, s2_=s2_: e.tensor_scalar(
                        sig[:, s2_ * L:(s2_ + 1) * L],
                        uld[:, s4 * L:(s4 + 1) * L],
                        scb[:, 3:4], MAGIC, ALU.mult, ALU.add))
                    c.setg(("uld", s4), "sv")
                    c.selfw("vector")
                    c.v(lambda e, s2_=s2_: e.tensor_single_scalar(
                        gint[:, s2_ * L:(s2_ + 1) * L],
                        gint[:, s2_ * L:(s2_ + 1) * L], MAGIC,
                        ALU.subtract))
                    c.v(lambda e, s2_=s2_: e.tensor_single_scalar(
                        sig[:, s2_ * L:(s2_ + 1) * L],
                        sig[:, s2_ * L:(s2_ + 1) * L], MAGIC,
                        ALU.subtract))
                    c.selfw("vector")
                    c.use("vector", ("g2o", s4))
                    c.v(lambda e, s4=s4, s2_=s2_: e.tensor_tensor(
                        g2o[:, s4 * L:(s4 + 1) * L],
                        gint[:, s2_ * L:(s2_ + 1) * L],
                        sig[:, s2_ * L:(s2_ + 1) * L], ALU.mult))
                    c.setg(("gint", s2_), "sv")
                    c.setg(("sig", s2_), "sv")
                    c.selfw("vector")
                    c.v(lambda e, ic=ic, s4=s4: e.tensor_reduce(
                        mx5[:, ic:ic + 1],
                        g2o[:, s4 * L:(s4 + 1) * L], AX.X,
                        ALU.max, apply_absolute_value=True))
                    mv = c.mark("sv")
                    c.wait_marks("sync", mv)
                    c.dma(sp_o[ic * 128:(ic + 1) * 128, :],
                          g2o[:, s4 * L:(s4 + 1) * L], lane=splanes[s4])
                    c.guards[("g2o", s4)] = {splanes[s4]: c.cnt[splanes[s4]],
                                             "sv": c.cnt["sv"]}

                c.wait_marks("vector", c.mark("sg"))
                reduce_to_all(mx5[:, 0:NIC], 0)
                c.v(lambda e: e.tensor_tensor(sca[:, 0:1], sca[:, 0:1],
                                              sca[:, 2:3], ALU.mult))
                c.selfw("vector")
                c.v(lambda e: e.tensor_tensor(sca[:, 0:1], sca[:, 0:1],
                                              scb[:, 2:3], ALU.mult))
                c.selfw("vector")
                collective(3, 1)
                c.v(lambda e: e.tensor_scalar(scb[:, 8:9], scg[:, 0:1],
                                              1.0 / QMAX, 1e-12, ALU.mult,
                                              ALU.max))
                c.selfw("vector")
                c.v(lambda e: e.reciprocal(sca[:, 4:5], scb[:, 8:9]))
                c.selfw("vector")
                c.v(lambda e: e.tensor_tensor(scb[:, 7:8], sca[:, 2:3],
                                              scb[:, 2:3], ALU.mult))
                c.selfw("vector")
                c.v(lambda e: e.tensor_tensor(scb[:, 7:8], scb[:, 7:8],
                                              sca[:, 4:5], ALU.mult))
                c.selfw("vector")
                if taps:
                    c.wait_marks("sync", c.mark("sv"))
                    c.dma(scb_h[:, :], scb[:, :])
                f34.close()
            c.barrier()

            if cut == 9:
                raise _Cut()
            f5scope = ExitStack()
            with f5scope:
                _e = f5scope.enter_context
                wdn = _e(nc.sbuf_tensor("wdn", [128, NIC * D], BF16))
                oqt = _e(nc.sbuf_tensor("oqt", [128, 2 * NIC * TL2], BF16))
                ot = _e(nc.sbuf_tensor("ot", [128, 4 * 512], F32))
                pre3 = _e(nc.sbuf_tensor("pre3", [128, (TL2 // 128) * D], F32))
                x1r = _e(nc.sbuf_tensor("x1r", [128, 2 * D], F32))
                outt = _e(nc.sbuf_tensor("outt", [128, 2 * D], F32))
                c.lanewait("sync", "dp0", "dp1", "dt0", "dt1")
                for ic in range(NIC):
                    c.dma(wdn[:, ic * D:(ic + 1) * D],
                          wdnT_h[ic * 128:(ic + 1) * 128, :])
                f5lanes = ("ds0", "ds1", "du0", "du1")
                for tt in range(NT2):
                    oqb = (tt % 2) * NIC * TL2
                    c.use("vector", ("oqt", tt % 2))
                    for ic in range(NIC):
                        osl = ic % 4
                        c.use("sync", ("ot", osl))
                        c.dma(ot[:, osl * 512: osl * 512 + TL2],
                              sp_o[ic * 128:(ic + 1) * 128,
                                   tt * TL2:(tt + 1) * TL2],
                              lane=f5lanes[osl])
                        c.lanewait("vector", f5lanes[osl])
                        c.v(lambda e, osl=osl: e.tensor_scalar(
                            ot[:, osl * 512: osl * 512 + TL2],
                            ot[:, osl * 512: osl * 512 + TL2],
                            scb[:, 7:8], MAGIC, ALU.mult, ALU.add))
                        c.selfw("vector")
                        c.v(lambda e, ic=ic, osl=osl, oqb=oqb: e.tensor_scalar(
                            oqt[:, oqb + ic * TL2: oqb + (ic + 1) * TL2],
                            ot[:, osl * 512: osl * 512 + TL2],
                            MAGIC, scb[:, 8:9], ALU.subtract, ALU.mult))
                        c.setg(("ot", osl), "sv")
                    moq = c.mark("sv")
                    c.wait_marks("tensor", moq)
                    for tcl in range(TL2 // 128):
                        tc = tt * (TL2 // 128) + tcl
                        xsl = tc % 2
                        c.use("sync", ("x1r", xsl))
                        c.dma(x1r[:, xsl * D:(xsl + 1) * D],
                              x1_h[tc * 128:(tc + 1) * 128, :],
                              lane="dx%d" % xsl)
                        for hc in range(2):
                            psl = (tcl * 2 + hc) % 4
                            c.use("tensor", ("ps", psl))
                            if tt == 0 and tcl == 0 and hc == 0:
                                c.lanewait("tensor", "dh")
                            for ic in range(NIC):
                                c.t(lambda e, ic=ic, tcl=tcl, hc=hc, psl=psl,
                                    oqb=oqb: e.matmul(
                                        pt[psl][:, 0:512],
                                        oqt[:, oqb + ic * TL2 + tcl * 128:
                                            oqb + ic * TL2 + (tcl + 1) * 128],
                                        wdn[:, ic * D + hc * 512:
                                            ic * D + (hc + 1) * 512],
                                        start=(ic == 0), stop=(ic == NIC - 1)))
                            mt = c.mark("stn")
                            c.wait_marks("vector", mt)
                            c.lanewait("vector", "dx%d" % xsl)
                            c.v(lambda e, tcl=tcl, hc=hc, psl=psl, xsl=xsl:
                                e.tensor_tensor(
                                    pre3[:, tcl * D + hc * 512:
                                         tcl * D + hc * 512 + 512],
                                    pt[psl][:, 0:512],
                                    x1r[:, xsl * D + hc * 512:
                                        xsl * D + hc * 512 + 512], ALU.add))
                            c.setg(("ps", psl), "sv")
                        c.setg(("x1r", xsl), "sv")
                        c.selfw("vector")
                        c.v(lambda e, tcl=tcl: e.tensor_reduce(
                            stat[:, tcl:tcl + 1], pre3[:, tcl * D:(tcl + 1) * D],
                            AX.X, ALU.add))
                        mv = c.mark("sv")
                        c.wait_marks("scalar", mv)
                        c.s(lambda e, tcl=tcl: e.activation(
                            sqd[:, :], pre3[:, tcl * D:(tcl + 1) * D], ACT.Square,
                            accum_out=stat[:, 8 + tcl: 9 + tcl]))
                    ms = c.mark("ss")
                    c.wait_marks("vector", ms)
                    ln_rstd(TL2 // 128)
                    for tcl in range(TL2 // 128):
                        tc = tt * (TL2 // 128) + tcl
                        osl = tcl % 2
                        c.use("vector", ("outt", osl))
                        c.v(lambda e, tcl=tcl, osl=osl: e.tensor_scalar(
                            outt[:, osl * D:(osl + 1) * D],
                            pre3[:, tcl * D:(tcl + 1) * D],
                            stat[:, 16 + tcl:17 + tcl],
                            stat[:, 24 + tcl:25 + tcl],
                            ALU.subtract, ALU.mult))
                        mv = c.mark("sv")
                        c.wait_marks("sync", mv)
                        c.dma(outT[tc * 128:(tc + 1) * 128, :],
                              outt[:, osl * D:(osl + 1) * D], lane="dst")
                        c.guards[("outt", osl)] = {"dst": c.cnt["dst"]}
                    c.setg(("oqt", tt % 2), "stn")

        except _Cut:
            pass
        if cut < 99:
            c.barrier()
            for tc in range(L // 128):
                c.dma(outT[tc * 128:(tc + 1) * 128, :], sqd[:, :])

        c.barrier()
        for ln in c.dma_lanes:
            c.wait("sync", ln)
        c.wait("gpsimd", "dg")

        with nc.Block() as blk:
            for eng in ENGINES:
                ops = c.q[eng]

                def make(ops):
                    def body(e):
                        for f in ops:
                            f(e)
                    return body
                getattr(blk, eng)(make(ops))

    return nc


_NC_CACHE = {}


def _get_nc(L, n_cores, taps=False, cut=99):
    key = (L, n_cores, taps, cut)
    if key not in _NC_CACHE:
        _NC_CACHE[key] = build_nc(L, n_cores, taps, cut=cut)
    return _NC_CACHE[key]


def make_in_maps(inputs, n_cores=8):
    f = lambda k: np.ascontiguousarray(np.asarray(inputs[k], dtype=np.float32))
    x = f("L0_attn_x")
    base = {
        "ln_g": f("L0_attn_ln_gamma"), "ln_b": f("L0_attn_ln_beta"),
        "w_in": f("L0_attn_W_in"), "b_in": f("L0_attn_b_in"),
        "conv_w": f("L0_attn_conv_w"), "conv_b": f("L0_attn_conv_b"),
        "w_x": f("L0_attn_W_x"), "w_dt": f("L0_attn_W_dt"),
        "b_dt": f("L0_attn_b_dt"), "a_log": f("L0_attn_A_log"),
        "d_vec": f("L0_attn_D"), "w_out": f("L0_attn_W_out"),
        "b_out": f("L0_attn_b_out"), "w_gate": f("L0_ffn_w_gate"),
        "w_up": f("L0_ffn_w_up"), "w_down": f("L0_ffn_w_down"),
    }
    return [dict(base, x=np.ascontiguousarray(x[i])) for i in range(n_cores)]


def kernel(**inputs):
    n_cores = 8
    in_maps = make_in_maps(inputs, n_cores)
    nc = _get_nc(in_maps[0]["x"].shape[0], n_cores)
    res = run_bass_kernel_spmd(nc, in_maps, core_ids=list(range(n_cores)))
    return np.stack([res.results[i]["out"] for i in range(n_cores)], axis=0)


# version 18
# speedup vs baseline: 1.1166x; 1.1166x over previous
from contextlib import ExitStack

import numpy as np

import concourse.bass as bass
import concourse.mybir as mybir
from concourse.bass_utils import run_bass_kernel_spmd

F32 = mybir.dt.float32
BF16 = mybir.dt.bfloat16
ALU = mybir.AluOpType
ACT = mybir.ActivationFunctionType
AX = mybir.AxisListType

MAGIC = float(np.float32(1.5 * 2**23))
QMAX = 32767.0
EPS = 1e-5

D, DI, NS, RK, KC, HI = 1024, 2048, 16, 64, 4, 3072
NKC, NFC, NIC = DI // 128, D // 128, HI // 128

ENGINES = ("sync", "vector", "scalar", "tensor", "gpsimd")
DBG = {"no_collective": False, "no_r2s": False, "no_bcast": False}


class Ctx:

    def __init__(self, nc, stack):
        self.nc = nc
        self.q = {e: [] for e in ENGINES}
        self.sem = {}
        self.cnt = {}
        self.dma_lanes = ("dh", "dx0", "dx1", "dw0", "dw1", "db0", "db1",
                          "ds0", "ds1", "du0", "du1", "dr0", "dr1",
                          "dt0", "dt1", "dst", "dp0", "dp1", "dg")
        for name in self.dma_lanes + ("sv", "ss", "stn", "sg", "cc"):
            self.sem[name] = stack.enter_context(nc.semaphore(name))
            self.cnt[name] = 0
        self.guards = {}
        self.eng_sem = {"vector": "sv", "scalar": "ss", "tensor": "stn",
                        "gpsimd": "sg"}

    def wait(self, eng, name, val=None):
        if val is None:
            val = self.cnt[name]
        if val <= 0:
            return
        h = self.sem[name]
        self.q[eng].append(lambda e, h=h, v=val: e.wait_ge(h, v))

    def run(self, eng, fn, inc=None):
        if inc is None:
            self.q[eng].append(fn)
            return None
        step = 16 if inc in self.dma_lanes else 1
        self.cnt[inc] += step
        h = self.sem[inc]
        self.q[eng].append(lambda e, fn=fn, h=h, s=step: fn(e).then_inc(h, s))
        return self.cnt[inc]

    def v(self, fn):
        return self.run("vector", fn, "sv")

    def s(self, fn):
        return self.run("scalar", fn, "ss")

    def t(self, fn):
        return self.run("tensor", fn, "stn")

    def g(self, fn):
        return self.run("gpsimd", fn, "sg")

    def dma(self, out, in_, eng="sync", slow=False, lane=None):
        sem = lane or ("dg" if eng == "gpsimd" else "dh")
        return self.run(eng, lambda e, o=out, i=in_, s=slow: e.dma_start(
            out=o, in_=i, allow_slow_non_contiguous=s), sem)

    def mark(self, *names):
        if not names:
            names = ("sv", "ss", "stn", "sg", "dh", "dg")
        return {n: self.cnt[n] for n in names}

    def wait_marks(self, eng, marks):
        for name, val in marks.items():
            if val > 0:
                self.wait(eng, name, val)

    def use(self, eng, key):
        m = self.guards.get(key)
        if m:
            self.wait_marks(eng, m)

    def setg(self, key, *names):
        self.guards[key] = self.mark(*names)

    def selfw(self, eng):
        self.wait(eng, self.eng_sem[eng])

    def lanewait(self, eng, *lanes):
        for ln in lanes:
            self.wait(eng, ln)

    def barrier(self):
        for eng, sname in (("vector", "sv"), ("scalar", "ss"),
                           ("tensor", "stn"), ("gpsimd", "sg")):
            self.run(eng, lambda e: e.nop(), sname)
        m = {n: self.cnt[n] for n in
             ("sv", "ss", "stn", "sg") + self.dma_lanes}
        for eng in ENGINES:
            self.wait_marks(eng, m)


def build_nc(L=2048, n_cores=8, taps=False, cut=99):
    TH = L // 2
    NTC = TH // 128
    TTL = min(512, TH)
    NTT = TH // TTL
    TL2 = min(512, L)
    NT2 = L // TL2
    YB = max(1, TH // 512)
    LNB = min(2, NTC)

    nc = bass.Bass("TRN2", target_bir_lowering=False,
                   detect_race_conditions=False)

    xT = nc.dram_tensor("x", [L, D], F32, kind="ExternalInput").ap()
    g_ln = nc.dram_tensor("ln_g", [D], F32, kind="ExternalInput").ap()
    b_ln = nc.dram_tensor("ln_b", [D], F32, kind="ExternalInput").ap()
    w_in = nc.dram_tensor("w_in", [D, 2 * DI], F32, kind="ExternalInput").ap()
    b_in = nc.dram_tensor("b_in", [2 * DI], F32, kind="ExternalInput").ap()
    conv_w = nc.dram_tensor("conv_w", [KC, DI], F32, kind="ExternalInput").ap()
    conv_b = nc.dram_tensor("conv_b", [DI], F32, kind="ExternalInput").ap()
    w_x = nc.dram_tensor("w_x", [DI, RK + 2 * NS], F32, kind="ExternalInput").ap()
    w_dt = nc.dram_tensor("w_dt", [RK, DI], F32, kind="ExternalInput").ap()
    b_dt = nc.dram_tensor("b_dt", [DI], F32, kind="ExternalInput").ap()
    a_log = nc.dram_tensor("a_log", [DI, NS], F32, kind="ExternalInput").ap()
    d_vec = nc.dram_tensor("d_vec", [DI], F32, kind="ExternalInput").ap()
    w_out = nc.dram_tensor("w_out", [DI, D], F32, kind="ExternalInput").ap()
    b_out = nc.dram_tensor("b_out", [D], F32, kind="ExternalInput").ap()
    w_gate = nc.dram_tensor("w_gate", [HI, D], F32, kind="ExternalInput").ap()
    w_up = nc.dram_tensor("w_up", [HI, D], F32, kind="ExternalInput").ap()
    w_down = nc.dram_tensor("w_down", [D, HI], F32, kind="ExternalInput").ap()
    outT = nc.dram_tensor("out", [L, D], F32, kind="ExternalOutput").ap()

    wkind = "ExternalOutput" if taps else "Internal"
    win_h = nc.dram_tensor("win_h", [D, 2 * DI], BF16).ap()
    wout_h = nc.dram_tensor("wout_h", [DI, D], BF16).ap()
    wupT_h = nc.dram_tensor("wupT_h", [D, HI], BF16, kind=wkind).ap()
    wgtT_h = nc.dram_tensor("wgtT_h", [D, HI], BF16, kind=wkind).ap()
    wdnT_h = nc.dram_tensor("wdnT_h", [HI, D], BF16, kind=wkind).ap()
    sres_h = nc.dram_tensor("sres_h", [DI, L], BF16).ap()
    spkind = "ExternalOutput" if taps else "Internal"
    sp_up = nc.dram_tensor("sp_up", [HI, L], F32, kind=spkind).ap()
    sp_gate = nc.dram_tensor("sp_gate", [HI, L], F32, kind=spkind).ap()
    sp_g2 = nc.dram_tensor("sp_g2", [HI, L], F32, kind=spkind).ap()
    sp_o = nc.dram_tensor("sp_o", [HI, L], F32, kind=spkind).ap()
    x1_h = nc.dram_tensor("x1_h", [L, D], F32).ap()
    bc_h = nc.dram_tensor("bc_h", [2 * NS, L], BF16).ap()
    sc_h = nc.dram_tensor("sc_h", [4, 32], F32,
                          kind="ExternalOutput" if taps else "Internal").ap()
    r2s_h = nc.dram_tensor("r2s_h", [16, 128], F32,
                           kind="ExternalOutput" if taps else "Internal").ap()
    cc_bufs = [(nc.dram_tensor("cci%d" % i, [1, 8], F32).ap(),
                nc.dram_tensor("cco%d" % i, [1, 8], F32).ap())
               for i in range(4)]
    scb_h = (nc.dram_tensor("scb_h", [128, 12], F32, kind="ExternalOutput").ap()
             if taps else None)
    xq_h = (nc.dram_tensor("xq_h", [128, NFC * L], BF16,
                           kind="ExternalOutput").ap() if taps else None)
    tap = {}
    if taps:
        for nm, shp in (("tap_u", [DI, L]), ("tap_delta", [DI, L]),
                        ("tap_ymod", [DI, L]), ("tap_x1", [L, D])):
            tap[nm] = nc.dram_tensor(nm, shp, F32, kind="ExternalOutput").ap()

    stack = ExitStack()
    with stack:
        c = Ctx(nc, stack)

        def sb(name, shape, dt=F32):
            return stack.enter_context(nc.sbuf_tensor(name, shape, dt))

        pt = [stack.enter_context(nc.psum_tensor("ps%d" % i, [128, 512], F32))
              for i in range(8)]

        A_sb = sb("A_sb", [128, NKC * NS])
        cw_sb = sb("cw_sb", [128, KC * NKC])
        cb_sb = sb("cb_sb", [128, NKC])
        dv_sb = sb("dv_sb", [128, NKC])
        bdt_sb = sb("bdt_sb", [128, NKC])
        bin_sb = sb("bin_sb", [128, 2 * NKC])
        gam_sb = sb("gam_sb", [128, NFC])
        bet_sb = sb("bet_sb", [128, NFC])
        bout_b = sb("bout_b", [128, D], BF16)
        I_bf = sb("I_bf", [128, 128], BF16)
        I_f32 = sb("I_f32", [128, 128])
        state = sb("state", [128, NKC * NS])
        scal = sb("scal", [1, 32])
        scb = sb("scb", [128, 12])
        sca = sb("sca", [128, 8])
        scg = sb("scg", [128, 8])
        zero8 = sb("zero8", [1, 8])
        wx_bf = sb("wx_bf", [128, NKC * 128], BF16)
        wdt_bf = sb("wdt_bf", [64, DI], BF16)
        stat = sb("stat", [128, 64])
        sqd = sb("sqd", [128, D])
        mxu = sb("mxu", [128, max(16, NIC * NT2)])
        mxg = sb("mxg", [128, NIC * NT2])
        mx4 = sb("mx4", [128, NIC * NT2])
        mx5 = sb("mx5", [128, NIC * NT2])

        c.dma(A_sb[:, :].rearrange("p (k n) -> p k n", n=NS),
              a_log.rearrange("(k p) n -> p k n", p=128))
        c.dma(cw_sb[:, :].rearrange("p (i k) -> p i k", i=KC),
              conv_w.rearrange("i (k p) -> p i k", p=128), slow=True)
        c.dma(cb_sb[:, :], conv_b.rearrange("(k p) -> p k", p=128), slow=True)
        c.dma(dv_sb[:, :], d_vec.rearrange("(k p) -> p k", p=128), slow=True)
        c.dma(bdt_sb[:, :], b_dt.rearrange("(k p) -> p k", p=128), slow=True)
        c.dma(bin_sb[:, :], b_in.rearrange("(k p) -> p k", p=128), slow=True)
        c.dma(gam_sb[:, :], g_ln.rearrange("(k p) -> p k", p=128), slow=True)
        c.dma(bet_sb[:, :], b_ln.rearrange("(k p) -> p k", p=128), slow=True)
        c.dma(bout_b[:, :],
              b_out.rearrange("(o f) -> o f", o=1).broadcast_to((128, D)),
              eng="gpsimd")
        c.g(lambda e: e.memset(wx_bf[:, :], 0.0))
        c.wait("gpsimd", "sg")
        c.dma(wx_bf[:, :].rearrange("p (k j) -> p k j", j=128)[:, :, 0:RK],
              w_x.rearrange("(k p) j -> p k j", p=128)[:, :, 0:RK],
              eng="gpsimd")
        c.dma(wx_bf[:, :].rearrange("p (k j) -> p k j", j=128)[:, :, 64:64 + NS],
              w_x.rearrange("(k p) j -> p k j", p=128)[:, :, RK:RK + NS],
              eng="gpsimd")
        c.dma(wx_bf[:, :].rearrange("p (k j) -> p k j", j=128)[:, :, 96:96 + NS],
              w_x.rearrange("(k p) j -> p k j", p=128)[:, :, RK + NS:RK + 2 * NS],
              eng="gpsimd")
        c.dma(wdt_bf[0:64, :], w_dt[:, :], eng="gpsimd")
        c.wait("scalar", "dh")
        c.s(lambda e: e.activation(A_sb[:, :], A_sb[:, :], ACT.Exp))
        c.wait("vector", "ss")
        c.v(lambda e: e.tensor_scalar_mul(bdt_sb[:, :], bdt_sb[:, :], -1.0))
        c.g(lambda e: e.iota(I_f32[:, :], [[1, 128]], base=0,
                             channel_multiplier=0,
                             allow_small_or_imprecise_dtypes=True))
        c.g(lambda e: e.iota(scb[:, 9:10], [[0, 1]], base=0,
                             channel_multiplier=1,
                             allow_small_or_imprecise_dtypes=True))
        mg = c.mark("sg")
        c.wait_marks("vector", mg)
        c.v(lambda e: e.tensor_tensor(I_f32[:, :], I_f32[:, :],
                                      scb[:, 9:10].broadcast_to((128, 128)),
                                      ALU.is_equal))
        c.selfw("vector")
        c.v(lambda e: e.tensor_copy(I_bf[:, :], I_f32[:, :]))
        c.g(lambda e: e.memset(state[:, :], 0.0))
        c.g(lambda e: e.memset(zero8[:, :], 0.0))
        c.g(lambda e: e.memset(scal[:, :], 0.0))
        c.g(lambda e: e.memset(scb[:, 10:12], 0.0))
        c.g(lambda e: e.memset(sca[:, :], 0.0))
        c.g(lambda e: e.memset(scg[:, :], 0.0))
        c.barrier()

        r2s_idx = [0]

        def reduce_to_all(src_ap, dst_col):
            idx = r2s_idx[0] % 16
            r2s_idx[0] += 1
            c.selfw("vector")
            c.use("vector", "stat63")
            c.v(lambda e: e.tensor_reduce(stat[:, 63:64], src_ap, AX.X,
                                          ALU.max, apply_absolute_value=True))
            mv = c.mark("sv")
            c.wait_marks("sync", mv)
            c.dma(r2s_h[idx, :].rearrange("(p o) -> p o", o=1),
                  stat[:, 63:64])
            c.setg("stat63", "dh")
            c.lanewait("sync", "dh")
            c.dma(sqd[:, 0:128],
                  r2s_h[idx, :].rearrange("(o f) -> o f", o=1)
                  .broadcast_to((128, 128)))
            c.lanewait("vector", "dh")
            c.v(lambda e, d=dst_col: e.tensor_reduce(
                sca[:, d:d + 1], sqd[:, 0:128], AX.X, ALU.max,
                apply_absolute_value=True))
            c.selfw("vector")

        def collective(idx, n_in):
            cci, cco = cc_bufs[idx]
            if DBG["no_collective"]:
                c.v(lambda e, n=n_in: e.tensor_copy(scg[:, 0:n],
                                                    sca[:, 0:n]))
                c.selfw("vector")
                return
            mv = c.mark("sv")
            c.wait_marks("sync", mv)
            c.dma(cci[0:1, 0:8], zero8[0:1, 0:8])
            c.lanewait("sync", "dh")
            c.dma(cci[0:1, 0:n_in], sca[0:1, 0:n_in])
            c.lanewait("gpsimd", "dh")
            c.run("gpsimd", lambda e: e.collective_compute(
                "AllReduce", ALU.max,
                replica_groups=[list(range(n_cores))],
                ins=[cci.opt()], outs=[cco.opt()]), "cc")
            c.wait("sync", "cc")
            c.dma(scg[:, 0:n_in],
                  cco[0:1, 0:n_in].broadcast_to((128, n_in)))
            c.lanewait("vector", "dh")

        def ln_rstd(n):
            c.selfw("scalar")
            c.s(lambda e: e.nop())
            c.wait_marks("vector", c.mark("ss"))
            c.selfw("vector")
            c.v(lambda e: e.tensor_scalar_mul(stat[:, 16:16 + n],
                                              stat[:, 0:n], 1.0 / D))
            c.v(lambda e: e.tensor_scalar_mul(stat[:, 32:32 + n],
                                              stat[:, 8:8 + n], 1.0 / D))
            c.selfw("vector")
            c.v(lambda e: e.tensor_tensor(stat[:, 40:40 + n],
                                          stat[:, 16:16 + n],
                                          stat[:, 16:16 + n], ALU.mult))
            c.selfw("vector")
            c.v(lambda e: e.tensor_tensor(stat[:, 40:40 + n],
                                          stat[:, 32:32 + n],
                                          stat[:, 40:40 + n], ALU.subtract))
            c.selfw("vector")
            c.v(lambda e: e.tensor_scalar_add(stat[:, 40:40 + n],
                                              stat[:, 40:40 + n], EPS))
            mv = c.mark("sv")
            c.wait_marks("scalar", mv)
            c.s(lambda e: e.activation(stat[:, 48:48 + n],
                                       stat[:, 40:40 + n], ACT.Sqrt))
            c.selfw("scalar")
            c.s(lambda e: e.nop())
            ms = c.mark("ss")
            c.wait_marks("vector", ms)
            c.v(lambda e: e.reciprocal(stat[:, 24:24 + n], stat[:, 48:48 + n]))
            c.selfw("vector")
            c.v(lambda e: e.tensor_tensor(stat[:, 56:56 + n],
                                          stat[:, 40:40 + n],
                                          stat[:, 24:24 + n], ALU.mult))
            c.selfw("vector")
            c.v(lambda e: e.tensor_tensor(stat[:, 56:56 + n],
                                          stat[:, 56:56 + n],
                                          stat[:, 24:24 + n], ALU.mult))
            c.selfw("vector")
            c.v(lambda e: e.tensor_scalar(stat[:, 56:56 + n],
                                          stat[:, 56:56 + n], -0.5, 1.5,
                                          ALU.mult, ALU.add))
            c.selfw("vector")
            c.v(lambda e: e.tensor_tensor(stat[:, 24:24 + n],
                                          stat[:, 24:24 + n],
                                          stat[:, 56:56 + n], ALU.mult))
            c.selfw("vector")

        class _Cut(Exception):
            pass

        try:
            p0scope = ExitStack()
            with p0scope:
                _e = p0scope.enter_context
                raw = _e(nc.sbuf_tensor("raw", [128, 4 * 4096], F32))
                dqs = _e(nc.sbuf_tensor("dqs", [128, 4 * 128], F32))
                dqr = _e(nc.sbuf_tensor("dqr", [128, 4 * 128], F32))
                trb = _e(nc.sbuf_tensor("trb", [128, 8 * 128], BF16))
                cstb = _e(nc.sbuf_tensor("cstb", [128, 2 * 4096], BF16))
                p0lanes = ("dw0", "dw1", "dx0", "dx1",
                           "db0", "db1", "ds0", "ds1")
                casts = [("c", (w_in, win_h, rc, 2 * DI)) for rc in range(NFC)] + \
                        [("c", (w_out, wout_h, rc, D)) for rc in range(NKC)]
                groups = [("g", (w_up, wupT_h, 4 * g, 4, D)) for g in range(6)] + \
                         [("g", (w_gate, wgtT_h, 4 * g, 4, D)) for g in range(6)] + \
                         [("g", (w_down, wdnT_h, oc, 1, HI)) for oc in range(8)]
                seq = []
                while casts or groups:
                    if casts:
                        seq.append(casts.pop(0))
                    if groups:
                        seq.append(groups.pop(0))
                tri = 0
                for idx, (kind, item) in enumerate(seq):
                    s4 = idx % 4
                    ldl = p0lanes[idx % 8]
                    if kind == "c":
                        src, dst, rc, NCOL = item
                        sl = idx % 2
                        c.use("sync", ("p0raw", s4))
                        c.dma(raw[:, s4 * 4096: s4 * 4096 + NCOL],
                              src[rc * 128:(rc + 1) * 128, :], lane=ldl)
                        c.lanewait("vector", ldl)
                        c.use("vector", ("p0cst", sl))
                        c.v(lambda e, s4=s4, sl=sl, NCOL=NCOL: e.tensor_copy(
                            cstb[:, sl * 4096: sl * 4096 + NCOL],
                            raw[:, s4 * 4096: s4 * 4096 + NCOL]))
                        c.setg(("p0raw", s4), "sv")
                        mv = c.mark("sv")
                        c.wait_marks("sync", mv)
                        ln = "dp%d" % sl
                        c.dma(dst[rc * 128:(rc + 1) * 128, :],
                              cstb[:, sl * 4096: sl * 4096 + NCOL], lane=ln)
                        c.guards[("p0cst", sl)] = {ln: c.cnt[ln]}
                        continue
                    wsrc, wdst, oc0, GRP, I = item
                    gw = GRP * I
                    NB = gw // 32
                    rawt = raw[:, s4 * 4096: s4 * 4096 + gw]
                    sct = dqs[:, s4 * 128: s4 * 128 + NB]
                    rct = dqr[:, s4 * 128: s4 * 128 + NB]
                    c.use("sync", ("p0raw", s4))
                    if GRP == 1:
                        c.dma(rawt, wsrc[oc0 * 128:(oc0 + 1) * 128, :],
                              lane=ldl)
                    else:
                        c.dma(rawt.rearrange("p (o i) -> p o i", o=GRP),
                              wsrc[oc0 * 128:(oc0 + GRP) * 128, :]
                              .rearrange("(o p) i -> p o i", p=128),
                              lane=ldl)
                    c.lanewait("vector", ldl)
                    c.use("vector", ("p0dq", s4))
                    c.v(lambda e, r=rawt, s=sct: e.tensor_reduce(
                        s, r.rearrange("p (b k) -> p b k", k=32), AX.X,
                        ALU.max, apply_absolute_value=True))
                    c.selfw("vector")
                    c.v(lambda e, s=sct: e.tensor_scalar(
                        s, s, 1.0 / 7.0, 1e-12, ALU.mult, ALU.max))
                    c.selfw("vector")
                    c.v(lambda e, s=sct, r=rct: e.reciprocal(r, s))
                    c.selfw("vector")
                    c.v(lambda e, r=rawt, rc_=rct, NB=NB: e.tensor_tensor(
                        r.rearrange("p (b k) -> p b k", k=32),
                        r.rearrange("p (b k) -> p b k", k=32),
                        rc_.rearrange("p (b o) -> p b o", o=1)
                        .broadcast_to((128, NB, 32)), ALU.mult))
                    c.v(lambda e, r=rawt: e.tensor_scalar(
                        r, r, MAGIC, MAGIC, ALU.add, ALU.subtract))
                    c.v(lambda e, r=rawt, s=sct, NB=NB: e.tensor_tensor(
                        r.rearrange("p (b k) -> p b k", k=32),
                        r.rearrange("p (b k) -> p b k", k=32),
                        s.rearrange("p (b o) -> p b o", o=1)
                        .broadcast_to((128, NB, 32)), ALU.mult))
                    mwq = c.mark("sv")
                    c.setg(("p0dq", s4), "sv")
                    NBT = I // 128
                    for b in range(gw // 128):
                        ocl, bb = divmod(b, NBT)
                        psl = tri % 8
                        tsl = tri % 8
                        c.wait_marks("tensor", mwq)
                        c.use("tensor", ("ps", psl))
                        c.t(lambda e, b=b, w=rawt, psl=psl: e.transpose(
                            pt[psl][:, 0:128], w[:, b * 128:(b + 1) * 128],
                            I_f32[:, :]))
                        mt = c.mark("stn")
                        c.wait_marks("scalar", mt)
                        c.use("scalar", ("p0trb", tsl))
                        c.s(lambda e, psl=psl, tsl=tsl: e.activation(
                            trb[:, tsl * 128:(tsl + 1) * 128],
                            pt[psl][:, 0:128], ACT.Copy))
                        c.setg(("ps", psl), "ss")
                        ms = c.mark("ss")
                        c.wait_marks("sync", ms)
                        ln = ("dt0", "dt1", "dst", "dh")[tsl % 4]
                        c.dma(wdst[bb * 128:(bb + 1) * 128,
                                   (oc0 + ocl) * 128:(oc0 + ocl + 1) * 128],
                              trb[:, tsl * 128:(tsl + 1) * 128], lane=ln)
                        c.guards[("p0trb", tsl)] = {ln: c.cnt[ln]}
                        tri += 1
                    c.setg(("p0raw", s4), "stn")
            c.barrier()
            if cut == 2:
                raise _Cut()

            mscope = ExitStack()
            with mscope:
                _e = mscope.enter_context
                xbuf = _e(nc.sbuf_tensor("xbuf", [128, LNB * D], F32))
                xnp = _e(nc.sbuf_tensor("xnp", [128, D], F32))
                xn_fm = _e(nc.sbuf_tensor("xn_fm", [128, NFC * TH], BF16))
                u0 = _e(nc.sbuf_tensor("u0", [128, NKC * (TH + 3)], BF16))
                delta = _e(nc.sbuf_tensor("delta", [128, NKC * TH], BF16))
                dt_sb = _e(nc.sbuf_tensor("dt_sb", [64, TH], BF16))
                bcB_sb = _e(nc.sbuf_tensor("bcB_sb", [16, TH], BF16))
                bcC_sb = _e(nc.sbuf_tensor("bcC_sb", [16, TH], BF16))
                wint = _e(nc.sbuf_tensor("wint", [128, 2 * 1024], BF16))
                srstg = _e(nc.sbuf_tensor("srstg", [128, 2 * TTL], BF16))
                acc2 = _e(nc.sbuf_tensor("acc2", [128, TH], BF16))
                tails = _e(nc.sbuf_tensor("tails", [128, NKC * 3], BF16))
                gkt = _e(nc.sbuf_tensor("gkt", [128, 4 * TH], BF16))
                srkg = _e(nc.sbuf_tensor("srkg", [128, 2 * TH], BF16))
                Bb = _e(nc.sbuf_tensor("Bb", [128, 3 * TH], BF16))
                Cb = _e(nc.sbuf_tensor("Cb", [128, 3 * TH], BF16))
                ab = _e(nc.sbuf_tensor("ab", [128, 3 * TH], F32))
                bb = _e(nc.sbuf_tensor("bb", [128, 3 * TH], BF16))
                hb = _e(nc.sbuf_tensor("hb", [128, 2 * TH], BF16))
                wout_bf = _e(nc.sbuf_tensor("wout_bf", [128, NKC * D], BF16))
                xb2 = _e(nc.sbuf_tensor("xb2", [128, D], F32))

                def u0k(kc):
                    return u0[:, kc * (TH + 3): kc * (TH + 3) + TH + 3]

                for h in range(2):
                    t0 = h * TH
                    for tcg in range(NTC // LNB):
                        for tc2 in range(LNB):
                            tc = tcg * LNB + tc2
                            c.use("sync", ("xbuf", tc2))
                            c.dma(xbuf[:, tc2 * D:(tc2 + 1) * D],
                                  xT[t0 + tc * 128: t0 + (tc + 1) * 128, :],
                                  lane="dx%d" % tc2)
                            c.lanewait("vector", "dx%d" % tc2)
                            c.v(lambda e, tc2=tc2: e.tensor_reduce(
                                stat[:, tc2:tc2 + 1],
                                xbuf[:, tc2 * D:(tc2 + 1) * D], AX.X, ALU.add))
                            c.lanewait("scalar", "dx%d" % tc2)
                            c.s(lambda e, tc2=tc2: e.activation(
                                sqd[:, :], xbuf[:, tc2 * D:(tc2 + 1) * D],
                                ACT.Square, accum_out=stat[:, 8 + tc2: 9 + tc2]))
                        ms = c.mark("ss")
                        c.wait_marks("vector", ms)
                        ln_rstd(LNB)
                        for tc2 in range(LNB):
                            tc = tcg * LNB + tc2
                            c.use("vector", ("xnp", 0))
                            c.v(lambda e, tc2=tc2: e.tensor_scalar(
                                xnp[:, :],
                                xbuf[:, tc2 * D:(tc2 + 1) * D],
                                stat[:, 16 + tc2:17 + tc2],
                                stat[:, 24 + tc2:25 + tc2],
                                ALU.subtract, ALU.mult))
                            c.setg(("xbuf", tc2), "sv")
                            mv = c.mark("sv")
                            for fc in range(NFC):
                                psl = 6 + fc % 2
                                c.wait_marks("tensor", mv)
                                c.use("tensor", ("ps", psl))
                                c.t(lambda e, fc=fc, psl=psl: e.transpose(
                                    pt[psl][:, 0:128],
                                    xnp[:, fc * 128:(fc + 1) * 128],
                                    I_f32[:, :]))
                                mt = c.mark("stn")
                                c.wait_marks("vector", mt)
                                c.v(lambda e, fc=fc, tc=tc, psl=psl: e.tensor_scalar(
                                    xn_fm[:, fc * TH + tc * 128:
                                          fc * TH + (tc + 1) * 128],
                                    pt[psl][:, 0:128], gam_sb[:, fc:fc + 1],
                                    bet_sb[:, fc:fc + 1], ALU.mult, ALU.add))
                                c.setg(("ps", psl), "sv")
                            c.setg(("xnp", 0), "stn")
                    mxn = c.mark("sv")
                    for jc in range(2 * NKC):
                        wsl = jc % 2
                        c.use("sync", ("wint", wsl))
                        c.dma(wint[:, wsl * 1024:(wsl + 1) * 1024]
                              .rearrange("p (k j) -> p k j", k=NFC),
                              win_h[:, jc * 128:(jc + 1) * 128]
                              .rearrange("(k p) j -> p k j", p=128),
                              lane="dw%d" % wsl)
                        for tt in range(NTT):
                            psl = (jc * NTT + tt) % 4
                            c.use("tensor", ("ps", psl))
                            c.lanewait("tensor", "dw%d" % wsl)
                            c.wait_marks("tensor", mxn)
                            for kc in range(NFC):
                                c.t(lambda e, kc=kc, tt=tt, wsl=wsl, psl=psl:
                                    e.matmul(pt[psl][:, 0:TTL],
                                             wint[:, wsl * 1024 + kc * 128:
                                                  wsl * 1024 + (kc + 1) * 128],
                                             xn_fm[:, kc * TH + tt * TTL:
                                                   kc * TH + (tt + 1) * TTL],
                                             start=(kc == 0),
                                             stop=(kc == NFC - 1)))
                            mt = c.mark("stn")
                            c.wait_marks("scalar", mt)
                            if jc < NKC:
                                c.s(lambda e, jc=jc, tt=tt, psl=psl: e.activation(
                                    u0[:, jc * (TH + 3) + 3 + tt * TTL:
                                       jc * (TH + 3) + 3 + (tt + 1) * TTL],
                                    pt[psl][:, 0:TTL], ACT.Identity,
                                    bias=bin_sb[:, jc:jc + 1]))
                                c.setg(("ps", psl), "ss")
                            else:
                                ssl = (jc * NTT + tt) % 2
                                c.use("scalar", ("srstg", ssl))
                                c.s(lambda e, jc=jc, psl=psl, ssl=ssl: e.activation(
                                    srstg[:, ssl * TTL:(ssl + 1) * TTL],
                                    pt[psl][:, 0:TTL], ACT.Silu,
                                    bias=bin_sb[:, jc:jc + 1]))
                                c.setg(("ps", psl), "ss")
                                msl = c.mark("ss")
                                c.wait_marks("sync", msl)
                                lnr = "dr%d" % ssl
                                c.dma(sres_h[(jc - NKC) * 128:
                                             (jc - NKC + 1) * 128,
                                             t0 + tt * TTL:
                                             t0 + (tt + 1) * TTL],
                                      srstg[:, ssl * TTL:(ssl + 1) * TTL],
                                      lane=lnr)
                                c.guards[("srstg", ssl)] = {lnr: c.cnt[lnr]}
                        c.setg(("wint", wsl), "stn")
                    msu = c.mark("ss")
                    c.wait_marks("vector", {"ss": msu["ss"]})
                    if h == 0:
                        for kc in range(NKC):
                            c.v(lambda e, kc=kc: e.memset(u0k(kc)[:, 0:3], 0.0))
                    else:
                        for kc in range(NKC):
                            c.v(lambda e, kc=kc: e.tensor_copy(
                                u0k(kc)[:, 0:3], tails[:, kc * 3:kc * 3 + 3]))
                    c.selfw("vector")
                    for kc in range(NKC):
                        if h == 0:
                            c.v(lambda e, kc=kc: e.tensor_copy(
                                tails[:, kc * 3:kc * 3 + 3],
                                u0k(kc)[:, TH:TH + 3]))
                        c.v(lambda e, kc=kc: e.tensor_scalar(
                            acc2[:, 0:TH], u0k(kc)[:, 0:TH],
                            cw_sb[:, 0 * NKC + kc:0 * NKC + kc + 1],
                            cb_sb[:, kc:kc + 1], ALU.mult, ALU.add))
                        c.v(lambda e, kc=kc: e.scalar_tensor_tensor(
                            acc2[:, 0:TH], u0k(kc)[:, 1:TH + 1],
                            cw_sb[:, 1 * NKC + kc:1 * NKC + kc + 1],
                            acc2[:, 0:TH], ALU.mult, ALU.add))
                        c.v(lambda e, kc=kc: e.scalar_tensor_tensor(
                            acc2[:, 0:TH], u0k(kc)[:, 2:TH + 2],
                            cw_sb[:, 2 * NKC + kc:2 * NKC + kc + 1],
                            acc2[:, 0:TH], ALU.mult, ALU.add))
                        c.v(lambda e, kc=kc: e.scalar_tensor_tensor(
                            u0k(kc)[:, 3:TH + 3], u0k(kc)[:, 3:TH + 3],
                            cw_sb[:, 3 * NKC + kc:3 * NKC + kc + 1],
                            acc2[:, 0:TH], ALU.mult, ALU.add))
                        mv = c.mark("sv")
                        c.wait_marks("scalar", mv)
                        c.s(lambda e, kc=kc: e.activation(
                            u0k(kc)[:, 3:TH + 3], u0k(kc)[:, 3:TH + 3], ACT.Silu))
                    if taps:
                        c.wait_marks("gpsimd", c.mark("ss"))
                        for kc in range(NKC):
                            c.dma(tap["tap_u"][kc * 128:(kc + 1) * 128,
                                               t0:t0 + TH], u0k(kc)[:, 3:TH + 3],
                                  eng="gpsimd")
                    msu = c.mark("ss")
                    c.wait_marks("tensor", msu)
                    for tt in range(NTT):
                        psl = 4 + tt % 2
                        c.use("tensor", ("ps", psl))
                        for kc in range(NKC):
                            c.t(lambda e, kc=kc, tt=tt, psl=psl: e.matmul(
                                pt[psl][:, 0:TTL],
                                wx_bf[:, kc * 128:(kc + 1) * 128],
                                u0k(kc)[:, 3 + tt * TTL: 3 + (tt + 1) * TTL],
                                start=(kc == 0), stop=(kc == NKC - 1)))
                        mt = c.mark("stn")
                        c.wait_marks("vector", mt)
                        c.v(lambda e, tt=tt, psl=psl: e.tensor_copy(
                            dt_sb[0:64, tt * TTL:(tt + 1) * TTL],
                            pt[psl][0:64, 0:TTL]))
                        c.v(lambda e, tt=tt, psl=psl: e.tensor_scalar_mul(
                            bcB_sb[0:NS, tt * TTL:(tt + 1) * TTL],
                            pt[psl][64:64 + NS, 0:TTL], -1.0))
                        c.v(lambda e, tt=tt, psl=psl: e.tensor_copy(
                            bcC_sb[0:NS, tt * TTL:(tt + 1) * TTL],
                            pt[psl][96:96 + NS, 0:TTL]))
                        c.setg(("ps", psl), "sv")
                    c.wait_marks("sync", c.mark("sv"))
                    c.dma(bc_h[0:NS, t0:t0 + TH], bcB_sb[0:NS, :])
                    c.dma(bc_h[NS:2 * NS, t0:t0 + TH], bcC_sb[0:NS, :])
                    mdt = c.mark("sv")
                    c.wait_marks("tensor", mdt)
                    for jc in range(NKC):
                        for tt in range(NTT):
                            psl = (jc * NTT + tt) % 4
                            c.use("tensor", ("ps", psl))
                            c.t(lambda e, jc=jc, tt=tt, psl=psl: e.matmul(
                                pt[psl][:, 0:TTL],
                                wdt_bf[0:64, jc * 128:(jc + 1) * 128],
                                dt_sb[0:64, tt * TTL:(tt + 1) * TTL],
                                start=True, stop=True))
                            mt = c.mark("stn")
                            c.wait_marks("scalar", mt)
                            c.s(lambda e, jc=jc, tt=tt, psl=psl: e.activation(
                                delta[:, jc * TH + tt * TTL:
                                      jc * TH + (tt + 1) * TTL],
                                pt[psl][:, 0:TTL], ACT.Sigmoid,
                                bias=bdt_sb[:, jc:jc + 1], scale=-1.0))
                            c.setg(("ps", psl), "ss")
                    for jc in range(NKC):
                        c.s(lambda e, jc=jc: e.activation(
                            delta[:, jc * TH:(jc + 1) * TH],
                            delta[:, jc * TH:(jc + 1) * TH], ACT.Ln))
                    c.selfw("scalar")
                    c.s(lambda e: e.nop())
                    if taps:
                        c.wait_marks("gpsimd", c.mark("ss"))
                        for kc in range(NKC):
                            c.dma(tap["tap_delta"][kc * 128:(kc + 1) * 128,
                                                   t0:t0 + TH],
                                  delta[:, kc * TH:(kc + 1) * TH], eng="gpsimd")
                    if cut == 3:
                        mscope.close()
                        raise _Cut()
                    for kc in range(NKC):
                        c.use("sync", ("woutld", kc))
                        c.dma(wout_bf[:, kc * D:(kc + 1) * D],
                              wout_h[kc * 128:(kc + 1) * 128, :])
                    c.lanewait("sync", "dh", "dr0", "dr1")
                    mdl = c.mark("ss")
                    c.wait_marks("vector", mdl)
                    ibb = [0]
                    for kg in range(NKC // 4):
                        ks = [kg * 4 + j for j in range(4)]
                        for j, k in enumerate(ks[:2]):
                            c.use("sync", ("srkg", j % 2))
                            c.dma(srkg[:, (j % 2) * TH:(j % 2 + 1) * TH],
                                  sres_h[k * 128:(k + 1) * 128, t0:t0 + TH])
                        for j, k in enumerate(ks):
                            c.use("vector", ("ypsum", j))
                            c.v(lambda e, j=j, k=k: e.tensor_tensor(
                                gkt[:, j * TH:(j + 1) * TH],
                                delta[:, k * TH:(k + 1) * TH],
                                u0k(k)[:, 3:TH + 3], ALU.mult))
                            b0 = ibb[0] % 3
                            ibb[0] += 1
                            c.use("vector", ("bb", b0))
                            c.v(lambda e, k=k, b0=b0: e.tensor_scalar_mul(
                                bb[:, b0 * TH:(b0 + 1) * TH],
                                u0k(k)[:, 3:TH + 3], dv_sb[:, k:k + 1]))
                            mv = c.mark("sv")
                            c.wait_marks("tensor", mv)
                            for tb in range(YB):
                                c.t(lambda e, j=j, tb=tb, b0=b0: e.matmul(
                                    pt[2 * j + tb][:, 0:min(512, TH)], I_bf[:, :],
                                    bb[:, b0 * TH + tb * 512:
                                       b0 * TH + tb * 512 + min(512, TH)],
                                    start=True, stop=False))
                            c.setg(("bb", b0), "stn")
                        for n in range(NS):
                            bsl = n % 3
                            bln = ("db0", "db1", "ds0")[bsl]
                            c.use("sync", ("bcb", bsl))
                            c.dma(Bb[:, bsl * TH:(bsl + 1) * TH],
                                  bc_h[n:n + 1, t0:t0 + TH]
                                  .broadcast_to((128, TH)), lane=bln)
                            c.dma(Cb[:, bsl * TH:(bsl + 1) * TH],
                                  bc_h[NS + n:NS + n + 1, t0:t0 + TH]
                                  .broadcast_to((128, TH)), lane=bln)
                            for j, k in enumerate(ks):
                                asl = (n * 4 + j) % 3
                                hsl = (n * 4 + j) % 2
                                kn = k * NS + n
                                c.use("scalar", ("ab", asl))
                                c.s(lambda e, k=k, kn=kn, asl=asl: e.activation(
                                    ab[:, asl * TH:(asl + 1) * TH],
                                    delta[:, k * TH:(k + 1) * TH], ACT.Exp,
                                    scale=A_sb[:, kn:kn + 1]))
                                msa = c.mark("ss")
                                b1 = ibb[0] % 3
                                ibb[0] += 1
                                c.use("vector", ("bb", b1))
                                c.lanewait("vector", bln)
                                c.v(lambda e, j=j, bsl=bsl, b1=b1:
                                    e.tensor_tensor(
                                        bb[:, b1 * TH:(b1 + 1) * TH],
                                        gkt[:, j * TH:(j + 1) * TH],
                                        Bb[:, bsl * TH:(bsl + 1) * TH],
                                        ALU.mult))
                                c.wait_marks("vector", msa)
                                c.use("vector", ("hb", hsl))
                                c.v(lambda e, kn=kn, asl=asl, hsl=hsl, b1=b1:
                                    e.tensor_tensor_scan(
                                        hb[:, hsl * TH:(hsl + 1) * TH],
                                        ab[:, asl * TH:(asl + 1) * TH],
                                        bb[:, b1 * TH:(b1 + 1) * TH],
                                        state[:, kn:kn + 1], ALU.mult,
                                        ALU.add))
                                c.setg(("ab", asl), "sv")
                                c.setg(("bb", b1), "sv")
                                b2 = ibb[0] % 3
                                ibb[0] += 1
                                c.use("vector", ("bb", b2))
                                c.v(lambda e, hsl=hsl, bsl=bsl, b2=b2:
                                    e.tensor_tensor(
                                        bb[:, b2 * TH:(b2 + 1) * TH],
                                        hb[:, hsl * TH:(hsl + 1) * TH],
                                        Cb[:, bsl * TH:(bsl + 1) * TH],
                                        ALU.mult))
                                if h == 0:
                                    mv = c.mark("sv")
                                    c.wait_marks("gpsimd", mv)
                                    c.g(lambda e, kn=kn, hsl=hsl: e.tensor_copy(
                                        state[:, kn:kn + 1],
                                        hb[:, hsl * TH + TH - 1: hsl * TH + TH]))
                                    c.setg(("hb", hsl), "sv", "sg")
                                else:
                                    c.setg(("hb", hsl), "sv")
                                mv = c.mark("sv")
                                c.wait_marks("tensor", mv)
                                for tb in range(YB):
                                    c.t(lambda e, j=j, tb=tb, n=n, b2=b2:
                                        e.matmul(
                                            pt[2 * j + tb][:, 0:min(512, TH)],
                                            I_bf[:, :],
                                            bb[:, b2 * TH + tb * 512:
                                               b2 * TH + tb * 512 +
                                               min(512, TH)],
                                            start=False, stop=(n == NS - 1)))
                                c.setg(("bb", b2), "stn")
                            c.setg(("bcb", bsl), "sv")
                        mt = c.mark("stn")
                        c.wait_marks("vector", mt)
                        for j, k in enumerate(ks):
                            c.wait("vector", "dh")
                            for tb in range(YB):
                                w = min(512, TH)
                                c.v(lambda e, j=j, k=k, tb=tb, w=w: e.tensor_tensor(
                                    delta[:, k * TH + tb * 512:
                                          k * TH + tb * 512 + w],
                                    pt[2 * j + tb][:, 0:w],
                                    srkg[:, (j % 2) * TH + tb * 512:
                                         (j % 2) * TH + tb * 512 + w], ALU.mult))
                            c.setg(("ypsum", j), "sv")
                            c.setg(("srkg", j % 2), "sv")
                            if j + 2 < 4:
                                c.use("sync", ("srkg", j % 2))
                                c.dma(srkg[:, (j % 2) * TH:(j % 2 + 1) * TH],
                                      sres_h[ks[j + 2] * 128:
                                             (ks[j + 2] + 1) * 128, t0:t0 + TH])
                    if taps:
                        c.wait_marks("gpsimd", c.mark("sv"))
                        for kc in range(NKC):
                            c.dma(tap["tap_ymod"][kc * 128:(kc + 1) * 128,
                                                  t0:t0 + TH],
                                  delta[:, kc * TH:(kc + 1) * TH], eng="gpsimd")
                    if cut == 4:
                        mscope.close()
                        raise _Cut()
                    mym = c.mark("sv")
                    c.wait_marks("tensor", mym)
                    c.lanewait("tensor", "dh")
                    for kc in range(NKC):
                        c.setg(("woutld", kc), "stn")
                    for tcg in range(NTC // LNB):
                        for tc2 in range(LNB):
                            tc = tcg * LNB + tc2
                            c.use("sync", ("xbuf", tc2))
                            c.dma(xbuf[:, tc2 * D:(tc2 + 1) * D],
                                  xT[t0 + tc * 128: t0 + (tc + 1) * 128, :],
                                  lane="dx%d" % tc2)
                            c.lanewait("vector", "dx%d" % tc2)
                            c.use("vector", "xb2")
                            c.v(lambda e, tc2=tc2: e.scalar_tensor_tensor(
                                xb2[:, :], xbuf[:, tc2 * D:(tc2 + 1) * D],
                                2.0, bout_b[:, :], ALU.mult, ALU.add))
                            for hc in range(2):
                                psl = (tc * 2 + hc) % 4
                                c.use("tensor", ("ps", psl))
                                for kc in range(NKC):
                                    c.t(lambda e, kc=kc, tc=tc, hc=hc, psl=psl:
                                        e.matmul(
                                            pt[psl][:, 0:512],
                                            delta[:, kc * TH + tc * 128:
                                                  kc * TH + (tc + 1) * 128],
                                            wout_bf[:, kc * D + hc * 512:
                                                    kc * D + (hc + 1) * 512],
                                            start=(kc == 0),
                                            stop=(kc == NKC - 1)))
                                mt = c.mark("stn")
                                c.wait_marks("vector", mt)
                                c.v(lambda e, tc2=tc2, hc=hc, psl=psl:
                                    e.tensor_tensor(
                                        xbuf[:, tc2 * D + hc * 512:
                                             tc2 * D + hc * 512 + 512],
                                        pt[psl][:, 0:512],
                                        xb2[:, hc * 512: hc * 512 + 512],
                                        ALU.add))
                                c.setg(("ps", psl), "sv")
                            c.setg("xb2", "sv")
                            c.selfw("vector")
                            c.v(lambda e, tc2=tc2: e.tensor_reduce(
                                stat[:, tc2:tc2 + 1],
                                xbuf[:, tc2 * D:(tc2 + 1) * D], AX.X, ALU.add))
                            mv = c.mark("sv")
                            c.wait_marks("scalar", mv)
                            c.s(lambda e, tc2=tc2: e.activation(
                                sqd[:, :], xbuf[:, tc2 * D:(tc2 + 1) * D],
                                ACT.Square, accum_out=stat[:, 8 + tc2: 9 + tc2]))
                        ms = c.mark("ss")
                        c.wait_marks("vector", ms)
                        ln_rstd(LNB)
                        for tc2 in range(LNB):
                            tc = tcg * LNB + tc2
                            c.use("vector", "x1t")
                            c.use("vector", ("xnp", 0))
                            c.v(lambda e, tc2=tc2: e.tensor_scalar(
                                xnp[:, :], xbuf[:, tc2 * D:(tc2 + 1) * D],
                                stat[:, 16 + tc2:17 + tc2],
                                stat[:, 24 + tc2:25 + tc2],
                                ALU.subtract, ALU.mult))
                            c.setg(("xbuf", tc2), "sv")
                            c.v(lambda e, tc=tc, h=h: e.tensor_reduce(
                                mxu[:, h * NTC + tc: h * NTC + tc + 1],
                                xnp[:, :], AX.X, ALU.max,
                                apply_absolute_value=True))
                            mv = c.mark("sv")
                            c.wait_marks("sync", mv)
                            c.dma(x1_h[t0 + tc * 128: t0 + (tc + 1) * 128, :],
                                  xnp[:, :], lane="dst")
                            if taps:
                                c.dma(tap["tap_x1"][t0 + tc * 128:
                                                    t0 + (tc + 1) * 128, :],
                                      xnp[:, :], lane="dst")
                            c.guards["x1t"] = {"dst": c.cnt["dst"]}
                    c.barrier()

                if cut == 5:
                    mscope.close()
                    raise _Cut()
                reduce_to_all(mxu[:, 0:2 * NTC], 0)
                collective(0, 1)
                c.v(lambda e: e.tensor_scalar(scb[:, 0:1], scg[:, 0:1],
                                              1.0 / QMAX, 1e-12, ALU.mult,
                                              ALU.max))
                c.selfw("vector")
                c.v(lambda e: e.reciprocal(scb[:, 1:2], scb[:, 0:1]))
                c.selfw("vector")
            c.barrier()
            if cut == 55:
                raise _Cut()

            fscope = ExitStack()
            with fscope:
                _e = fscope.enter_context
                xq = _e(nc.sbuf_tensor("xq", [128, NFC * L], BF16))
                xr = _e(nc.sbuf_tensor("xr", [128, 2 * D], F32))
                xqp = _e(nc.sbuf_tensor("xqp", [128, 2 * D], F32))
                fwt = _e(nc.sbuf_tensor("fwt", [128, 4 * 1024], BF16))
                stg = _e(nc.sbuf_tensor("stg", [128, 8 * 512], F32))

                for tc in range(L // 128):
                    xsl = tc % 2
                    c.use("sync", ("xr", xsl))
                    c.dma(xr[:, xsl * D:(xsl + 1) * D],
                          x1_h[tc * 128:(tc + 1) * 128, :],
                          lane="dx%d" % xsl)
                    c.lanewait("vector", "dx%d" % xsl)
                    c.use("vector", ("xqp", xsl))
                    c.v(lambda e, xsl=xsl: e.tensor_scalar(
                        xr[:, xsl * D:(xsl + 1) * D],
                        xr[:, xsl * D:(xsl + 1) * D],
                        scb[:, 1:2], MAGIC, ALU.mult, ALU.add))
                    c.selfw("vector")
                    c.v(lambda e, xsl=xsl: e.tensor_scalar(
                        xqp[:, xsl * D:(xsl + 1) * D],
                        xr[:, xsl * D:(xsl + 1) * D],
                        MAGIC, scb[:, 0:1], ALU.subtract, ALU.mult))
                    c.setg(("xr", xsl), "sv")
                    mv = c.mark("sv")
                    for fc in range(NFC):
                        psl = 6 + fc % 2
                        c.wait_marks("tensor", mv)
                        c.use("tensor", ("ps", psl))
                        c.t(lambda e, fc=fc, xsl=xsl, psl=psl: e.transpose(
                            pt[psl][:, 0:128],
                            xqp[:, xsl * D + fc * 128: xsl * D + (fc + 1) * 128],
                            I_f32[:, :]))
                        mt = c.mark("stn")
                        c.wait_marks("scalar", mt)
                        c.s(lambda e, fc=fc, tc=tc, psl=psl: e.activation(
                            xq[:, fc * L + tc * 128: fc * L + (tc + 1) * 128],
                            pt[psl][:, 0:128], ACT.Copy))
                        c.setg(("ps", psl), "ss")
                    c.setg(("xqp", xsl), "stn")

                if taps:
                    c.wait_marks("gpsimd", c.mark("ss"))
                    c.dma(xq_h[:, :], xq[:, :], eng="gpsimd")
                if cut == 6:
                    fscope.close()
                    raise _Cut()
                mxq = c.mark("ss")
                spc = [0]
                for ic in range(NIC):
                    wsl = ic % 2
                    c.use("sync", ("fwt", wsl))
                    c.dma(fwt[:, wsl * 1024: wsl * 1024 + 1024]
                          .rearrange("p (fc i) -> p fc i", fc=NFC),
                          wupT_h[:, ic * 128:(ic + 1) * 128]
                          .rearrange("(fc p) i -> p fc i", p=128),
                          lane="dw%d" % wsl)
                    c.dma(fwt[:, (2 + wsl) * 1024:
                              (2 + wsl) * 1024 + 1024]
                          .rearrange("p (fc i) -> p fc i", fc=NFC),
                          wgtT_h[:, ic * 128:(ic + 1) * 128]
                          .rearrange("(fc p) i -> p fc i", p=128),
                          lane="dw%d" % wsl)
                    for tt in range(NT2):
                        for gi, (spill, mx) in enumerate(
                                ((sp_up, mxu), (sp_gate, mxg))):
                            psl = (tt * 2 + gi) % 4
                            c.use("tensor", ("ps", psl))
                            c.lanewait("tensor", "dw%d" % wsl)
                            c.wait_marks("tensor", mxq)
                            for fc in range(NFC):
                                c.t(lambda e, fc=fc, tt=tt, gi=gi, wsl=wsl,
                                    psl=psl: e.matmul(
                                        pt[psl][:, 0:TL2],
                                        fwt[:, (2 * gi + wsl) * 1024 + fc * 128:
                                            (2 * gi + wsl) * 1024 + (fc + 1) * 128],
                                        xq[:, fc * L + tt * TL2:
                                           fc * L + (tt + 1) * TL2],
                                        start=(fc == 0), stop=(fc == NFC - 1)))
                            mt = c.mark("stn")
                            ssl = (ic * NT2 * 2 + tt * 2 + gi) % 4
                            c.wait_marks("scalar", mt)
                            c.use("scalar", ("stg", ssl))
                            c.s(lambda e, psl=psl, ssl=ssl: e.activation(
                                stg[:, ssl * 512: ssl * 512 + TL2],
                                pt[psl][:, 0:TL2], ACT.Copy))
                            c.setg(("ps", psl), "ss")
                            c.selfw("scalar")
                            c.s(lambda e: e.nop())
                            ms = c.mark("ss")
                            c.wait_marks("vector", ms)
                            c.v(lambda e, ic=ic, tt=tt, ssl=ssl, mx=mx:
                                e.tensor_reduce(
                                    mx[:, ic * NT2 + tt: ic * NT2 + tt + 1],
                                    stg[:, ssl * 512: ssl * 512 + TL2], AX.X,
                                    ALU.max, apply_absolute_value=True))
                            lnp = ("dp0", "dp1", "dt0", "dt1", "ds0", "ds1",
                                   "du0", "du1")[spc[0] % 8]
                            spc[0] += 1
                            c.dma(spill[ic * 128:(ic + 1) * 128,
                                        tt * TL2:(tt + 1) * TL2],
                                  stg[:, ssl * 512: ssl * 512 + TL2],
                                  eng="scalar", lane=lnp)
                            c.guards[("stg", ssl)] = {lnp: c.cnt[lnp],
                                                      "sv": c.cnt["sv"]}
                    c.setg(("fwt", wsl), "stn")

                reduce_to_all(mxu[:, :], 0)
                reduce_to_all(mxg[:, :], 1)
                collective(1, 2)
                c.v(lambda e: e.tensor_scalar(scb[:, 2:3], scg[:, 0:1],
                                              1.0 / QMAX, 1e-12, ALU.mult,
                                              ALU.max))
                c.v(lambda e: e.tensor_scalar(scb[:, 4:5], scg[:, 1:2],
                                              1.0 / QMAX, 1e-12, ALU.mult,
                                              ALU.max))
                c.selfw("vector")
                c.v(lambda e: e.reciprocal(scb[:, 3:4], scb[:, 2:3]))
                c.v(lambda e: e.reciprocal(scb[:, 5:6], scb[:, 4:5]))
                c.selfw("vector")
                mv = c.mark("sv")
                c.wait_marks("scalar", mv)
                c.wait_marks("gpsimd", mv)

                fscope.close()
                f34 = ExitStack()
                _e34 = f34.enter_context
                gld = _e34(nc.sbuf_tensor("gld", [128, 4 * L], F32))
                uld = _e34(nc.sbuf_tensor("uld", [128, 4 * L], F32))
                g2o = _e34(nc.sbuf_tensor("g2o", [128, 4 * L], F32))
                gint = _e34(nc.sbuf_tensor("gint", [128, 2 * L], F32))
                sig = _e34(nc.sbuf_tensor("sig", [128, 2 * L], F32))
                ldlanes = ("ds0", "ds1", "dr0", "dr1")
                uplanes = ("du0", "du1", "db0", "db1")
                splanes = ("dp0", "dp1", "dt0", "dt1")

                c.lanewait("sync", "dp0", "dp1", "dt0", "dt1",
                           "ds0", "ds1", "du0", "du1")
                for ic in range(NIC):
                    s4 = ic % 4
                    s2_ = ic % 2
                    c.use("sync", ("gld", s4))
                    c.dma(gld[:, s4 * L:(s4 + 1) * L],
                          sp_gate[ic * 128:(ic + 1) * 128, :],
                          lane=ldlanes[s4])
                    c.lanewait("vector", ldlanes[s4])
                    c.use("vector", ("gint", s2_))
                    c.v(lambda e, s4=s4, s2_=s2_: e.tensor_scalar(
                        gint[:, s2_ * L:(s2_ + 1) * L],
                        gld[:, s4 * L:(s4 + 1) * L],
                        scb[:, 5:6], MAGIC, ALU.mult, ALU.add))
                    c.setg(("gld", s4), "sv")
                    c.selfw("vector")
                    c.v(lambda e, s2_=s2_: e.tensor_single_scalar(
                        gint[:, s2_ * L:(s2_ + 1) * L],
                        gint[:, s2_ * L:(s2_ + 1) * L], MAGIC,
                        ALU.subtract))
                    mv = c.mark("sv")
                    c.wait_marks("scalar", mv)
                    c.use("scalar", ("sig", s2_))
                    c.s(lambda e, s2_=s2_: e.activation(
                        sig[:, s2_ * L:(s2_ + 1) * L],
                        gint[:, s2_ * L:(s2_ + 1) * L], ACT.Sigmoid,
                        scale=scb[:, 4:5]))
                    c.selfw("scalar")
                    c.s(lambda e: e.nop())
                    ms = c.mark("ss")
                    c.setg(("gint", s2_), "ss")
                    c.wait_marks("vector", ms)
                    c.v(lambda e, s2_=s2_: e.tensor_scalar(
                        sig[:, s2_ * L:(s2_ + 1) * L],
                        sig[:, s2_ * L:(s2_ + 1) * L],
                        65536.0, MAGIC, ALU.mult, ALU.add))
                    c.selfw("vector")
                    c.v(lambda e, s2_=s2_: e.tensor_scalar(
                        sig[:, s2_ * L:(s2_ + 1) * L],
                        sig[:, s2_ * L:(s2_ + 1) * L],
                        MAGIC, 65535.0, ALU.subtract, ALU.min))
                    c.selfw("vector")
                    c.use("vector", ("g2o", s4))
                    c.v(lambda e, s4=s4, s2_=s2_: e.tensor_tensor(
                        g2o[:, s4 * L:(s4 + 1) * L],
                        gint[:, s2_ * L:(s2_ + 1) * L],
                        sig[:, s2_ * L:(s2_ + 1) * L], ALU.mult))
                    c.setg(("sig", s2_), "sv")
                    c.selfw("vector")
                    c.v(lambda e, ic=ic, s4=s4: e.tensor_reduce(
                        mx4[:, ic:ic + 1],
                        g2o[:, s4 * L:(s4 + 1) * L], AX.X,
                        ALU.max, apply_absolute_value=True))
                    mv = c.mark("sv")
                    c.wait_marks("scalar", mv)
                    c.dma(sp_g2[ic * 128:(ic + 1) * 128, :],
                          g2o[:, s4 * L:(s4 + 1) * L], eng="scalar",
                          lane=splanes[s4])
                    c.guards[("g2o", s4)] = {splanes[s4]: c.cnt[splanes[s4]],
                                             "sv": c.cnt["sv"]}

                c.wait_marks("vector", c.mark("sg"))
                reduce_to_all(mx4[:, 0:NIC], 0)
                c.v(lambda e: e.tensor_tensor(sca[:, 0:1], sca[:, 0:1],
                                              scb[:, 4:5], ALU.mult))
                c.selfw("vector")
                c.v(lambda e: e.tensor_scalar_mul(sca[:, 0:1], sca[:, 0:1],
                                                  1.0 / 65536.0))
                c.selfw("vector")
                collective(2, 1)
                c.v(lambda e: e.tensor_scalar(sca[:, 2:3], scg[:, 0:1],
                                              1.0 / QMAX, 1e-12, ALU.mult,
                                              ALU.max))
                c.selfw("vector")
                c.v(lambda e: e.reciprocal(sca[:, 3:4], sca[:, 2:3]))
                c.selfw("vector")
                c.v(lambda e: e.tensor_tensor(scb[:, 6:7], scb[:, 4:5],
                                              sca[:, 3:4], ALU.mult))
                c.selfw("vector")
                c.v(lambda e: e.tensor_scalar_mul(scb[:, 6:7], scb[:, 6:7],
                                                  1.0 / 65536.0))
                c.selfw("vector")
                mv = c.mark("sv")
                c.wait_marks("gpsimd", mv)

                c.lanewait("sync", "dp0", "dp1", "dt0", "dt1")
                for ic in range(NIC):
                    s4 = ic % 4
                    s2_ = ic % 2
                    c.use("sync", ("gld", s4))
                    c.dma(gld[:, s4 * L:(s4 + 1) * L],
                          sp_g2[ic * 128:(ic + 1) * 128, :],
                          lane=ldlanes[s4])
                    c.use("sync", ("uld", s4))
                    c.dma(uld[:, s4 * L:(s4 + 1) * L],
                          sp_up[ic * 128:(ic + 1) * 128, :],
                          lane=uplanes[s4])
                    c.lanewait("vector", ldlanes[s4])
                    c.use("vector", ("gint", s2_))
                    c.v(lambda e, s4=s4, s2_=s2_: e.tensor_scalar(
                        gint[:, s2_ * L:(s2_ + 1) * L],
                        gld[:, s4 * L:(s4 + 1) * L],
                        scb[:, 6:7], MAGIC, ALU.mult, ALU.add))
                    c.setg(("gld", s4), "sv")
                    c.lanewait("vector", uplanes[s4])
                    c.use("vector", ("sig", s2_))
                    c.v(lambda e, s4=s4, s2_=s2_: e.tensor_scalar(
                        sig[:, s2_ * L:(s2_ + 1) * L],
                        uld[:, s4 * L:(s4 + 1) * L],
                        scb[:, 3:4], MAGIC, ALU.mult, ALU.add))
                    c.setg(("uld", s4), "sv")
                    c.selfw("vector")
                    c.v(lambda e, s2_=s2_: e.tensor_single_scalar(
                        gint[:, s2_ * L:(s2_ + 1) * L],
                        gint[:, s2_ * L:(s2_ + 1) * L], MAGIC,
                        ALU.subtract))
                    c.v(lambda e, s2_=s2_: e.tensor_single_scalar(
                        sig[:, s2_ * L:(s2_ + 1) * L],
                        sig[:, s2_ * L:(s2_ + 1) * L], MAGIC,
                        ALU.subtract))
                    c.selfw("vector")
                    c.use("vector", ("g2o", s4))
                    c.v(lambda e, s4=s4, s2_=s2_: e.tensor_tensor(
                        g2o[:, s4 * L:(s4 + 1) * L],
                        gint[:, s2_ * L:(s2_ + 1) * L],
                        sig[:, s2_ * L:(s2_ + 1) * L], ALU.mult))
                    c.setg(("gint", s2_), "sv")
                    c.setg(("sig", s2_), "sv")
                    c.selfw("vector")
                    c.v(lambda e, ic=ic, s4=s4: e.tensor_reduce(
                        mx5[:, ic:ic + 1],
                        g2o[:, s4 * L:(s4 + 1) * L], AX.X,
                        ALU.max, apply_absolute_value=True))
                    mv = c.mark("sv")
                    c.wait_marks("scalar", mv)
                    c.dma(sp_o[ic * 128:(ic + 1) * 128, :],
                          g2o[:, s4 * L:(s4 + 1) * L], eng="scalar",
                          lane=splanes[s4])
                    c.guards[("g2o", s4)] = {splanes[s4]: c.cnt[splanes[s4]],
                                             "sv": c.cnt["sv"]}

                c.wait_marks("vector", c.mark("sg"))
                reduce_to_all(mx5[:, 0:NIC], 0)
                c.v(lambda e: e.tensor_tensor(sca[:, 0:1], sca[:, 0:1],
                                              sca[:, 2:3], ALU.mult))
                c.selfw("vector")
                c.v(lambda e: e.tensor_tensor(sca[:, 0:1], sca[:, 0:1],
                                              scb[:, 2:3], ALU.mult))
                c.selfw("vector")
                collective(3, 1)
                c.v(lambda e: e.tensor_scalar(scb[:, 8:9], scg[:, 0:1],
                                              1.0 / QMAX, 1e-12, ALU.mult,
                                              ALU.max))
                c.selfw("vector")
                c.v(lambda e: e.reciprocal(sca[:, 4:5], scb[:, 8:9]))
                c.selfw("vector")
                c.v(lambda e: e.tensor_tensor(scb[:, 7:8], sca[:, 2:3],
                                              scb[:, 2:3], ALU.mult))
                c.selfw("vector")
                c.v(lambda e: e.tensor_tensor(scb[:, 7:8], scb[:, 7:8],
                                              sca[:, 4:5], ALU.mult))
                c.selfw("vector")
                if taps:
                    c.wait_marks("sync", c.mark("sv"))
                    c.dma(scb_h[:, :], scb[:, :])
                f34.close()
            c.barrier()

            if cut == 9:
                raise _Cut()
            f5scope = ExitStack()
            with f5scope:
                _e = f5scope.enter_context
                wdn = _e(nc.sbuf_tensor("wdn", [128, NIC * D], BF16))
                oqt = _e(nc.sbuf_tensor("oqt", [128, 2 * NIC * TL2], BF16))
                ot = _e(nc.sbuf_tensor("ot", [128, 4 * 512], F32))
                pre3 = _e(nc.sbuf_tensor("pre3", [128, (TL2 // 128) * D], F32))
                x1r = _e(nc.sbuf_tensor("x1r", [128, 2 * D], F32))
                outt = _e(nc.sbuf_tensor("outt", [128, 2 * D], F32))
                c.lanewait("sync", "dp0", "dp1", "dt0", "dt1")
                for ic in range(NIC):
                    c.dma(wdn[:, ic * D:(ic + 1) * D],
                          wdnT_h[ic * 128:(ic + 1) * 128, :])
                f5lanes = ("ds0", "ds1", "du0", "du1")
                for tt in range(NT2):
                    oqb = (tt % 2) * NIC * TL2
                    c.use("vector", ("oqt", tt % 2))
                    for ic in range(NIC):
                        osl = ic % 4
                        c.use("sync", ("ot", osl))
                        c.dma(ot[:, osl * 512: osl * 512 + TL2],
                              sp_o[ic * 128:(ic + 1) * 128,
                                   tt * TL2:(tt + 1) * TL2],
                              lane=f5lanes[osl])
                        c.lanewait("vector", f5lanes[osl])
                        c.v(lambda e, osl=osl: e.tensor_scalar(
                            ot[:, osl * 512: osl * 512 + TL2],
                            ot[:, osl * 512: osl * 512 + TL2],
                            scb[:, 7:8], MAGIC, ALU.mult, ALU.add))
                        c.selfw("vector")
                        c.v(lambda e, ic=ic, osl=osl, oqb=oqb: e.tensor_scalar(
                            oqt[:, oqb + ic * TL2: oqb + (ic + 1) * TL2],
                            ot[:, osl * 512: osl * 512 + TL2],
                            MAGIC, scb[:, 8:9], ALU.subtract, ALU.mult))
                        c.setg(("ot", osl), "sv")
                    moq = c.mark("sv")
                    c.wait_marks("tensor", moq)
                    for tcl in range(TL2 // 128):
                        tc = tt * (TL2 // 128) + tcl
                        xsl = tc % 2
                        c.use("sync", ("x1r", xsl))
                        c.dma(x1r[:, xsl * D:(xsl + 1) * D],
                              x1_h[tc * 128:(tc + 1) * 128, :],
                              lane="dx%d" % xsl)
                        for hc in range(2):
                            psl = (tcl * 2 + hc) % 4
                            c.use("tensor", ("ps", psl))
                            if tt == 0 and tcl == 0 and hc == 0:
                                c.lanewait("tensor", "dh")
                            for ic in range(NIC):
                                c.t(lambda e, ic=ic, tcl=tcl, hc=hc, psl=psl,
                                    oqb=oqb: e.matmul(
                                        pt[psl][:, 0:512],
                                        oqt[:, oqb + ic * TL2 + tcl * 128:
                                            oqb + ic * TL2 + (tcl + 1) * 128],
                                        wdn[:, ic * D + hc * 512:
                                            ic * D + (hc + 1) * 512],
                                        start=(ic == 0), stop=(ic == NIC - 1)))
                            mt = c.mark("stn")
                            c.wait_marks("vector", mt)
                            c.lanewait("vector", "dx%d" % xsl)
                            c.v(lambda e, tcl=tcl, hc=hc, psl=psl, xsl=xsl:
                                e.tensor_tensor(
                                    pre3[:, tcl * D + hc * 512:
                                         tcl * D + hc * 512 + 512],
                                    pt[psl][:, 0:512],
                                    x1r[:, xsl * D + hc * 512:
                                        xsl * D + hc * 512 + 512], ALU.add))
                            c.setg(("ps", psl), "sv")
                        c.setg(("x1r", xsl), "sv")
                        c.selfw("vector")
                        c.v(lambda e, tcl=tcl: e.tensor_reduce(
                            stat[:, tcl:tcl + 1], pre3[:, tcl * D:(tcl + 1) * D],
                            AX.X, ALU.add))
                        mv = c.mark("sv")
                        c.wait_marks("scalar", mv)
                        c.s(lambda e, tcl=tcl: e.activation(
                            sqd[:, :], pre3[:, tcl * D:(tcl + 1) * D], ACT.Square,
                            accum_out=stat[:, 8 + tcl: 9 + tcl]))
                    ms = c.mark("ss")
                    c.wait_marks("vector", ms)
                    ln_rstd(TL2 // 128)
                    for tcl in range(TL2 // 128):
                        tc = tt * (TL2 // 128) + tcl
                        osl = tcl % 2
                        c.use("vector", ("outt", osl))
                        c.v(lambda e, tcl=tcl, osl=osl: e.tensor_scalar(
                            outt[:, osl * D:(osl + 1) * D],
                            pre3[:, tcl * D:(tcl + 1) * D],
                            stat[:, 16 + tcl:17 + tcl],
                            stat[:, 24 + tcl:25 + tcl],
                            ALU.subtract, ALU.mult))
                        mv = c.mark("sv")
                        c.wait_marks("sync", mv)
                        c.dma(outT[tc * 128:(tc + 1) * 128, :],
                              outt[:, osl * D:(osl + 1) * D], lane="dst")
                        c.guards[("outt", osl)] = {"dst": c.cnt["dst"]}
                    c.setg(("oqt", tt % 2), "stn")

        except _Cut:
            pass
        if cut < 99:
            c.barrier()
            for tc in range(L // 128):
                c.dma(outT[tc * 128:(tc + 1) * 128, :], sqd[:, :])

        c.barrier()
        for ln in c.dma_lanes:
            c.wait("sync", ln)
        c.wait("gpsimd", "dg")

        with nc.Block() as blk:
            for eng in ENGINES:
                ops = c.q[eng]

                def make(ops):
                    def body(e):
                        for f in ops:
                            f(e)
                    return body
                getattr(blk, eng)(make(ops))

    return nc


_NC_CACHE = {}


def _get_nc(L, n_cores, taps=False, cut=99):
    key = (L, n_cores, taps, cut)
    if key not in _NC_CACHE:
        _NC_CACHE[key] = build_nc(L, n_cores, taps, cut=cut)
    return _NC_CACHE[key]


def make_in_maps(inputs, n_cores=8):
    f = lambda k: np.ascontiguousarray(np.asarray(inputs[k], dtype=np.float32))
    x = f("L0_attn_x")
    base = {
        "ln_g": f("L0_attn_ln_gamma"), "ln_b": f("L0_attn_ln_beta"),
        "w_in": f("L0_attn_W_in"), "b_in": f("L0_attn_b_in"),
        "conv_w": f("L0_attn_conv_w"), "conv_b": f("L0_attn_conv_b"),
        "w_x": f("L0_attn_W_x"), "w_dt": f("L0_attn_W_dt"),
        "b_dt": f("L0_attn_b_dt"), "a_log": f("L0_attn_A_log"),
        "d_vec": f("L0_attn_D"), "w_out": f("L0_attn_W_out"),
        "b_out": f("L0_attn_b_out"), "w_gate": f("L0_ffn_w_gate"),
        "w_up": f("L0_ffn_w_up"), "w_down": f("L0_ffn_w_down"),
    }
    return [dict(base, x=np.ascontiguousarray(x[i])) for i in range(n_cores)]


def kernel(**inputs):
    n_cores = 8
    in_maps = make_in_maps(inputs, n_cores)
    nc = _get_nc(in_maps[0]["x"].shape[0], n_cores)
    res = run_bass_kernel_spmd(nc, in_maps, core_ids=list(range(n_cores)))
    return np.stack([res.results[i]["out"] for i in range(n_cores)], axis=0)


# version 19
# speedup vs baseline: 1.1289x; 1.0110x over previous
from contextlib import ExitStack

import numpy as np

import concourse.bass as bass
import concourse.mybir as mybir
from concourse.bass_utils import run_bass_kernel_spmd

F32 = mybir.dt.float32
BF16 = mybir.dt.bfloat16
ALU = mybir.AluOpType
ACT = mybir.ActivationFunctionType
AX = mybir.AxisListType

MAGIC = float(np.float32(1.5 * 2**23))
QMAX = 32767.0
EPS = 1e-5

D, DI, NS, RK, KC, HI = 1024, 2048, 16, 64, 4, 3072
NKC, NFC, NIC = DI // 128, D // 128, HI // 128

ENGINES = ("sync", "vector", "scalar", "tensor", "gpsimd")
DBG = {"no_collective": False, "no_r2s": False, "no_bcast": False}


class Ctx:

    def __init__(self, nc, stack):
        self.nc = nc
        self.q = {e: [] for e in ENGINES}
        self.sem = {}
        self.cnt = {}
        self.dma_lanes = ("dh", "dx0", "dx1", "dw0", "dw1", "db0", "db1",
                          "ds0", "ds1", "du0", "du1", "dr0", "dr1",
                          "dt0", "dt1", "dst", "dp0", "dp1", "dg")
        for name in self.dma_lanes + ("sv", "ss", "stn", "sg", "cc"):
            self.sem[name] = stack.enter_context(nc.semaphore(name))
            self.cnt[name] = 0
        self.guards = {}
        self.eng_sem = {"vector": "sv", "scalar": "ss", "tensor": "stn",
                        "gpsimd": "sg"}

    def wait(self, eng, name, val=None):
        if val is None:
            val = self.cnt[name]
        if val <= 0:
            return
        h = self.sem[name]
        self.q[eng].append(lambda e, h=h, v=val: e.wait_ge(h, v))

    def run(self, eng, fn, inc=None):
        if inc is None:
            self.q[eng].append(fn)
            return None
        step = 16 if inc in self.dma_lanes else 1
        self.cnt[inc] += step
        h = self.sem[inc]
        self.q[eng].append(lambda e, fn=fn, h=h, s=step: fn(e).then_inc(h, s))
        return self.cnt[inc]

    def v(self, fn):
        return self.run("vector", fn, "sv")

    def s(self, fn):
        return self.run("scalar", fn, "ss")

    def t(self, fn):
        return self.run("tensor", fn, "stn")

    def g(self, fn):
        return self.run("gpsimd", fn, "sg")

    def dma(self, out, in_, eng="sync", slow=False, lane=None):
        sem = lane or ("dg" if eng == "gpsimd" else "dh")
        return self.run(eng, lambda e, o=out, i=in_, s=slow: e.dma_start(
            out=o, in_=i, allow_slow_non_contiguous=s), sem)

    def mark(self, *names):
        if not names:
            names = ("sv", "ss", "stn", "sg", "dh", "dg")
        return {n: self.cnt[n] for n in names}

    def wait_marks(self, eng, marks):
        for name, val in marks.items():
            if val > 0:
                self.wait(eng, name, val)

    def use(self, eng, key):
        m = self.guards.get(key)
        if m:
            self.wait_marks(eng, m)

    def setg(self, key, *names):
        self.guards[key] = self.mark(*names)

    def selfw(self, eng):
        self.wait(eng, self.eng_sem[eng])

    def lanewait(self, eng, *lanes):
        for ln in lanes:
            self.wait(eng, ln)

    def barrier(self):
        for eng, sname in (("vector", "sv"), ("scalar", "ss"),
                           ("tensor", "stn"), ("gpsimd", "sg")):
            self.run(eng, lambda e: e.nop(), sname)
        m = {n: self.cnt[n] for n in
             ("sv", "ss", "stn", "sg") + self.dma_lanes}
        for eng in ENGINES:
            self.wait_marks(eng, m)


def build_nc(L=2048, n_cores=8, taps=False, cut=99):
    TH = L // 2
    NTC = TH // 128
    TTL = min(512, TH)
    NTT = TH // TTL
    TL2 = min(512, L)
    NT2 = L // TL2
    YB = max(1, TH // 512)
    LNB = min(2, NTC)

    nc = bass.Bass("TRN2", target_bir_lowering=False,
                   detect_race_conditions=False)

    xT = nc.dram_tensor("x", [L, D], F32, kind="ExternalInput").ap()
    g_ln = nc.dram_tensor("ln_g", [D], F32, kind="ExternalInput").ap()
    b_ln = nc.dram_tensor("ln_b", [D], F32, kind="ExternalInput").ap()
    w_in = nc.dram_tensor("w_in", [D, 2 * DI], F32, kind="ExternalInput").ap()
    b_in = nc.dram_tensor("b_in", [2 * DI], F32, kind="ExternalInput").ap()
    conv_w = nc.dram_tensor("conv_w", [KC, DI], F32, kind="ExternalInput").ap()
    conv_b = nc.dram_tensor("conv_b", [DI], F32, kind="ExternalInput").ap()
    w_x = nc.dram_tensor("w_x", [DI, RK + 2 * NS], F32, kind="ExternalInput").ap()
    w_dt = nc.dram_tensor("w_dt", [RK, DI], F32, kind="ExternalInput").ap()
    b_dt = nc.dram_tensor("b_dt", [DI], F32, kind="ExternalInput").ap()
    a_log = nc.dram_tensor("a_log", [DI, NS], F32, kind="ExternalInput").ap()
    d_vec = nc.dram_tensor("d_vec", [DI], F32, kind="ExternalInput").ap()
    w_out = nc.dram_tensor("w_out", [DI, D], F32, kind="ExternalInput").ap()
    b_out = nc.dram_tensor("b_out", [D], F32, kind="ExternalInput").ap()
    w_gate = nc.dram_tensor("w_gate", [HI, D], F32, kind="ExternalInput").ap()
    w_up = nc.dram_tensor("w_up", [HI, D], F32, kind="ExternalInput").ap()
    w_down = nc.dram_tensor("w_down", [D, HI], F32, kind="ExternalInput").ap()
    outT = nc.dram_tensor("out", [L, D], F32, kind="ExternalOutput").ap()

    wkind = "ExternalOutput" if taps else "Internal"
    win_h = nc.dram_tensor("win_h", [D, 2 * DI], BF16).ap()
    wout_h = nc.dram_tensor("wout_h", [DI, D], BF16).ap()
    wupT_h = nc.dram_tensor("wupT_h", [D, HI], BF16, kind=wkind).ap()
    wgtT_h = nc.dram_tensor("wgtT_h", [D, HI], BF16, kind=wkind).ap()
    wdnT_h = nc.dram_tensor("wdnT_h", [HI, D], BF16, kind=wkind).ap()
    sres_h = nc.dram_tensor("sres_h", [DI, L], BF16).ap()
    spkind = "ExternalOutput" if taps else "Internal"
    sp_up = nc.dram_tensor("sp_up", [HI, L], F32, kind=spkind).ap()
    sp_gate = nc.dram_tensor("sp_gate", [HI, L], F32, kind=spkind).ap()
    sp_g2 = nc.dram_tensor("sp_g2", [HI, L], F32, kind=spkind).ap()
    sp_o = nc.dram_tensor("sp_o", [HI, L], F32, kind=spkind).ap()
    x1_h = nc.dram_tensor("x1_h", [L, D], F32).ap()
    bc_h = nc.dram_tensor("bc_h", [2 * NS, L], BF16).ap()
    sc_h = nc.dram_tensor("sc_h", [4, 32], F32,
                          kind="ExternalOutput" if taps else "Internal").ap()
    r2s_h = nc.dram_tensor("r2s_h", [16, 128], F32,
                           kind="ExternalOutput" if taps else "Internal").ap()
    cc_bufs = [(nc.dram_tensor("cci%d" % i, [1, 8], F32).ap(),
                nc.dram_tensor("cco%d" % i, [1, 8], F32).ap())
               for i in range(4)]
    scb_h = (nc.dram_tensor("scb_h", [128, 12], F32, kind="ExternalOutput").ap()
             if taps else None)
    xq_h = (nc.dram_tensor("xq_h", [128, NFC * L], BF16,
                           kind="ExternalOutput").ap() if taps else None)
    tap = {}
    if taps:
        for nm, shp in (("tap_u", [DI, L]), ("tap_delta", [DI, L]),
                        ("tap_ymod", [DI, L]), ("tap_x1", [L, D])):
            tap[nm] = nc.dram_tensor(nm, shp, F32, kind="ExternalOutput").ap()

    stack = ExitStack()
    with stack:
        c = Ctx(nc, stack)

        def sb(name, shape, dt=F32):
            return stack.enter_context(nc.sbuf_tensor(name, shape, dt))

        pt = [stack.enter_context(nc.psum_tensor("ps%d" % i, [128, 512], F32))
              for i in range(8)]

        A_sb = sb("A_sb", [128, NKC * NS])
        cw_sb = sb("cw_sb", [128, KC * NKC])
        cb_sb = sb("cb_sb", [128, NKC])
        dv_sb = sb("dv_sb", [128, NKC])
        bdt_sb = sb("bdt_sb", [128, NKC])
        bin_sb = sb("bin_sb", [128, 2 * NKC])
        gam_sb = sb("gam_sb", [128, NFC])
        bet_sb = sb("bet_sb", [128, NFC])
        bout_b = sb("bout_b", [128, D], BF16)
        I_bf = sb("I_bf", [128, 128], BF16)
        I_f32 = sb("I_f32", [128, 128])
        state = sb("state", [128, NKC * NS])
        scal = sb("scal", [1, 32])
        scb = sb("scb", [128, 12])
        sca = sb("sca", [128, 8])
        scg = sb("scg", [128, 8])
        zero8 = sb("zero8", [1, 8])
        wx_bf = sb("wx_bf", [128, NKC * 128], BF16)
        wdt_bf = sb("wdt_bf", [64, DI], BF16)
        stat = sb("stat", [128, 64])
        sqd = sb("sqd", [128, D])
        mxu = sb("mxu", [128, max(16, NIC * NT2)])
        mxg = sb("mxg", [128, NIC * NT2])
        mx4 = sb("mx4", [128, NIC * NT2])
        mx5 = sb("mx5", [128, NIC * NT2])

        c.dma(A_sb[:, :].rearrange("p (k n) -> p k n", n=NS),
              a_log.rearrange("(k p) n -> p k n", p=128))
        c.dma(cw_sb[:, :].rearrange("p (i k) -> p i k", i=KC),
              conv_w.rearrange("i (k p) -> p i k", p=128), slow=True)
        c.dma(cb_sb[:, :], conv_b.rearrange("(k p) -> p k", p=128), slow=True)
        c.dma(dv_sb[:, :], d_vec.rearrange("(k p) -> p k", p=128), slow=True)
        c.dma(bdt_sb[:, :], b_dt.rearrange("(k p) -> p k", p=128), slow=True)
        c.dma(bin_sb[:, :], b_in.rearrange("(k p) -> p k", p=128), slow=True)
        c.dma(gam_sb[:, :], g_ln.rearrange("(k p) -> p k", p=128), slow=True)
        c.dma(bet_sb[:, :], b_ln.rearrange("(k p) -> p k", p=128), slow=True)
        c.dma(bout_b[:, :],
              b_out.rearrange("(o f) -> o f", o=1).broadcast_to((128, D)),
              eng="gpsimd")
        c.g(lambda e: e.memset(wx_bf[:, :], 0.0))
        c.wait("gpsimd", "sg")
        c.dma(wx_bf[:, :].rearrange("p (k j) -> p k j", j=128)[:, :, 0:RK],
              w_x.rearrange("(k p) j -> p k j", p=128)[:, :, 0:RK],
              eng="gpsimd")
        c.dma(wx_bf[:, :].rearrange("p (k j) -> p k j", j=128)[:, :, 64:64 + NS],
              w_x.rearrange("(k p) j -> p k j", p=128)[:, :, RK:RK + NS],
              eng="gpsimd")
        c.dma(wx_bf[:, :].rearrange("p (k j) -> p k j", j=128)[:, :, 96:96 + NS],
              w_x.rearrange("(k p) j -> p k j", p=128)[:, :, RK + NS:RK + 2 * NS],
              eng="gpsimd")
        c.dma(wdt_bf[0:64, :], w_dt[:, :], eng="gpsimd")
        c.wait("scalar", "dh")
        c.s(lambda e: e.activation(A_sb[:, :], A_sb[:, :], ACT.Exp))
        c.wait("vector", "ss")
        c.v(lambda e: e.tensor_scalar_mul(bdt_sb[:, :], bdt_sb[:, :], -1.0))
        c.g(lambda e: e.iota(I_f32[:, :], [[1, 128]], base=0,
                             channel_multiplier=0,
                             allow_small_or_imprecise_dtypes=True))
        c.g(lambda e: e.iota(scb[:, 9:10], [[0, 1]], base=0,
                             channel_multiplier=1,
                             allow_small_or_imprecise_dtypes=True))
        mg = c.mark("sg")
        c.wait_marks("vector", mg)
        c.v(lambda e: e.tensor_tensor(I_f32[:, :], I_f32[:, :],
                                      scb[:, 9:10].broadcast_to((128, 128)),
                                      ALU.is_equal))
        c.selfw("vector")
        c.v(lambda e: e.tensor_copy(I_bf[:, :], I_f32[:, :]))
        c.g(lambda e: e.memset(state[:, :], 0.0))
        c.g(lambda e: e.memset(zero8[:, :], 0.0))
        c.g(lambda e: e.memset(scal[:, :], 0.0))
        c.g(lambda e: e.memset(scb[:, 10:12], 0.0))
        c.g(lambda e: e.memset(sca[:, :], 0.0))
        c.g(lambda e: e.memset(scg[:, :], 0.0))
        c.barrier()

        r2s_idx = [0]

        def reduce_to_all(src_ap, dst_col):
            idx = r2s_idx[0] % 16
            r2s_idx[0] += 1
            c.selfw("vector")
            c.use("vector", "stat63")
            c.v(lambda e: e.tensor_reduce(stat[:, 63:64], src_ap, AX.X,
                                          ALU.max, apply_absolute_value=True))
            mv = c.mark("sv")
            c.wait_marks("sync", mv)
            c.dma(r2s_h[idx, :].rearrange("(p o) -> p o", o=1),
                  stat[:, 63:64])
            c.setg("stat63", "dh")
            c.lanewait("sync", "dh")
            c.dma(sqd[:, 0:128],
                  r2s_h[idx, :].rearrange("(o f) -> o f", o=1)
                  .broadcast_to((128, 128)))
            c.lanewait("vector", "dh")
            c.v(lambda e, d=dst_col: e.tensor_reduce(
                sca[:, d:d + 1], sqd[:, 0:128], AX.X, ALU.max,
                apply_absolute_value=True))
            c.selfw("vector")

        def collective(idx, n_in):
            cci, cco = cc_bufs[idx]
            if DBG["no_collective"]:
                c.v(lambda e, n=n_in: e.tensor_copy(scg[:, 0:n],
                                                    sca[:, 0:n]))
                c.selfw("vector")
                return
            mv = c.mark("sv")
            c.wait_marks("sync", mv)
            c.dma(cci[0:1, 0:8], zero8[0:1, 0:8])
            c.lanewait("sync", "dh")
            c.dma(cci[0:1, 0:n_in], sca[0:1, 0:n_in])
            c.lanewait("gpsimd", "dh")
            c.run("gpsimd", lambda e: e.collective_compute(
                "AllReduce", ALU.max,
                replica_groups=[list(range(n_cores))],
                ins=[cci.opt()], outs=[cco.opt()]), "cc")
            c.wait("sync", "cc")
            c.dma(scg[:, 0:n_in],
                  cco[0:1, 0:n_in].broadcast_to((128, n_in)))
            c.lanewait("vector", "dh")

        def ln_rstd(n):
            c.selfw("scalar")
            c.s(lambda e: e.nop())
            c.wait_marks("vector", c.mark("ss"))
            c.selfw("vector")
            c.v(lambda e: e.tensor_scalar_mul(stat[:, 16:16 + n],
                                              stat[:, 0:n], 1.0 / D))
            c.v(lambda e: e.tensor_scalar_mul(stat[:, 32:32 + n],
                                              stat[:, 8:8 + n], 1.0 / D))
            c.selfw("vector")
            c.v(lambda e: e.tensor_tensor(stat[:, 40:40 + n],
                                          stat[:, 16:16 + n],
                                          stat[:, 16:16 + n], ALU.mult))
            c.selfw("vector")
            c.v(lambda e: e.tensor_tensor(stat[:, 40:40 + n],
                                          stat[:, 32:32 + n],
                                          stat[:, 40:40 + n], ALU.subtract))
            c.selfw("vector")
            c.v(lambda e: e.tensor_scalar_add(stat[:, 40:40 + n],
                                              stat[:, 40:40 + n], EPS))
            mv = c.mark("sv")
            c.wait_marks("scalar", mv)
            c.s(lambda e: e.activation(stat[:, 48:48 + n],
                                       stat[:, 40:40 + n], ACT.Sqrt))
            c.selfw("scalar")
            c.s(lambda e: e.nop())
            ms = c.mark("ss")
            c.wait_marks("vector", ms)
            c.v(lambda e: e.reciprocal(stat[:, 24:24 + n], stat[:, 48:48 + n]))
            c.selfw("vector")
            c.v(lambda e: e.tensor_tensor(stat[:, 56:56 + n],
                                          stat[:, 40:40 + n],
                                          stat[:, 24:24 + n], ALU.mult))
            c.selfw("vector")
            c.v(lambda e: e.tensor_tensor(stat[:, 56:56 + n],
                                          stat[:, 56:56 + n],
                                          stat[:, 24:24 + n], ALU.mult))
            c.selfw("vector")
            c.v(lambda e: e.tensor_scalar(stat[:, 56:56 + n],
                                          stat[:, 56:56 + n], -0.5, 1.5,
                                          ALU.mult, ALU.add))
            c.selfw("vector")
            c.v(lambda e: e.tensor_tensor(stat[:, 24:24 + n],
                                          stat[:, 24:24 + n],
                                          stat[:, 56:56 + n], ALU.mult))
            c.selfw("vector")

        class _Cut(Exception):
            pass

        try:
            p0scope = ExitStack()
            with p0scope:
                _e = p0scope.enter_context
                raw = _e(nc.sbuf_tensor("raw", [128, 4 * 4096], F32))
                dqs = _e(nc.sbuf_tensor("dqs", [128, 4 * 128], F32))
                dqr = _e(nc.sbuf_tensor("dqr", [128, 4 * 128], F32))
                trb = _e(nc.sbuf_tensor("trb", [128, 8 * 128], BF16))
                cstb = _e(nc.sbuf_tensor("cstb", [128, 2 * 4096], BF16))
                p0lanes = ("dw0", "dw1", "dx0", "dx1",
                           "db0", "db1", "ds0", "ds1")
                casts = [("c", (w_in, win_h, rc, 2 * DI)) for rc in range(NFC)] + \
                        [("c", (w_out, wout_h, rc, D)) for rc in range(NKC)]
                groups = [("g", (w_up, wupT_h, 4 * g, 4, D)) for g in range(6)] + \
                         [("g", (w_gate, wgtT_h, 4 * g, 4, D)) for g in range(6)] + \
                         [("g", (w_down, wdnT_h, oc, 1, HI)) for oc in range(8)]
                seq = []
                while casts or groups:
                    if casts:
                        seq.append(casts.pop(0))
                    if groups:
                        seq.append(groups.pop(0))
                tri = 0
                for idx, (kind, item) in enumerate(seq):
                    s4 = idx % 4
                    ldl = p0lanes[idx % 8]
                    if kind == "c":
                        src, dst, rc, NCOL = item
                        sl = idx % 2
                        c.use("sync", ("p0raw", s4))
                        c.dma(raw[:, s4 * 4096: s4 * 4096 + NCOL],
                              src[rc * 128:(rc + 1) * 128, :], lane=ldl)
                        c.lanewait("vector", ldl)
                        c.use("vector", ("p0cst", sl))
                        c.v(lambda e, s4=s4, sl=sl, NCOL=NCOL: e.tensor_copy(
                            cstb[:, sl * 4096: sl * 4096 + NCOL],
                            raw[:, s4 * 4096: s4 * 4096 + NCOL]))
                        c.setg(("p0raw", s4), "sv")
                        mv = c.mark("sv")
                        c.wait_marks("sync", mv)
                        ln = "dp%d" % sl
                        c.dma(dst[rc * 128:(rc + 1) * 128, :],
                              cstb[:, sl * 4096: sl * 4096 + NCOL], lane=ln)
                        c.guards[("p0cst", sl)] = {ln: c.cnt[ln]}
                        continue
                    wsrc, wdst, oc0, GRP, I = item
                    gw = GRP * I
                    NB = gw // 32
                    rawt = raw[:, s4 * 4096: s4 * 4096 + gw]
                    sct = dqs[:, s4 * 128: s4 * 128 + NB]
                    rct = dqr[:, s4 * 128: s4 * 128 + NB]
                    c.use("sync", ("p0raw", s4))
                    if GRP == 1:
                        c.dma(rawt, wsrc[oc0 * 128:(oc0 + 1) * 128, :],
                              lane=ldl)
                    else:
                        c.dma(rawt.rearrange("p (o i) -> p o i", o=GRP),
                              wsrc[oc0 * 128:(oc0 + GRP) * 128, :]
                              .rearrange("(o p) i -> p o i", p=128),
                              lane=ldl)
                    c.lanewait("vector", ldl)
                    c.use("vector", ("p0dq", s4))
                    c.v(lambda e, r=rawt, s=sct: e.tensor_reduce(
                        s, r.rearrange("p (b k) -> p b k", k=32), AX.X,
                        ALU.max, apply_absolute_value=True))
                    c.selfw("vector")
                    c.v(lambda e, s=sct: e.tensor_scalar(
                        s, s, 1.0 / 7.0, 1e-12, ALU.mult, ALU.max))
                    c.selfw("vector")
                    c.v(lambda e, s=sct, r=rct: e.reciprocal(r, s))
                    c.selfw("vector")
                    c.v(lambda e, r=rawt, rc_=rct, NB=NB: e.tensor_tensor(
                        r.rearrange("p (b k) -> p b k", k=32),
                        r.rearrange("p (b k) -> p b k", k=32),
                        rc_.rearrange("p (b o) -> p b o", o=1)
                        .broadcast_to((128, NB, 32)), ALU.mult))
                    c.v(lambda e, r=rawt: e.tensor_scalar(
                        r, r, MAGIC, MAGIC, ALU.add, ALU.subtract))
                    c.v(lambda e, r=rawt, s=sct, NB=NB: e.tensor_tensor(
                        r.rearrange("p (b k) -> p b k", k=32),
                        r.rearrange("p (b k) -> p b k", k=32),
                        s.rearrange("p (b o) -> p b o", o=1)
                        .broadcast_to((128, NB, 32)), ALU.mult))
                    mwq = c.mark("sv")
                    c.setg(("p0dq", s4), "sv")
                    NBT = I // 128
                    for b in range(gw // 128):
                        ocl, bb = divmod(b, NBT)
                        psl = tri % 8
                        tsl = tri % 8
                        c.wait_marks("tensor", mwq)
                        c.use("tensor", ("ps", psl))
                        c.t(lambda e, b=b, w=rawt, psl=psl: e.transpose(
                            pt[psl][:, 0:128], w[:, b * 128:(b + 1) * 128],
                            I_f32[:, :]))
                        mt = c.mark("stn")
                        c.wait_marks("scalar", mt)
                        c.use("scalar", ("p0trb", tsl))
                        c.s(lambda e, psl=psl, tsl=tsl: e.activation(
                            trb[:, tsl * 128:(tsl + 1) * 128],
                            pt[psl][:, 0:128], ACT.Copy))
                        c.setg(("ps", psl), "ss")
                        ms = c.mark("ss")
                        c.wait_marks("sync", ms)
                        ln = ("dt0", "dt1", "dst", "dh")[tsl % 4]
                        c.dma(wdst[bb * 128:(bb + 1) * 128,
                                   (oc0 + ocl) * 128:(oc0 + ocl + 1) * 128],
                              trb[:, tsl * 128:(tsl + 1) * 128], lane=ln)
                        c.guards[("p0trb", tsl)] = {ln: c.cnt[ln]}
                        tri += 1
                    c.setg(("p0raw", s4), "stn")
            c.barrier()
            if cut == 2:
                raise _Cut()

            mscope = ExitStack()
            with mscope:
                _e = mscope.enter_context
                xbuf = _e(nc.sbuf_tensor("xbuf", [128, LNB * D], F32))
                xnp = _e(nc.sbuf_tensor("xnp", [128, D], F32))
                xn_fm = _e(nc.sbuf_tensor("xn_fm", [128, NFC * TH], BF16))
                u0 = _e(nc.sbuf_tensor("u0", [128, NKC * (TH + 3)], BF16))
                delta = _e(nc.sbuf_tensor("delta", [128, NKC * TH], BF16))
                dt_sb = _e(nc.sbuf_tensor("dt_sb", [64, TH], BF16))
                bcB_sb = _e(nc.sbuf_tensor("bcB_sb", [16, TH], BF16))
                bcC_sb = _e(nc.sbuf_tensor("bcC_sb", [16, TH], BF16))
                wint = _e(nc.sbuf_tensor("wint", [128, 2 * 1024], BF16))
                srstg = _e(nc.sbuf_tensor("srstg", [128, 2 * TTL], BF16))
                acc2 = _e(nc.sbuf_tensor("acc2", [128, TH], BF16))
                tails = _e(nc.sbuf_tensor("tails", [128, NKC * 3], BF16))
                gkt = _e(nc.sbuf_tensor("gkt", [128, 4 * TH], BF16))
                srkg = _e(nc.sbuf_tensor("srkg", [128, 2 * TH], BF16))
                Bb = _e(nc.sbuf_tensor("Bb", [128, 3 * TH], BF16))
                Cb = _e(nc.sbuf_tensor("Cb", [128, 3 * TH], BF16))
                ab = _e(nc.sbuf_tensor("ab", [128, 3 * TH], F32))
                bb = _e(nc.sbuf_tensor("bb", [128, 3 * TH], BF16))
                hb = _e(nc.sbuf_tensor("hb", [128, 2 * TH], BF16))
                wout_bf = _e(nc.sbuf_tensor("wout_bf", [128, NKC * D], BF16))
                xb2 = _e(nc.sbuf_tensor("xb2", [128, D], F32))

                def u0k(kc):
                    return u0[:, kc * (TH + 3): kc * (TH + 3) + TH + 3]

                for h in range(2):
                    t0 = h * TH
                    for tcg in range(NTC // LNB):
                        for tc2 in range(LNB):
                            tc = tcg * LNB + tc2
                            c.use("sync", ("xbuf", tc2))
                            c.dma(xbuf[:, tc2 * D:(tc2 + 1) * D],
                                  xT[t0 + tc * 128: t0 + (tc + 1) * 128, :],
                                  lane="dx%d" % tc2)
                            c.lanewait("vector", "dx%d" % tc2)
                            c.v(lambda e, tc2=tc2: e.tensor_reduce(
                                stat[:, tc2:tc2 + 1],
                                xbuf[:, tc2 * D:(tc2 + 1) * D], AX.X, ALU.add))
                            c.lanewait("scalar", "dx%d" % tc2)
                            c.s(lambda e, tc2=tc2: e.activation(
                                sqd[:, :], xbuf[:, tc2 * D:(tc2 + 1) * D],
                                ACT.Square, accum_out=stat[:, 8 + tc2: 9 + tc2]))
                        ms = c.mark("ss")
                        c.wait_marks("vector", ms)
                        ln_rstd(LNB)
                        for tc2 in range(LNB):
                            tc = tcg * LNB + tc2
                            c.use("vector", ("xnp", 0))
                            c.v(lambda e, tc2=tc2: e.tensor_scalar(
                                xnp[:, :],
                                xbuf[:, tc2 * D:(tc2 + 1) * D],
                                stat[:, 16 + tc2:17 + tc2],
                                stat[:, 24 + tc2:25 + tc2],
                                ALU.subtract, ALU.mult))
                            c.setg(("xbuf", tc2), "sv")
                            mv = c.mark("sv")
                            for fc in range(NFC):
                                psl = 6 + fc % 2
                                c.wait_marks("tensor", mv)
                                c.use("tensor", ("ps", psl))
                                c.t(lambda e, fc=fc, psl=psl: e.transpose(
                                    pt[psl][:, 0:128],
                                    xnp[:, fc * 128:(fc + 1) * 128],
                                    I_f32[:, :]))
                                mt = c.mark("stn")
                                c.wait_marks("vector", mt)
                                c.v(lambda e, fc=fc, tc=tc, psl=psl: e.tensor_scalar(
                                    xn_fm[:, fc * TH + tc * 128:
                                          fc * TH + (tc + 1) * 128],
                                    pt[psl][:, 0:128], gam_sb[:, fc:fc + 1],
                                    bet_sb[:, fc:fc + 1], ALU.mult, ALU.add))
                                c.setg(("ps", psl), "sv")
                            c.setg(("xnp", 0), "stn")
                    mxn = c.mark("sv")
                    for jc in range(2 * NKC):
                        wsl = jc % 2
                        c.use("sync", ("wint", wsl))
                        c.dma(wint[:, wsl * 1024:(wsl + 1) * 1024]
                              .rearrange("p (k j) -> p k j", k=NFC),
                              win_h[:, jc * 128:(jc + 1) * 128]
                              .rearrange("(k p) j -> p k j", p=128),
                              lane="dw%d" % wsl)
                        for tt in range(NTT):
                            psl = (jc * NTT + tt) % 4
                            c.use("tensor", ("ps", psl))
                            c.lanewait("tensor", "dw%d" % wsl)
                            c.wait_marks("tensor", mxn)
                            for kc in range(NFC):
                                c.t(lambda e, kc=kc, tt=tt, wsl=wsl, psl=psl:
                                    e.matmul(pt[psl][:, 0:TTL],
                                             wint[:, wsl * 1024 + kc * 128:
                                                  wsl * 1024 + (kc + 1) * 128],
                                             xn_fm[:, kc * TH + tt * TTL:
                                                   kc * TH + (tt + 1) * TTL],
                                             start=(kc == 0),
                                             stop=(kc == NFC - 1)))
                            mt = c.mark("stn")
                            c.wait_marks("scalar", mt)
                            if jc < NKC:
                                c.s(lambda e, jc=jc, tt=tt, psl=psl: e.activation(
                                    u0[:, jc * (TH + 3) + 3 + tt * TTL:
                                       jc * (TH + 3) + 3 + (tt + 1) * TTL],
                                    pt[psl][:, 0:TTL], ACT.Identity,
                                    bias=bin_sb[:, jc:jc + 1]))
                                c.setg(("ps", psl), "ss")
                            else:
                                ssl = (jc * NTT + tt) % 2
                                c.use("scalar", ("srstg", ssl))
                                c.s(lambda e, jc=jc, psl=psl, ssl=ssl: e.activation(
                                    srstg[:, ssl * TTL:(ssl + 1) * TTL],
                                    pt[psl][:, 0:TTL], ACT.Silu,
                                    bias=bin_sb[:, jc:jc + 1]))
                                c.setg(("ps", psl), "ss")
                                msl = c.mark("ss")
                                c.wait_marks("sync", msl)
                                lnr = "dr%d" % ssl
                                c.dma(sres_h[(jc - NKC) * 128:
                                             (jc - NKC + 1) * 128,
                                             t0 + tt * TTL:
                                             t0 + (tt + 1) * TTL],
                                      srstg[:, ssl * TTL:(ssl + 1) * TTL],
                                      lane=lnr)
                                c.guards[("srstg", ssl)] = {lnr: c.cnt[lnr]}
                        c.setg(("wint", wsl), "stn")
                    msu = c.mark("ss")
                    c.wait_marks("vector", {"ss": msu["ss"]})
                    if h == 0:
                        for kc in range(NKC):
                            c.v(lambda e, kc=kc: e.memset(u0k(kc)[:, 0:3], 0.0))
                    else:
                        for kc in range(NKC):
                            c.v(lambda e, kc=kc: e.tensor_copy(
                                u0k(kc)[:, 0:3], tails[:, kc * 3:kc * 3 + 3]))
                    c.selfw("vector")
                    for kc in range(NKC):
                        if h == 0:
                            c.v(lambda e, kc=kc: e.tensor_copy(
                                tails[:, kc * 3:kc * 3 + 3],
                                u0k(kc)[:, TH:TH + 3]))
                        c.v(lambda e, kc=kc: e.tensor_scalar(
                            acc2[:, 0:TH], u0k(kc)[:, 0:TH],
                            cw_sb[:, 0 * NKC + kc:0 * NKC + kc + 1],
                            cb_sb[:, kc:kc + 1], ALU.mult, ALU.add))
                        c.v(lambda e, kc=kc: e.scalar_tensor_tensor(
                            acc2[:, 0:TH], u0k(kc)[:, 1:TH + 1],
                            cw_sb[:, 1 * NKC + kc:1 * NKC + kc + 1],
                            acc2[:, 0:TH], ALU.mult, ALU.add))
                        c.v(lambda e, kc=kc: e.scalar_tensor_tensor(
                            acc2[:, 0:TH], u0k(kc)[:, 2:TH + 2],
                            cw_sb[:, 2 * NKC + kc:2 * NKC + kc + 1],
                            acc2[:, 0:TH], ALU.mult, ALU.add))
                        c.v(lambda e, kc=kc: e.scalar_tensor_tensor(
                            u0k(kc)[:, 3:TH + 3], u0k(kc)[:, 3:TH + 3],
                            cw_sb[:, 3 * NKC + kc:3 * NKC + kc + 1],
                            acc2[:, 0:TH], ALU.mult, ALU.add))
                        mv = c.mark("sv")
                        c.wait_marks("scalar", mv)
                        c.s(lambda e, kc=kc: e.activation(
                            u0k(kc)[:, 3:TH + 3], u0k(kc)[:, 3:TH + 3], ACT.Silu))
                    if taps:
                        c.wait_marks("gpsimd", c.mark("ss"))
                        for kc in range(NKC):
                            c.dma(tap["tap_u"][kc * 128:(kc + 1) * 128,
                                               t0:t0 + TH], u0k(kc)[:, 3:TH + 3],
                                  eng="gpsimd")
                    msu = c.mark("ss")
                    c.wait_marks("tensor", msu)
                    for tt in range(NTT):
                        psl = 4 + tt % 2
                        c.use("tensor", ("ps", psl))
                        for kc in range(NKC):
                            c.t(lambda e, kc=kc, tt=tt, psl=psl: e.matmul(
                                pt[psl][:, 0:TTL],
                                wx_bf[:, kc * 128:(kc + 1) * 128],
                                u0k(kc)[:, 3 + tt * TTL: 3 + (tt + 1) * TTL],
                                start=(kc == 0), stop=(kc == NKC - 1)))
                        mt = c.mark("stn")
                        c.wait_marks("vector", mt)
                        c.v(lambda e, tt=tt, psl=psl: e.tensor_copy(
                            dt_sb[0:64, tt * TTL:(tt + 1) * TTL],
                            pt[psl][0:64, 0:TTL]))
                        c.v(lambda e, tt=tt, psl=psl: e.tensor_scalar_mul(
                            bcB_sb[0:NS, tt * TTL:(tt + 1) * TTL],
                            pt[psl][64:64 + NS, 0:TTL], -1.0))
                        c.v(lambda e, tt=tt, psl=psl: e.tensor_copy(
                            bcC_sb[0:NS, tt * TTL:(tt + 1) * TTL],
                            pt[psl][96:96 + NS, 0:TTL]))
                        c.setg(("ps", psl), "sv")
                    c.wait_marks("sync", c.mark("sv"))
                    c.dma(bc_h[0:NS, t0:t0 + TH], bcB_sb[0:NS, :])
                    c.dma(bc_h[NS:2 * NS, t0:t0 + TH], bcC_sb[0:NS, :])
                    mdt = c.mark("sv")
                    c.wait_marks("tensor", mdt)
                    for jc in range(NKC):
                        for tt in range(NTT):
                            psl = (jc * NTT + tt) % 4
                            c.use("tensor", ("ps", psl))
                            c.t(lambda e, jc=jc, tt=tt, psl=psl: e.matmul(
                                pt[psl][:, 0:TTL],
                                wdt_bf[0:64, jc * 128:(jc + 1) * 128],
                                dt_sb[0:64, tt * TTL:(tt + 1) * TTL],
                                start=True, stop=True))
                            mt = c.mark("stn")
                            c.wait_marks("scalar", mt)
                            c.s(lambda e, jc=jc, tt=tt, psl=psl: e.activation(
                                delta[:, jc * TH + tt * TTL:
                                      jc * TH + (tt + 1) * TTL],
                                pt[psl][:, 0:TTL], ACT.Sigmoid,
                                bias=bdt_sb[:, jc:jc + 1], scale=-1.0))
                            c.setg(("ps", psl), "ss")
                    for jc in range(NKC):
                        c.s(lambda e, jc=jc: e.activation(
                            delta[:, jc * TH:(jc + 1) * TH],
                            delta[:, jc * TH:(jc + 1) * TH], ACT.Ln))
                    c.selfw("scalar")
                    c.s(lambda e: e.nop())
                    if taps:
                        c.wait_marks("gpsimd", c.mark("ss"))
                        for kc in range(NKC):
                            c.dma(tap["tap_delta"][kc * 128:(kc + 1) * 128,
                                                   t0:t0 + TH],
                                  delta[:, kc * TH:(kc + 1) * TH], eng="gpsimd")
                    if cut == 3:
                        mscope.close()
                        raise _Cut()
                    for kc in range(NKC):
                        c.use("sync", ("woutld", kc))
                        c.dma(wout_bf[:, kc * D:(kc + 1) * D],
                              wout_h[kc * 128:(kc + 1) * 128, :])
                    c.lanewait("sync", "dh", "dr0", "dr1")
                    mdl = c.mark("ss")
                    c.wait_marks("vector", mdl)
                    ibb = [0]
                    for kg in range(NKC // 4):
                        ks = [kg * 4 + j for j in range(4)]
                        for j, k in enumerate(ks[:2]):
                            c.use("sync", ("srkg", j % 2))
                            c.dma(srkg[:, (j % 2) * TH:(j % 2 + 1) * TH],
                                  sres_h[k * 128:(k + 1) * 128, t0:t0 + TH])
                        for j, k in enumerate(ks):
                            c.use("vector", ("ypsum", j))
                            c.v(lambda e, j=j, k=k: e.tensor_tensor(
                                gkt[:, j * TH:(j + 1) * TH],
                                delta[:, k * TH:(k + 1) * TH],
                                u0k(k)[:, 3:TH + 3], ALU.mult))
                            b0 = ibb[0] % 3
                            ibb[0] += 1
                            c.use("vector", ("bb", b0))
                            c.v(lambda e, k=k, b0=b0: e.tensor_scalar_mul(
                                bb[:, b0 * TH:(b0 + 1) * TH],
                                u0k(k)[:, 3:TH + 3], dv_sb[:, k:k + 1]))
                            mv = c.mark("sv")
                            c.wait_marks("tensor", mv)
                            for tb in range(YB):
                                c.t(lambda e, j=j, tb=tb, b0=b0: e.matmul(
                                    pt[2 * j + tb][:, 0:min(512, TH)], I_bf[:, :],
                                    bb[:, b0 * TH + tb * 512:
                                       b0 * TH + tb * 512 + min(512, TH)],
                                    start=True, stop=False))
                            c.setg(("bb", b0), "stn")
                        for n in range(NS):
                            bsl = n % 3
                            bln = ("db0", "db1", "ds0")[bsl]
                            c.use("sync", ("bcb", bsl))
                            c.dma(Bb[:, bsl * TH:(bsl + 1) * TH],
                                  bc_h[n:n + 1, t0:t0 + TH]
                                  .broadcast_to((128, TH)), lane=bln)
                            c.dma(Cb[:, bsl * TH:(bsl + 1) * TH],
                                  bc_h[NS + n:NS + n + 1, t0:t0 + TH]
                                  .broadcast_to((128, TH)), lane=bln)
                            for j, k in enumerate(ks):
                                asl = (n * 4 + j) % 3
                                hsl = (n * 4 + j) % 2
                                kn = k * NS + n
                                c.use("scalar", ("ab", asl))
                                c.s(lambda e, k=k, kn=kn, asl=asl: e.activation(
                                    ab[:, asl * TH:(asl + 1) * TH],
                                    delta[:, k * TH:(k + 1) * TH], ACT.Exp,
                                    scale=A_sb[:, kn:kn + 1]))
                                msa = c.mark("ss")
                                b1 = ibb[0] % 3
                                ibb[0] += 1
                                c.use("vector", ("bb", b1))
                                c.lanewait("vector", bln)
                                c.v(lambda e, j=j, bsl=bsl, b1=b1:
                                    e.tensor_tensor(
                                        bb[:, b1 * TH:(b1 + 1) * TH],
                                        gkt[:, j * TH:(j + 1) * TH],
                                        Bb[:, bsl * TH:(bsl + 1) * TH],
                                        ALU.mult))
                                c.wait_marks("vector", msa)
                                c.use("vector", ("hb", hsl))
                                c.v(lambda e, kn=kn, asl=asl, hsl=hsl, b1=b1:
                                    e.tensor_tensor_scan(
                                        hb[:, hsl * TH:(hsl + 1) * TH],
                                        ab[:, asl * TH:(asl + 1) * TH],
                                        bb[:, b1 * TH:(b1 + 1) * TH],
                                        state[:, kn:kn + 1], ALU.mult,
                                        ALU.add))
                                c.setg(("ab", asl), "sv")
                                c.setg(("bb", b1), "sv")
                                b2 = ibb[0] % 3
                                ibb[0] += 1
                                c.use("vector", ("bb", b2))
                                c.v(lambda e, hsl=hsl, bsl=bsl, b2=b2:
                                    e.tensor_tensor(
                                        bb[:, b2 * TH:(b2 + 1) * TH],
                                        hb[:, hsl * TH:(hsl + 1) * TH],
                                        Cb[:, bsl * TH:(bsl + 1) * TH],
                                        ALU.mult))
                                if h == 0:
                                    mv = c.mark("sv")
                                    c.wait_marks("gpsimd", mv)
                                    c.g(lambda e, kn=kn, hsl=hsl: e.tensor_copy(
                                        state[:, kn:kn + 1],
                                        hb[:, hsl * TH + TH - 1: hsl * TH + TH]))
                                    c.setg(("hb", hsl), "sv", "sg")
                                else:
                                    c.setg(("hb", hsl), "sv")
                                mv = c.mark("sv")
                                c.wait_marks("tensor", mv)
                                for tb in range(YB):
                                    c.t(lambda e, j=j, tb=tb, n=n, b2=b2:
                                        e.matmul(
                                            pt[2 * j + tb][:, 0:min(512, TH)],
                                            I_bf[:, :],
                                            bb[:, b2 * TH + tb * 512:
                                               b2 * TH + tb * 512 +
                                               min(512, TH)],
                                            start=False, stop=(n == NS - 1)))
                                c.setg(("bb", b2), "stn")
                            c.setg(("bcb", bsl), "sv")
                        mt = c.mark("stn")
                        c.wait_marks("vector", mt)
                        for j, k in enumerate(ks):
                            c.wait("vector", "dh")
                            for tb in range(YB):
                                w = min(512, TH)
                                c.v(lambda e, j=j, k=k, tb=tb, w=w: e.tensor_tensor(
                                    delta[:, k * TH + tb * 512:
                                          k * TH + tb * 512 + w],
                                    pt[2 * j + tb][:, 0:w],
                                    srkg[:, (j % 2) * TH + tb * 512:
                                         (j % 2) * TH + tb * 512 + w], ALU.mult))
                            c.setg(("ypsum", j), "sv")
                            c.setg(("srkg", j % 2), "sv")
                            if j + 2 < 4:
                                c.use("sync", ("srkg", j % 2))
                                c.dma(srkg[:, (j % 2) * TH:(j % 2 + 1) * TH],
                                      sres_h[ks[j + 2] * 128:
                                             (ks[j + 2] + 1) * 128, t0:t0 + TH])
                    if taps:
                        c.wait_marks("gpsimd", c.mark("sv"))
                        for kc in range(NKC):
                            c.dma(tap["tap_ymod"][kc * 128:(kc + 1) * 128,
                                                  t0:t0 + TH],
                                  delta[:, kc * TH:(kc + 1) * TH], eng="gpsimd")
                    if cut == 4:
                        mscope.close()
                        raise _Cut()
                    mym = c.mark("sv")
                    c.wait_marks("tensor", mym)
                    c.lanewait("tensor", "dh")
                    for kc in range(NKC):
                        c.setg(("woutld", kc), "stn")
                    for tcg in range(NTC // LNB):
                        for tc2 in range(LNB):
                            tc = tcg * LNB + tc2
                            c.use("sync", ("xbuf", tc2))
                            c.dma(xbuf[:, tc2 * D:(tc2 + 1) * D],
                                  xT[t0 + tc * 128: t0 + (tc + 1) * 128, :],
                                  lane="dx%d" % tc2)
                            c.lanewait("vector", "dx%d" % tc2)
                            c.use("vector", "xb2")
                            c.v(lambda e, tc2=tc2: e.scalar_tensor_tensor(
                                xb2[:, :], xbuf[:, tc2 * D:(tc2 + 1) * D],
                                2.0, bout_b[:, :], ALU.mult, ALU.add))
                            for hc in range(2):
                                psl = (tc * 2 + hc) % 4
                                c.use("tensor", ("ps", psl))
                                for kc in range(NKC):
                                    c.t(lambda e, kc=kc, tc=tc, hc=hc, psl=psl:
                                        e.matmul(
                                            pt[psl][:, 0:512],
                                            delta[:, kc * TH + tc * 128:
                                                  kc * TH + (tc + 1) * 128],
                                            wout_bf[:, kc * D + hc * 512:
                                                    kc * D + (hc + 1) * 512],
                                            start=(kc == 0),
                                            stop=(kc == NKC - 1)))
                                mt = c.mark("stn")
                                c.wait_marks("vector", mt)
                                c.v(lambda e, tc2=tc2, hc=hc, psl=psl:
                                    e.tensor_tensor(
                                        xbuf[:, tc2 * D + hc * 512:
                                             tc2 * D + hc * 512 + 512],
                                        pt[psl][:, 0:512],
                                        xb2[:, hc * 512: hc * 512 + 512],
                                        ALU.add))
                                c.setg(("ps", psl), "sv")
                            c.setg("xb2", "sv")
                            c.selfw("vector")
                            c.v(lambda e, tc2=tc2: e.tensor_reduce(
                                stat[:, tc2:tc2 + 1],
                                xbuf[:, tc2 * D:(tc2 + 1) * D], AX.X, ALU.add))
                            mv = c.mark("sv")
                            c.wait_marks("scalar", mv)
                            c.s(lambda e, tc2=tc2: e.activation(
                                sqd[:, :], xbuf[:, tc2 * D:(tc2 + 1) * D],
                                ACT.Square, accum_out=stat[:, 8 + tc2: 9 + tc2]))
                        ms = c.mark("ss")
                        c.wait_marks("vector", ms)
                        ln_rstd(LNB)
                        for tc2 in range(LNB):
                            tc = tcg * LNB + tc2
                            c.use("vector", "x1t")
                            c.use("vector", ("xnp", 0))
                            c.v(lambda e, tc2=tc2: e.tensor_scalar(
                                xnp[:, :], xbuf[:, tc2 * D:(tc2 + 1) * D],
                                stat[:, 16 + tc2:17 + tc2],
                                stat[:, 24 + tc2:25 + tc2],
                                ALU.subtract, ALU.mult))
                            c.setg(("xbuf", tc2), "sv")
                            c.v(lambda e, tc=tc, h=h: e.tensor_reduce(
                                mxu[:, h * NTC + tc: h * NTC + tc + 1],
                                xnp[:, :], AX.X, ALU.max,
                                apply_absolute_value=True))
                            mv = c.mark("sv")
                            c.wait_marks("scalar", mv)
                            c.dma(x1_h[t0 + tc * 128: t0 + (tc + 1) * 128, :],
                                  xnp[:, :], eng="scalar", lane="dst")
                            if taps:
                                c.dma(tap["tap_x1"][t0 + tc * 128:
                                                    t0 + (tc + 1) * 128, :],
                                      xnp[:, :], eng="scalar", lane="dst")
                            c.guards["x1t"] = {"dst": c.cnt["dst"]}
                    c.barrier()

                if cut == 5:
                    mscope.close()
                    raise _Cut()
                reduce_to_all(mxu[:, 0:2 * NTC], 0)
                collective(0, 1)
                c.v(lambda e: e.tensor_scalar(scb[:, 0:1], scg[:, 0:1],
                                              1.0 / QMAX, 1e-12, ALU.mult,
                                              ALU.max))
                c.selfw("vector")
                c.v(lambda e: e.reciprocal(scb[:, 1:2], scb[:, 0:1]))
                c.selfw("vector")
            c.barrier()
            if cut == 55:
                raise _Cut()

            fscope = ExitStack()
            with fscope:
                _e = fscope.enter_context
                xq = _e(nc.sbuf_tensor("xq", [128, NFC * L], BF16))
                xr = _e(nc.sbuf_tensor("xr", [128, 2 * D], F32))
                xqp = _e(nc.sbuf_tensor("xqp", [128, 2 * D], F32))
                fwt = _e(nc.sbuf_tensor("fwt", [128, 4 * 1024], BF16))
                stg = _e(nc.sbuf_tensor("stg", [128, 8 * 512], F32))

                for tc in range(L // 128):
                    xsl = tc % 2
                    c.use("sync", ("xr", xsl))
                    c.dma(xr[:, xsl * D:(xsl + 1) * D],
                          x1_h[tc * 128:(tc + 1) * 128, :],
                          lane="dx%d" % xsl)
                    c.lanewait("vector", "dx%d" % xsl)
                    c.use("vector", ("xqp", xsl))
                    c.v(lambda e, xsl=xsl: e.tensor_scalar(
                        xr[:, xsl * D:(xsl + 1) * D],
                        xr[:, xsl * D:(xsl + 1) * D],
                        scb[:, 1:2], MAGIC, ALU.mult, ALU.add))
                    c.selfw("vector")
                    c.v(lambda e, xsl=xsl: e.tensor_scalar(
                        xqp[:, xsl * D:(xsl + 1) * D],
                        xr[:, xsl * D:(xsl + 1) * D],
                        MAGIC, scb[:, 0:1], ALU.subtract, ALU.mult))
                    c.setg(("xr", xsl), "sv")
                    mv = c.mark("sv")
                    for fc in range(NFC):
                        psl = 6 + fc % 2
                        c.wait_marks("tensor", mv)
                        c.use("tensor", ("ps", psl))
                        c.t(lambda e, fc=fc, xsl=xsl, psl=psl: e.transpose(
                            pt[psl][:, 0:128],
                            xqp[:, xsl * D + fc * 128: xsl * D + (fc + 1) * 128],
                            I_f32[:, :]))
                        mt = c.mark("stn")
                        c.wait_marks("scalar", mt)
                        c.s(lambda e, fc=fc, tc=tc, psl=psl: e.activation(
                            xq[:, fc * L + tc * 128: fc * L + (tc + 1) * 128],
                            pt[psl][:, 0:128], ACT.Copy))
                        c.setg(("ps", psl), "ss")
                    c.setg(("xqp", xsl), "stn")

                if taps:
                    c.wait_marks("gpsimd", c.mark("ss"))
                    c.dma(xq_h[:, :], xq[:, :], eng="gpsimd")
                if cut == 6:
                    fscope.close()
                    raise _Cut()
                mxq = c.mark("ss")
                spc = [0]
                for ic in range(NIC):
                    wsl = ic % 2
                    c.use("sync", ("fwt", wsl))
                    c.dma(fwt[:, wsl * 1024: wsl * 1024 + 1024]
                          .rearrange("p (fc i) -> p fc i", fc=NFC),
                          wupT_h[:, ic * 128:(ic + 1) * 128]
                          .rearrange("(fc p) i -> p fc i", p=128),
                          lane="dw%d" % wsl)
                    c.dma(fwt[:, (2 + wsl) * 1024:
                              (2 + wsl) * 1024 + 1024]
                          .rearrange("p (fc i) -> p fc i", fc=NFC),
                          wgtT_h[:, ic * 128:(ic + 1) * 128]
                          .rearrange("(fc p) i -> p fc i", p=128),
                          lane="dw%d" % wsl)
                    for tt in range(NT2):
                        for gi, (spill, mx) in enumerate(
                                ((sp_up, mxu), (sp_gate, mxg))):
                            psl = (tt * 2 + gi) % 4
                            c.use("tensor", ("ps", psl))
                            c.lanewait("tensor", "dw%d" % wsl)
                            c.wait_marks("tensor", mxq)
                            for fc in range(NFC):
                                c.t(lambda e, fc=fc, tt=tt, gi=gi, wsl=wsl,
                                    psl=psl: e.matmul(
                                        pt[psl][:, 0:TL2],
                                        fwt[:, (2 * gi + wsl) * 1024 + fc * 128:
                                            (2 * gi + wsl) * 1024 + (fc + 1) * 128],
                                        xq[:, fc * L + tt * TL2:
                                           fc * L + (tt + 1) * TL2],
                                        start=(fc == 0), stop=(fc == NFC - 1)))
                            mt = c.mark("stn")
                            ssl = (ic * NT2 * 2 + tt * 2 + gi) % 4
                            c.wait_marks("scalar", mt)
                            c.use("scalar", ("stg", ssl))
                            c.s(lambda e, psl=psl, ssl=ssl: e.activation(
                                stg[:, ssl * 512: ssl * 512 + TL2],
                                pt[psl][:, 0:TL2], ACT.Copy))
                            c.setg(("ps", psl), "ss")
                            c.selfw("scalar")
                            c.s(lambda e: e.nop())
                            ms = c.mark("ss")
                            c.wait_marks("vector", ms)
                            c.v(lambda e, ic=ic, tt=tt, ssl=ssl, mx=mx:
                                e.tensor_reduce(
                                    mx[:, ic * NT2 + tt: ic * NT2 + tt + 1],
                                    stg[:, ssl * 512: ssl * 512 + TL2], AX.X,
                                    ALU.max, apply_absolute_value=True))
                            lnp = ("dp0", "dp1", "dt0", "dt1", "ds0", "ds1",
                                   "du0", "du1")[spc[0] % 8]
                            spc[0] += 1
                            c.dma(spill[ic * 128:(ic + 1) * 128,
                                        tt * TL2:(tt + 1) * TL2],
                                  stg[:, ssl * 512: ssl * 512 + TL2],
                                  eng="scalar", lane=lnp)
                            c.guards[("stg", ssl)] = {lnp: c.cnt[lnp],
                                                      "sv": c.cnt["sv"]}
                    c.setg(("fwt", wsl), "stn")

                reduce_to_all(mxu[:, :], 0)
                reduce_to_all(mxg[:, :], 1)
                collective(1, 2)
                c.v(lambda e: e.tensor_scalar(scb[:, 2:3], scg[:, 0:1],
                                              1.0 / QMAX, 1e-12, ALU.mult,
                                              ALU.max))
                c.v(lambda e: e.tensor_scalar(scb[:, 4:5], scg[:, 1:2],
                                              1.0 / QMAX, 1e-12, ALU.mult,
                                              ALU.max))
                c.selfw("vector")
                c.v(lambda e: e.reciprocal(scb[:, 3:4], scb[:, 2:3]))
                c.v(lambda e: e.reciprocal(scb[:, 5:6], scb[:, 4:5]))
                c.selfw("vector")
                mv = c.mark("sv")
                c.wait_marks("scalar", mv)
                c.wait_marks("gpsimd", mv)

                fscope.close()
                f34 = ExitStack()
                _e34 = f34.enter_context
                gld = _e34(nc.sbuf_tensor("gld", [128, 4 * L], F32))
                uld = _e34(nc.sbuf_tensor("uld", [128, 4 * L], F32))
                g2o = _e34(nc.sbuf_tensor("g2o", [128, 4 * L], F32))
                gint = _e34(nc.sbuf_tensor("gint", [128, 2 * L], F32))
                sig = _e34(nc.sbuf_tensor("sig", [128, 2 * L], F32))
                ldlanes = ("ds0", "ds1", "dr0", "dr1")
                uplanes = ("du0", "du1", "db0", "db1")
                splanes = ("dp0", "dp1", "dt0", "dt1")

                c.lanewait("sync", "dp0", "dp1", "dt0", "dt1",
                           "ds0", "ds1", "du0", "du1")
                for ic in range(NIC):
                    s4 = ic % 4
                    s2_ = ic % 2
                    c.use("sync", ("gld", s4))
                    c.dma(gld[:, s4 * L:(s4 + 1) * L],
                          sp_gate[ic * 128:(ic + 1) * 128, :],
                          lane=ldlanes[s4])
                    c.lanewait("vector", ldlanes[s4])
                    c.use("vector", ("gint", s2_))
                    c.v(lambda e, s4=s4, s2_=s2_: e.tensor_scalar(
                        gint[:, s2_ * L:(s2_ + 1) * L],
                        gld[:, s4 * L:(s4 + 1) * L],
                        scb[:, 5:6], MAGIC, ALU.mult, ALU.add))
                    c.setg(("gld", s4), "sv")
                    c.selfw("vector")
                    c.v(lambda e, s2_=s2_: e.tensor_single_scalar(
                        gint[:, s2_ * L:(s2_ + 1) * L],
                        gint[:, s2_ * L:(s2_ + 1) * L], MAGIC,
                        ALU.subtract))
                    mv = c.mark("sv")
                    c.wait_marks("scalar", mv)
                    c.use("scalar", ("sig", s2_))
                    c.s(lambda e, s2_=s2_: e.activation(
                        sig[:, s2_ * L:(s2_ + 1) * L],
                        gint[:, s2_ * L:(s2_ + 1) * L], ACT.Sigmoid,
                        scale=scb[:, 4:5]))
                    c.selfw("scalar")
                    c.s(lambda e: e.nop())
                    ms = c.mark("ss")
                    c.setg(("gint", s2_), "ss")
                    c.wait_marks("vector", ms)
                    c.v(lambda e, s2_=s2_: e.tensor_scalar(
                        sig[:, s2_ * L:(s2_ + 1) * L],
                        sig[:, s2_ * L:(s2_ + 1) * L],
                        65536.0, MAGIC, ALU.mult, ALU.add))
                    c.selfw("vector")
                    c.v(lambda e, s2_=s2_: e.tensor_scalar(
                        sig[:, s2_ * L:(s2_ + 1) * L],
                        sig[:, s2_ * L:(s2_ + 1) * L],
                        MAGIC, 65535.0, ALU.subtract, ALU.min))
                    c.selfw("vector")
                    c.use("vector", ("g2o", s4))
                    c.v(lambda e, s4=s4, s2_=s2_: e.tensor_tensor(
                        g2o[:, s4 * L:(s4 + 1) * L],
                        gint[:, s2_ * L:(s2_ + 1) * L],
                        sig[:, s2_ * L:(s2_ + 1) * L], ALU.mult))
                    c.setg(("sig", s2_), "sv")
                    c.selfw("vector")
                    c.v(lambda e, ic=ic, s4=s4: e.tensor_reduce(
                        mx4[:, ic:ic + 1],
                        g2o[:, s4 * L:(s4 + 1) * L], AX.X,
                        ALU.max, apply_absolute_value=True))
                    mv = c.mark("sv")
                    c.wait_marks("scalar", mv)
                    c.dma(sp_g2[ic * 128:(ic + 1) * 128, :],
                          g2o[:, s4 * L:(s4 + 1) * L], eng="scalar",
                          lane=splanes[s4])
                    c.guards[("g2o", s4)] = {splanes[s4]: c.cnt[splanes[s4]],
                                             "sv": c.cnt["sv"]}

                c.wait_marks("vector", c.mark("sg"))
                reduce_to_all(mx4[:, 0:NIC], 0)
                c.v(lambda e: e.tensor_tensor(sca[:, 0:1], sca[:, 0:1],
                                              scb[:, 4:5], ALU.mult))
                c.selfw("vector")
                c.v(lambda e: e.tensor_scalar_mul(sca[:, 0:1], sca[:, 0:1],
                                                  1.0 / 65536.0))
                c.selfw("vector")
                collective(2, 1)
                c.v(lambda e: e.tensor_scalar(sca[:, 2:3], scg[:, 0:1],
                                              1.0 / QMAX, 1e-12, ALU.mult,
                                              ALU.max))
                c.selfw("vector")
                c.v(lambda e: e.reciprocal(sca[:, 3:4], sca[:, 2:3]))
                c.selfw("vector")
                c.v(lambda e: e.tensor_tensor(scb[:, 6:7], scb[:, 4:5],
                                              sca[:, 3:4], ALU.mult))
                c.selfw("vector")
                c.v(lambda e: e.tensor_scalar_mul(scb[:, 6:7], scb[:, 6:7],
                                                  1.0 / 65536.0))
                c.selfw("vector")
                mv = c.mark("sv")
                c.wait_marks("gpsimd", mv)

                c.lanewait("sync", "dp0", "dp1", "dt0", "dt1")
                for ic in range(NIC):
                    s4 = ic % 4
                    s2_ = ic % 2
                    c.use("sync", ("gld", s4))
                    c.dma(gld[:, s4 * L:(s4 + 1) * L],
                          sp_g2[ic * 128:(ic + 1) * 128, :],
                          lane=ldlanes[s4])
                    c.use("sync", ("uld", s4))
                    c.dma(uld[:, s4 * L:(s4 + 1) * L],
                          sp_up[ic * 128:(ic + 1) * 128, :],
                          lane=uplanes[s4])
                    c.lanewait("vector", ldlanes[s4])
                    c.use("vector", ("gint", s2_))
                    c.v(lambda e, s4=s4, s2_=s2_: e.tensor_scalar(
                        gint[:, s2_ * L:(s2_ + 1) * L],
                        gld[:, s4 * L:(s4 + 1) * L],
                        scb[:, 6:7], MAGIC, ALU.mult, ALU.add))
                    c.setg(("gld", s4), "sv")
                    c.lanewait("vector", uplanes[s4])
                    c.use("vector", ("sig", s2_))
                    c.v(lambda e, s4=s4, s2_=s2_: e.tensor_scalar(
                        sig[:, s2_ * L:(s2_ + 1) * L],
                        uld[:, s4 * L:(s4 + 1) * L],
                        scb[:, 3:4], MAGIC, ALU.mult, ALU.add))
                    c.setg(("uld", s4), "sv")
                    c.selfw("vector")
                    c.v(lambda e, s2_=s2_: e.tensor_single_scalar(
                        gint[:, s2_ * L:(s2_ + 1) * L],
                        gint[:, s2_ * L:(s2_ + 1) * L], MAGIC,
                        ALU.subtract))
                    c.v(lambda e, s2_=s2_: e.tensor_single_scalar(
                        sig[:, s2_ * L:(s2_ + 1) * L],
                        sig[:, s2_ * L:(s2_ + 1) * L], MAGIC,
                        ALU.subtract))
                    c.selfw("vector")
                    c.use("vector", ("g2o", s4))
                    c.v(lambda e, s4=s4, s2_=s2_: e.tensor_tensor(
                        g2o[:, s4 * L:(s4 + 1) * L],
                        gint[:, s2_ * L:(s2_ + 1) * L],
                        sig[:, s2_ * L:(s2_ + 1) * L], ALU.mult))
                    c.setg(("gint", s2_), "sv")
                    c.setg(("sig", s2_), "sv")
                    c.selfw("vector")
                    c.v(lambda e, ic=ic, s4=s4: e.tensor_reduce(
                        mx5[:, ic:ic + 1],
                        g2o[:, s4 * L:(s4 + 1) * L], AX.X,
                        ALU.max, apply_absolute_value=True))
                    mv = c.mark("sv")
                    c.wait_marks("scalar", mv)
                    c.dma(sp_o[ic * 128:(ic + 1) * 128, :],
                          g2o[:, s4 * L:(s4 + 1) * L], eng="scalar",
                          lane=splanes[s4])
                    c.guards[("g2o", s4)] = {splanes[s4]: c.cnt[splanes[s4]],
                                             "sv": c.cnt["sv"]}

                c.wait_marks("vector", c.mark("sg"))
                reduce_to_all(mx5[:, 0:NIC], 0)
                c.v(lambda e: e.tensor_tensor(sca[:, 0:1], sca[:, 0:1],
                                              sca[:, 2:3], ALU.mult))
                c.selfw("vector")
                c.v(lambda e: e.tensor_tensor(sca[:, 0:1], sca[:, 0:1],
                                              scb[:, 2:3], ALU.mult))
                c.selfw("vector")
                collective(3, 1)
                c.v(lambda e: e.tensor_scalar(scb[:, 8:9], scg[:, 0:1],
                                              1.0 / QMAX, 1e-12, ALU.mult,
                                              ALU.max))
                c.selfw("vector")
                c.v(lambda e: e.reciprocal(sca[:, 4:5], scb[:, 8:9]))
                c.selfw("vector")
                c.v(lambda e: e.tensor_tensor(scb[:, 7:8], sca[:, 2:3],
                                              scb[:, 2:3], ALU.mult))
                c.selfw("vector")
                c.v(lambda e: e.tensor_tensor(scb[:, 7:8], scb[:, 7:8],
                                              sca[:, 4:5], ALU.mult))
                c.selfw("vector")
                if taps:
                    c.wait_marks("sync", c.mark("sv"))
                    c.dma(scb_h[:, :], scb[:, :])
                f34.close()
            c.barrier()

            if cut == 9:
                raise _Cut()
            f5scope = ExitStack()
            with f5scope:
                _e = f5scope.enter_context
                wdn = _e(nc.sbuf_tensor("wdn", [128, NIC * D], BF16))
                oqt = _e(nc.sbuf_tensor("oqt", [128, 2 * NIC * TL2], BF16))
                ot = _e(nc.sbuf_tensor("ot", [128, 4 * 512], F32))
                pre3 = _e(nc.sbuf_tensor("pre3", [128, (TL2 // 128) * D], F32))
                x1r = _e(nc.sbuf_tensor("x1r", [128, 2 * D], F32))
                outt = _e(nc.sbuf_tensor("outt", [128, 2 * D], F32))
                c.lanewait("sync", "dp0", "dp1", "dt0", "dt1")
                for ic in range(NIC):
                    c.dma(wdn[:, ic * D:(ic + 1) * D],
                          wdnT_h[ic * 128:(ic + 1) * 128, :])
                f5lanes = ("ds0", "ds1", "du0", "du1")
                for tt in range(NT2):
                    oqb = (tt % 2) * NIC * TL2
                    c.use("vector", ("oqt", tt % 2))
                    for ic in range(NIC):
                        osl = ic % 4
                        c.use("sync", ("ot", osl))
                        c.dma(ot[:, osl * 512: osl * 512 + TL2],
                              sp_o[ic * 128:(ic + 1) * 128,
                                   tt * TL2:(tt + 1) * TL2],
                              lane=f5lanes[osl])
                        c.lanewait("vector", f5lanes[osl])
                        c.v(lambda e, osl=osl: e.tensor_scalar(
                            ot[:, osl * 512: osl * 512 + TL2],
                            ot[:, osl * 512: osl * 512 + TL2],
                            scb[:, 7:8], MAGIC, ALU.mult, ALU.add))
                        c.selfw("vector")
                        c.v(lambda e, ic=ic, osl=osl, oqb=oqb: e.tensor_scalar(
                            oqt[:, oqb + ic * TL2: oqb + (ic + 1) * TL2],
                            ot[:, osl * 512: osl * 512 + TL2],
                            MAGIC, scb[:, 8:9], ALU.subtract, ALU.mult))
                        c.setg(("ot", osl), "sv")
                    moq = c.mark("sv")
                    c.wait_marks("tensor", moq)
                    for tcl in range(TL2 // 128):
                        tc = tt * (TL2 // 128) + tcl
                        xsl = tc % 2
                        c.use("sync", ("x1r", xsl))
                        c.dma(x1r[:, xsl * D:(xsl + 1) * D],
                              x1_h[tc * 128:(tc + 1) * 128, :],
                              lane="dx%d" % xsl)
                        for hc in range(2):
                            psl = (tcl * 2 + hc) % 4
                            c.use("tensor", ("ps", psl))
                            if tt == 0 and tcl == 0 and hc == 0:
                                c.lanewait("tensor", "dh")
                            for ic in range(NIC):
                                c.t(lambda e, ic=ic, tcl=tcl, hc=hc, psl=psl,
                                    oqb=oqb: e.matmul(
                                        pt[psl][:, 0:512],
                                        oqt[:, oqb + ic * TL2 + tcl * 128:
                                            oqb + ic * TL2 + (tcl + 1) * 128],
                                        wdn[:, ic * D + hc * 512:
                                            ic * D + (hc + 1) * 512],
                                        start=(ic == 0), stop=(ic == NIC - 1)))
                            mt = c.mark("stn")
                            c.wait_marks("vector", mt)
                            c.lanewait("vector", "dx%d" % xsl)
                            c.v(lambda e, tcl=tcl, hc=hc, psl=psl, xsl=xsl:
                                e.tensor_tensor(
                                    pre3[:, tcl * D + hc * 512:
                                         tcl * D + hc * 512 + 512],
                                    pt[psl][:, 0:512],
                                    x1r[:, xsl * D + hc * 512:
                                        xsl * D + hc * 512 + 512], ALU.add))
                            c.setg(("ps", psl), "sv")
                        c.setg(("x1r", xsl), "sv")
                        c.selfw("vector")
                        c.v(lambda e, tcl=tcl: e.tensor_reduce(
                            stat[:, tcl:tcl + 1], pre3[:, tcl * D:(tcl + 1) * D],
                            AX.X, ALU.add))
                        mv = c.mark("sv")
                        c.wait_marks("scalar", mv)
                        c.s(lambda e, tcl=tcl: e.activation(
                            sqd[:, :], pre3[:, tcl * D:(tcl + 1) * D], ACT.Square,
                            accum_out=stat[:, 8 + tcl: 9 + tcl]))
                    ms = c.mark("ss")
                    c.wait_marks("vector", ms)
                    ln_rstd(TL2 // 128)
                    for tcl in range(TL2 // 128):
                        tc = tt * (TL2 // 128) + tcl
                        osl = tcl % 2
                        c.use("vector", ("outt", osl))
                        c.v(lambda e, tcl=tcl, osl=osl: e.tensor_scalar(
                            outt[:, osl * D:(osl + 1) * D],
                            pre3[:, tcl * D:(tcl + 1) * D],
                            stat[:, 16 + tcl:17 + tcl],
                            stat[:, 24 + tcl:25 + tcl],
                            ALU.subtract, ALU.mult))
                        mv = c.mark("sv")
                        c.wait_marks("scalar", mv)
                        c.dma(outT[tc * 128:(tc + 1) * 128, :],
                              outt[:, osl * D:(osl + 1) * D], eng="scalar",
                              lane="dst")
                        c.guards[("outt", osl)] = {"dst": c.cnt["dst"]}
                    c.setg(("oqt", tt % 2), "stn")

        except _Cut:
            pass
        if cut < 99:
            c.barrier()
            for tc in range(L // 128):
                c.dma(outT[tc * 128:(tc + 1) * 128, :], sqd[:, :])

        c.barrier()
        for ln in c.dma_lanes:
            c.wait("sync", ln)
        c.wait("gpsimd", "dg")

        with nc.Block() as blk:
            for eng in ENGINES:
                ops = c.q[eng]

                def make(ops):
                    def body(e):
                        for f in ops:
                            f(e)
                    return body
                getattr(blk, eng)(make(ops))

    return nc


_NC_CACHE = {}


def _get_nc(L, n_cores, taps=False, cut=99):
    key = (L, n_cores, taps, cut)
    if key not in _NC_CACHE:
        _NC_CACHE[key] = build_nc(L, n_cores, taps, cut=cut)
    return _NC_CACHE[key]


def make_in_maps(inputs, n_cores=8):
    f = lambda k: np.ascontiguousarray(np.asarray(inputs[k], dtype=np.float32))
    x = f("L0_attn_x")
    base = {
        "ln_g": f("L0_attn_ln_gamma"), "ln_b": f("L0_attn_ln_beta"),
        "w_in": f("L0_attn_W_in"), "b_in": f("L0_attn_b_in"),
        "conv_w": f("L0_attn_conv_w"), "conv_b": f("L0_attn_conv_b"),
        "w_x": f("L0_attn_W_x"), "w_dt": f("L0_attn_W_dt"),
        "b_dt": f("L0_attn_b_dt"), "a_log": f("L0_attn_A_log"),
        "d_vec": f("L0_attn_D"), "w_out": f("L0_attn_W_out"),
        "b_out": f("L0_attn_b_out"), "w_gate": f("L0_ffn_w_gate"),
        "w_up": f("L0_ffn_w_up"), "w_down": f("L0_ffn_w_down"),
    }
    return [dict(base, x=np.ascontiguousarray(x[i])) for i in range(n_cores)]


def kernel(**inputs):
    n_cores = 8
    in_maps = make_in_maps(inputs, n_cores)
    nc = _get_nc(in_maps[0]["x"].shape[0], n_cores)
    res = run_bass_kernel_spmd(nc, in_maps, core_ids=list(range(n_cores)))
    return np.stack([res.results[i]["out"] for i in range(n_cores)], axis=0)


# version 20 (final)
# speedup vs baseline: 1.1748x; 1.0407x over previous
from contextlib import ExitStack

import numpy as np

import concourse.bass as bass
import concourse.mybir as mybir
from concourse.bass_utils import run_bass_kernel_spmd

F32 = mybir.dt.float32
BF16 = mybir.dt.bfloat16
ALU = mybir.AluOpType
ACT = mybir.ActivationFunctionType
AX = mybir.AxisListType

MAGIC = float(np.float32(1.5 * 2**23))
QMAX = 32767.0
EPS = 1e-5

D, DI, NS, RK, KC, HI = 1024, 2048, 16, 64, 4, 3072
NKC, NFC, NIC = DI // 128, D // 128, HI // 128

ENGINES = ("sync", "vector", "scalar", "tensor", "gpsimd")
DBG = {"no_collective": False, "no_r2s": False, "no_bcast": False}


class Ctx:

    def __init__(self, nc, stack):
        self.nc = nc
        self.q = {e: [] for e in ENGINES}
        self.sem = {}
        self.cnt = {}
        self.dma_lanes = ("dh", "dx0", "dx1", "dw0", "dw1", "db0", "db1",
                          "ds0", "ds1", "du0", "du1", "dr0", "dr1",
                          "dt0", "dt1", "dst", "dp0", "dp1", "dg")
        for name in self.dma_lanes + ("sv", "ss", "stn", "sg", "cc"):
            self.sem[name] = stack.enter_context(nc.semaphore(name))
            self.cnt[name] = 0
        self.guards = {}
        self.eng_sem = {"vector": "sv", "scalar": "ss", "tensor": "stn",
                        "gpsimd": "sg"}

    def wait(self, eng, name, val=None):
        if val is None:
            val = self.cnt[name]
        if val <= 0:
            return
        h = self.sem[name]
        self.q[eng].append(lambda e, h=h, v=val: e.wait_ge(h, v))

    def run(self, eng, fn, inc=None):
        if inc is None:
            self.q[eng].append(fn)
            return None
        step = 16 if inc in self.dma_lanes else 1
        self.cnt[inc] += step
        h = self.sem[inc]
        self.q[eng].append(lambda e, fn=fn, h=h, s=step: fn(e).then_inc(h, s))
        return self.cnt[inc]

    def v(self, fn):
        return self.run("vector", fn, "sv")

    def s(self, fn):
        return self.run("scalar", fn, "ss")

    def t(self, fn):
        return self.run("tensor", fn, "stn")

    def g(self, fn):
        return self.run("gpsimd", fn, "sg")

    def dma(self, out, in_, eng="sync", slow=False, lane=None):
        sem = lane or ("dg" if eng == "gpsimd" else "dh")
        return self.run(eng, lambda e, o=out, i=in_, s=slow: e.dma_start(
            out=o, in_=i, allow_slow_non_contiguous=s), sem)

    def mark(self, *names):
        if not names:
            names = ("sv", "ss", "stn", "sg", "dh", "dg")
        return {n: self.cnt[n] for n in names}

    def wait_marks(self, eng, marks):
        for name, val in marks.items():
            if val > 0:
                self.wait(eng, name, val)

    def use(self, eng, key):
        m = self.guards.get(key)
        if m:
            self.wait_marks(eng, m)

    def setg(self, key, *names):
        self.guards[key] = self.mark(*names)

    def selfw(self, eng):
        self.wait(eng, self.eng_sem[eng])

    def lanewait(self, eng, *lanes):
        for ln in lanes:
            self.wait(eng, ln)

    def barrier(self):
        for eng, sname in (("vector", "sv"), ("scalar", "ss"),
                           ("tensor", "stn"), ("gpsimd", "sg")):
            self.run(eng, lambda e: e.nop(), sname)
        m = {n: self.cnt[n] for n in
             ("sv", "ss", "stn", "sg") + self.dma_lanes}
        for eng in ENGINES:
            self.wait_marks(eng, m)


def build_nc(L=2048, n_cores=8, taps=False, cut=99):
    TH = L // 2
    NTC = TH // 128
    TTL = min(512, TH)
    NTT = TH // TTL
    TL2 = min(512, L)
    NT2 = L // TL2
    YB = max(1, TH // 512)
    LNB = min(2, NTC)

    nc = bass.Bass("TRN2", target_bir_lowering=False,
                   detect_race_conditions=False)

    xT = nc.dram_tensor("x", [L, D], F32, kind="ExternalInput").ap()
    g_ln = nc.dram_tensor("ln_g", [D], F32, kind="ExternalInput").ap()
    b_ln = nc.dram_tensor("ln_b", [D], F32, kind="ExternalInput").ap()
    w_in = nc.dram_tensor("w_in", [D, 2 * DI], F32, kind="ExternalInput").ap()
    b_in = nc.dram_tensor("b_in", [2 * DI], F32, kind="ExternalInput").ap()
    conv_w = nc.dram_tensor("conv_w", [KC, DI], F32, kind="ExternalInput").ap()
    conv_b = nc.dram_tensor("conv_b", [DI], F32, kind="ExternalInput").ap()
    w_x = nc.dram_tensor("w_x", [DI, RK + 2 * NS], F32, kind="ExternalInput").ap()
    w_dt = nc.dram_tensor("w_dt", [RK, DI], F32, kind="ExternalInput").ap()
    b_dt = nc.dram_tensor("b_dt", [DI], F32, kind="ExternalInput").ap()
    a_log = nc.dram_tensor("a_log", [DI, NS], F32, kind="ExternalInput").ap()
    d_vec = nc.dram_tensor("d_vec", [DI], F32, kind="ExternalInput").ap()
    w_out = nc.dram_tensor("w_out", [DI, D], F32, kind="ExternalInput").ap()
    b_out = nc.dram_tensor("b_out", [D], F32, kind="ExternalInput").ap()
    w_gate = nc.dram_tensor("w_gate", [HI, D], F32, kind="ExternalInput").ap()
    w_up = nc.dram_tensor("w_up", [HI, D], F32, kind="ExternalInput").ap()
    w_down = nc.dram_tensor("w_down", [D, HI], F32, kind="ExternalInput").ap()
    outT = nc.dram_tensor("out", [L, D], F32, kind="ExternalOutput").ap()

    wkind = "ExternalOutput" if taps else "Internal"
    win_h = nc.dram_tensor("win_h", [D, 2 * DI], BF16).ap()
    wout_h = nc.dram_tensor("wout_h", [DI, D], BF16).ap()
    wupT_h = nc.dram_tensor("wupT_h", [D, HI], BF16, kind=wkind).ap()
    wgtT_h = nc.dram_tensor("wgtT_h", [D, HI], BF16, kind=wkind).ap()
    wdnT_h = nc.dram_tensor("wdnT_h", [HI, D], BF16, kind=wkind).ap()
    sres_h = nc.dram_tensor("sres_h", [DI, L], BF16).ap()
    spkind = "ExternalOutput" if taps else "Internal"
    sp_up = nc.dram_tensor("sp_up", [HI, L], F32, kind=spkind).ap()
    sp_gate = nc.dram_tensor("sp_gate", [HI, L], F32, kind=spkind).ap()
    sp_g2 = nc.dram_tensor("sp_g2", [HI, L], F32, kind=spkind).ap()
    sp_o = nc.dram_tensor("sp_o", [HI, L], F32, kind=spkind).ap()
    x1_h = nc.dram_tensor("x1_h", [L, D], F32).ap()
    bc_h = nc.dram_tensor("bc_h", [2 * NS, L], BF16).ap()
    sc_h = nc.dram_tensor("sc_h", [4, 32], F32,
                          kind="ExternalOutput" if taps else "Internal").ap()
    r2s_h = nc.dram_tensor("r2s_h", [16, 128], F32,
                           kind="ExternalOutput" if taps else "Internal").ap()
    cc_bufs = [(nc.dram_tensor("cci%d" % i, [1, 8], F32).ap(),
                nc.dram_tensor("cco%d" % i, [1, 8], F32).ap())
               for i in range(4)]
    scb_h = (nc.dram_tensor("scb_h", [128, 12], F32, kind="ExternalOutput").ap()
             if taps else None)
    xq_h = (nc.dram_tensor("xq_h", [128, NFC * L], BF16,
                           kind="ExternalOutput").ap() if taps else None)
    tap = {}
    if taps:
        for nm, shp in (("tap_u", [DI, L]), ("tap_delta", [DI, L]),
                        ("tap_ymod", [DI, L]), ("tap_x1", [L, D])):
            tap[nm] = nc.dram_tensor(nm, shp, F32, kind="ExternalOutput").ap()

    stack = ExitStack()
    with stack:
        c = Ctx(nc, stack)

        def sb(name, shape, dt=F32):
            return stack.enter_context(nc.sbuf_tensor(name, shape, dt))

        pt = [stack.enter_context(nc.psum_tensor("ps%d" % i, [128, 512], F32))
              for i in range(8)]

        A_sb = sb("A_sb", [128, NKC * NS])
        cw_sb = sb("cw_sb", [128, KC * NKC])
        cb_sb = sb("cb_sb", [128, NKC])
        dv_sb = sb("dv_sb", [128, NKC])
        bdt_sb = sb("bdt_sb", [128, NKC])
        bin_sb = sb("bin_sb", [128, 2 * NKC])
        gam_sb = sb("gam_sb", [128, NFC])
        bet_sb = sb("bet_sb", [128, NFC])
        bout_b = sb("bout_b", [128, D], BF16)
        I_bf = sb("I_bf", [128, 128], BF16)
        I_f32 = sb("I_f32", [128, 128])
        state = sb("state", [128, NKC * NS])
        scal = sb("scal", [1, 32])
        scb = sb("scb", [128, 12])
        sca = sb("sca", [128, 8])
        scg = sb("scg", [128, 8])
        zero8 = sb("zero8", [1, 8])
        wx_bf = sb("wx_bf", [128, NKC * 128], BF16)
        wdt_bf = sb("wdt_bf", [64, DI], BF16)
        stat = sb("stat", [128, 64])
        sqd = sb("sqd", [128, D])
        mxu = sb("mxu", [128, max(16, NIC * NT2)])
        mxg = sb("mxg", [128, NIC * NT2])
        mx4 = sb("mx4", [128, NIC * NT2])
        mx5 = sb("mx5", [128, NIC * NT2])

        c.dma(A_sb[:, :].rearrange("p (k n) -> p k n", n=NS),
              a_log.rearrange("(k p) n -> p k n", p=128))
        c.dma(cw_sb[:, :].rearrange("p (i k) -> p i k", i=KC),
              conv_w.rearrange("i (k p) -> p i k", p=128), slow=True)
        c.dma(cb_sb[:, :], conv_b.rearrange("(k p) -> p k", p=128), slow=True)
        c.dma(dv_sb[:, :], d_vec.rearrange("(k p) -> p k", p=128), slow=True)
        c.dma(bdt_sb[:, :], b_dt.rearrange("(k p) -> p k", p=128), slow=True)
        c.dma(bin_sb[:, :], b_in.rearrange("(k p) -> p k", p=128), slow=True)
        c.dma(gam_sb[:, :], g_ln.rearrange("(k p) -> p k", p=128), slow=True)
        c.dma(bet_sb[:, :], b_ln.rearrange("(k p) -> p k", p=128), slow=True)
        c.dma(bout_b[:, :],
              b_out.rearrange("(o f) -> o f", o=1).broadcast_to((128, D)),
              eng="gpsimd")
        c.g(lambda e: e.memset(wx_bf[:, :], 0.0))
        c.wait("gpsimd", "sg")
        c.dma(wx_bf[:, :].rearrange("p (k j) -> p k j", j=128)[:, :, 0:RK],
              w_x.rearrange("(k p) j -> p k j", p=128)[:, :, 0:RK],
              eng="gpsimd")
        c.dma(wx_bf[:, :].rearrange("p (k j) -> p k j", j=128)[:, :, 64:64 + NS],
              w_x.rearrange("(k p) j -> p k j", p=128)[:, :, RK:RK + NS],
              eng="gpsimd")
        c.dma(wx_bf[:, :].rearrange("p (k j) -> p k j", j=128)[:, :, 96:96 + NS],
              w_x.rearrange("(k p) j -> p k j", p=128)[:, :, RK + NS:RK + 2 * NS],
              eng="gpsimd")
        c.dma(wdt_bf[0:64, :], w_dt[:, :], eng="gpsimd")
        c.wait("scalar", "dh")
        c.s(lambda e: e.activation(A_sb[:, :], A_sb[:, :], ACT.Exp))
        c.wait("vector", "ss")
        c.v(lambda e: e.tensor_scalar_mul(bdt_sb[:, :], bdt_sb[:, :], -1.0))
        c.g(lambda e: e.iota(I_f32[:, :], [[1, 128]], base=0,
                             channel_multiplier=0,
                             allow_small_or_imprecise_dtypes=True))
        c.g(lambda e: e.iota(scb[:, 9:10], [[0, 1]], base=0,
                             channel_multiplier=1,
                             allow_small_or_imprecise_dtypes=True))
        mg = c.mark("sg")
        c.wait_marks("vector", mg)
        c.v(lambda e: e.tensor_tensor(I_f32[:, :], I_f32[:, :],
                                      scb[:, 9:10].broadcast_to((128, 128)),
                                      ALU.is_equal))
        c.selfw("vector")
        c.v(lambda e: e.tensor_copy(I_bf[:, :], I_f32[:, :]))
        c.g(lambda e: e.memset(state[:, :], 0.0))
        c.g(lambda e: e.memset(zero8[:, :], 0.0))
        c.g(lambda e: e.memset(scal[:, :], 0.0))
        c.g(lambda e: e.memset(scb[:, 10:12], 0.0))
        c.g(lambda e: e.memset(sca[:, :], 0.0))
        c.g(lambda e: e.memset(scg[:, :], 0.0))
        c.barrier()

        r2s_idx = [0]

        def reduce_to_all(src_ap, dst_col):
            idx = r2s_idx[0] % 16
            r2s_idx[0] += 1
            c.selfw("vector")
            c.use("vector", "stat63")
            c.v(lambda e: e.tensor_reduce(stat[:, 63:64], src_ap, AX.X,
                                          ALU.max, apply_absolute_value=True))
            mv = c.mark("sv")
            c.wait_marks("sync", mv)
            c.dma(r2s_h[idx, :].rearrange("(p o) -> p o", o=1),
                  stat[:, 63:64])
            c.setg("stat63", "dh")
            c.lanewait("sync", "dh")
            c.dma(sqd[:, 0:128],
                  r2s_h[idx, :].rearrange("(o f) -> o f", o=1)
                  .broadcast_to((128, 128)))
            c.lanewait("vector", "dh")
            c.v(lambda e, d=dst_col: e.tensor_reduce(
                sca[:, d:d + 1], sqd[:, 0:128], AX.X, ALU.max,
                apply_absolute_value=True))
            c.selfw("vector")

        def collective(idx, n_in):
            cci, cco = cc_bufs[idx]
            if DBG["no_collective"]:
                c.v(lambda e, n=n_in: e.tensor_copy(scg[:, 0:n],
                                                    sca[:, 0:n]))
                c.selfw("vector")
                return
            mv = c.mark("sv")
            c.wait_marks("sync", mv)
            c.dma(cci[0:1, 0:8], zero8[0:1, 0:8])
            c.lanewait("sync", "dh")
            c.dma(cci[0:1, 0:n_in], sca[0:1, 0:n_in])
            c.lanewait("gpsimd", "dh")
            c.run("gpsimd", lambda e: e.collective_compute(
                "AllReduce", ALU.max,
                replica_groups=[list(range(n_cores))],
                ins=[cci.opt()], outs=[cco.opt()]), "cc")
            c.wait("sync", "cc")
            c.dma(scg[:, 0:n_in],
                  cco[0:1, 0:n_in].broadcast_to((128, n_in)))
            c.lanewait("vector", "dh")

        def ln_rstd(n):
            c.selfw("scalar")
            c.s(lambda e: e.nop())
            c.wait_marks("vector", c.mark("ss"))
            c.selfw("vector")
            c.v(lambda e: e.tensor_scalar_mul(stat[:, 16:16 + n],
                                              stat[:, 0:n], 1.0 / D))
            c.v(lambda e: e.tensor_scalar_mul(stat[:, 32:32 + n],
                                              stat[:, 8:8 + n], 1.0 / D))
            c.selfw("vector")
            c.v(lambda e: e.tensor_tensor(stat[:, 40:40 + n],
                                          stat[:, 16:16 + n],
                                          stat[:, 16:16 + n], ALU.mult))
            c.selfw("vector")
            c.v(lambda e: e.tensor_tensor(stat[:, 40:40 + n],
                                          stat[:, 32:32 + n],
                                          stat[:, 40:40 + n], ALU.subtract))
            c.selfw("vector")
            c.v(lambda e: e.tensor_scalar_add(stat[:, 40:40 + n],
                                              stat[:, 40:40 + n], EPS))
            mv = c.mark("sv")
            c.wait_marks("scalar", mv)
            c.s(lambda e: e.activation(stat[:, 48:48 + n],
                                       stat[:, 40:40 + n], ACT.Sqrt))
            c.selfw("scalar")
            c.s(lambda e: e.nop())
            ms = c.mark("ss")
            c.wait_marks("vector", ms)
            c.v(lambda e: e.reciprocal(stat[:, 24:24 + n], stat[:, 48:48 + n]))
            c.selfw("vector")
            c.v(lambda e: e.tensor_tensor(stat[:, 56:56 + n],
                                          stat[:, 40:40 + n],
                                          stat[:, 24:24 + n], ALU.mult))
            c.selfw("vector")
            c.v(lambda e: e.tensor_tensor(stat[:, 56:56 + n],
                                          stat[:, 56:56 + n],
                                          stat[:, 24:24 + n], ALU.mult))
            c.selfw("vector")
            c.v(lambda e: e.tensor_scalar(stat[:, 56:56 + n],
                                          stat[:, 56:56 + n], -0.5, 1.5,
                                          ALU.mult, ALU.add))
            c.selfw("vector")
            c.v(lambda e: e.tensor_tensor(stat[:, 24:24 + n],
                                          stat[:, 24:24 + n],
                                          stat[:, 56:56 + n], ALU.mult))
            c.selfw("vector")

        class _Cut(Exception):
            pass

        try:
            p0scope = ExitStack()
            with p0scope:
                _e = p0scope.enter_context
                raw = _e(nc.sbuf_tensor("raw", [128, 4 * 4096], F32))
                dqs = _e(nc.sbuf_tensor("dqs", [128, 4 * 128], F32))
                dqr = _e(nc.sbuf_tensor("dqr", [128, 4 * 128], F32))
                trb = _e(nc.sbuf_tensor("trb", [128, 8 * 128], BF16))
                cstb = _e(nc.sbuf_tensor("cstb", [128, 2 * 4096], BF16))
                p0lanes = ("dw0", "dw1", "dx0", "dx1",
                           "db0", "db1", "ds0", "ds1")
                casts = [("c", (w_in, win_h, rc, 2 * DI)) for rc in range(NFC)] + \
                        [("c", (w_out, wout_h, rc, D)) for rc in range(NKC)]
                groups = [("g", (w_up, wupT_h, 4 * g, 4, D)) for g in range(6)] + \
                         [("g", (w_gate, wgtT_h, 4 * g, 4, D)) for g in range(6)] + \
                         [("g", (w_down, wdnT_h, oc, 1, HI)) for oc in range(8)]
                seq = []
                while casts or groups:
                    if casts:
                        seq.append(casts.pop(0))
                    if groups:
                        seq.append(groups.pop(0))
                tri = 0
                for idx, (kind, item) in enumerate(seq):
                    s4 = idx % 4
                    ldl = p0lanes[idx % 8]
                    if kind == "c":
                        src, dst, rc, NCOL = item
                        sl = idx % 2
                        c.use("sync", ("p0raw", s4))
                        c.dma(raw[:, s4 * 4096: s4 * 4096 + NCOL],
                              src[rc * 128:(rc + 1) * 128, :], lane=ldl)
                        c.lanewait("vector", ldl)
                        c.use("vector", ("p0cst", sl))
                        c.v(lambda e, s4=s4, sl=sl, NCOL=NCOL: e.tensor_copy(
                            cstb[:, sl * 4096: sl * 4096 + NCOL],
                            raw[:, s4 * 4096: s4 * 4096 + NCOL]))
                        c.setg(("p0raw", s4), "sv")
                        mv = c.mark("sv")
                        c.wait_marks("scalar", mv)
                        ln = "dp%d" % sl
                        c.dma(dst[rc * 128:(rc + 1) * 128, :],
                              cstb[:, sl * 4096: sl * 4096 + NCOL],
                              eng="scalar", lane=ln)
                        c.guards[("p0cst", sl)] = {ln: c.cnt[ln]}
                        continue
                    wsrc, wdst, oc0, GRP, I = item
                    gw = GRP * I
                    NB = gw // 32
                    rawt = raw[:, s4 * 4096: s4 * 4096 + gw]
                    sct = dqs[:, s4 * 128: s4 * 128 + NB]
                    rct = dqr[:, s4 * 128: s4 * 128 + NB]
                    c.use("sync", ("p0raw", s4))
                    if GRP == 1:
                        c.dma(rawt, wsrc[oc0 * 128:(oc0 + 1) * 128, :],
                              lane=ldl)
                    else:
                        c.dma(rawt.rearrange("p (o i) -> p o i", o=GRP),
                              wsrc[oc0 * 128:(oc0 + GRP) * 128, :]
                              .rearrange("(o p) i -> p o i", p=128),
                              lane=ldl)
                    c.lanewait("vector", ldl)
                    c.use("vector", ("p0dq", s4))
                    c.v(lambda e, r=rawt, s=sct: e.tensor_reduce(
                        s, r.rearrange("p (b k) -> p b k", k=32), AX.X,
                        ALU.max, apply_absolute_value=True))
                    c.selfw("vector")
                    c.v(lambda e, s=sct: e.tensor_scalar(
                        s, s, 1.0 / 7.0, 1e-12, ALU.mult, ALU.max))
                    c.selfw("vector")
                    c.v(lambda e, s=sct, r=rct: e.reciprocal(r, s))
                    c.selfw("vector")
                    c.v(lambda e, r=rawt, rc_=rct, NB=NB: e.tensor_tensor(
                        r.rearrange("p (b k) -> p b k", k=32),
                        r.rearrange("p (b k) -> p b k", k=32),
                        rc_.rearrange("p (b o) -> p b o", o=1)
                        .broadcast_to((128, NB, 32)), ALU.mult))
                    c.v(lambda e, r=rawt: e.tensor_scalar(
                        r, r, MAGIC, MAGIC, ALU.add, ALU.subtract))
                    c.v(lambda e, r=rawt, s=sct, NB=NB: e.tensor_tensor(
                        r.rearrange("p (b k) -> p b k", k=32),
                        r.rearrange("p (b k) -> p b k", k=32),
                        s.rearrange("p (b o) -> p b o", o=1)
                        .broadcast_to((128, NB, 32)), ALU.mult))
                    mwq = c.mark("sv")
                    c.setg(("p0dq", s4), "sv")
                    NBT = I // 128
                    for b in range(gw // 128):
                        ocl, bb = divmod(b, NBT)
                        psl = tri % 8
                        tsl = tri % 8
                        c.wait_marks("tensor", mwq)
                        c.use("tensor", ("ps", psl))
                        c.t(lambda e, b=b, w=rawt, psl=psl: e.transpose(
                            pt[psl][:, 0:128], w[:, b * 128:(b + 1) * 128],
                            I_f32[:, :]))
                        mt = c.mark("stn")
                        c.wait_marks("scalar", mt)
                        c.use("scalar", ("p0trb", tsl))
                        c.s(lambda e, psl=psl, tsl=tsl: e.activation(
                            trb[:, tsl * 128:(tsl + 1) * 128],
                            pt[psl][:, 0:128], ACT.Copy))
                        c.setg(("ps", psl), "ss")
                        c.selfw("scalar")
                        c.s(lambda e: e.nop())
                        ln = ("dt0", "dt1", "dst", "dh")[tsl % 4]
                        c.dma(wdst[bb * 128:(bb + 1) * 128,
                                   (oc0 + ocl) * 128:(oc0 + ocl + 1) * 128],
                              trb[:, tsl * 128:(tsl + 1) * 128],
                              eng="scalar", lane=ln)
                        c.guards[("p0trb", tsl)] = {ln: c.cnt[ln]}
                        tri += 1
                    c.setg(("p0raw", s4), "stn")
            c.barrier()
            if cut == 2:
                raise _Cut()

            mscope = ExitStack()
            with mscope:
                _e = mscope.enter_context
                xbuf = _e(nc.sbuf_tensor("xbuf", [128, LNB * D], F32))
                xnp = _e(nc.sbuf_tensor("xnp", [128, D], F32))
                xn_fm = _e(nc.sbuf_tensor("xn_fm", [128, NFC * TH], BF16))
                u0 = _e(nc.sbuf_tensor("u0", [128, NKC * (TH + 3)], BF16))
                delta = _e(nc.sbuf_tensor("delta", [128, NKC * TH], BF16))
                dt_sb = _e(nc.sbuf_tensor("dt_sb", [64, TH], BF16))
                bcB_sb = _e(nc.sbuf_tensor("bcB_sb", [16, TH], BF16))
                bcC_sb = _e(nc.sbuf_tensor("bcC_sb", [16, TH], BF16))
                wint = _e(nc.sbuf_tensor("wint", [128, 2 * 1024], BF16))
                srstg = _e(nc.sbuf_tensor("srstg", [128, 2 * TTL], BF16))
                acc2 = _e(nc.sbuf_tensor("acc2", [128, TH], BF16))
                tails = _e(nc.sbuf_tensor("tails", [128, NKC * 3], BF16))
                gkt = _e(nc.sbuf_tensor("gkt", [128, 4 * TH], BF16))
                srkg = _e(nc.sbuf_tensor("srkg", [128, 2 * TH], BF16))
                Bb = _e(nc.sbuf_tensor("Bb", [128, 3 * TH], BF16))
                Cb = _e(nc.sbuf_tensor("Cb", [128, 3 * TH], BF16))
                ab = _e(nc.sbuf_tensor("ab", [128, 3 * TH], F32))
                bb = _e(nc.sbuf_tensor("bb", [128, 3 * TH], BF16))
                hb = _e(nc.sbuf_tensor("hb", [128, 2 * TH], BF16))
                wout_bf = _e(nc.sbuf_tensor("wout_bf", [128, NKC * D], BF16))
                xb2 = _e(nc.sbuf_tensor("xb2", [128, D], F32))

                def u0k(kc):
                    return u0[:, kc * (TH + 3): kc * (TH + 3) + TH + 3]

                for h in range(2):
                    t0 = h * TH
                    for tcg in range(NTC // LNB):
                        for tc2 in range(LNB):
                            tc = tcg * LNB + tc2
                            c.use("sync", ("xbuf", tc2))
                            c.dma(xbuf[:, tc2 * D:(tc2 + 1) * D],
                                  xT[t0 + tc * 128: t0 + (tc + 1) * 128, :],
                                  lane="dx%d" % tc2)
                            c.lanewait("vector", "dx%d" % tc2)
                            c.v(lambda e, tc2=tc2: e.tensor_reduce(
                                stat[:, tc2:tc2 + 1],
                                xbuf[:, tc2 * D:(tc2 + 1) * D], AX.X, ALU.add))
                            c.lanewait("scalar", "dx%d" % tc2)
                            c.s(lambda e, tc2=tc2: e.activation(
                                sqd[:, :], xbuf[:, tc2 * D:(tc2 + 1) * D],
                                ACT.Square, accum_out=stat[:, 8 + tc2: 9 + tc2]))
                        ms = c.mark("ss")
                        c.wait_marks("vector", ms)
                        ln_rstd(LNB)
                        for tc2 in range(LNB):
                            tc = tcg * LNB + tc2
                            c.use("vector", ("xnp", 0))
                            c.v(lambda e, tc2=tc2: e.tensor_scalar(
                                xnp[:, :],
                                xbuf[:, tc2 * D:(tc2 + 1) * D],
                                stat[:, 16 + tc2:17 + tc2],
                                stat[:, 24 + tc2:25 + tc2],
                                ALU.subtract, ALU.mult))
                            c.setg(("xbuf", tc2), "sv")
                            mv = c.mark("sv")
                            for fc in range(NFC):
                                psl = 6 + fc % 2
                                c.wait_marks("tensor", mv)
                                c.use("tensor", ("ps", psl))
                                c.t(lambda e, fc=fc, psl=psl: e.transpose(
                                    pt[psl][:, 0:128],
                                    xnp[:, fc * 128:(fc + 1) * 128],
                                    I_f32[:, :]))
                                mt = c.mark("stn")
                                c.wait_marks("vector", mt)
                                c.v(lambda e, fc=fc, tc=tc, psl=psl: e.tensor_scalar(
                                    xn_fm[:, fc * TH + tc * 128:
                                          fc * TH + (tc + 1) * 128],
                                    pt[psl][:, 0:128], gam_sb[:, fc:fc + 1],
                                    bet_sb[:, fc:fc + 1], ALU.mult, ALU.add))
                                c.setg(("ps", psl), "sv")
                            c.setg(("xnp", 0), "stn")
                    mxn = c.mark("sv")
                    for jc in range(2 * NKC):
                        wsl = jc % 2
                        c.use("sync", ("wint", wsl))
                        c.dma(wint[:, wsl * 1024:(wsl + 1) * 1024]
                              .rearrange("p (k j) -> p k j", k=NFC),
                              win_h[:, jc * 128:(jc + 1) * 128]
                              .rearrange("(k p) j -> p k j", p=128),
                              lane="dw%d" % wsl)
                        for tt in range(NTT):
                            psl = (jc * NTT + tt) % 4
                            c.use("tensor", ("ps", psl))
                            c.lanewait("tensor", "dw%d" % wsl)
                            c.wait_marks("tensor", mxn)
                            for kc in range(NFC):
                                c.t(lambda e, kc=kc, tt=tt, wsl=wsl, psl=psl:
                                    e.matmul(pt[psl][:, 0:TTL],
                                             wint[:, wsl * 1024 + kc * 128:
                                                  wsl * 1024 + (kc + 1) * 128],
                                             xn_fm[:, kc * TH + tt * TTL:
                                                   kc * TH + (tt + 1) * TTL],
                                             start=(kc == 0),
                                             stop=(kc == NFC - 1)))
                            mt = c.mark("stn")
                            c.wait_marks("scalar", mt)
                            if jc < NKC:
                                c.s(lambda e, jc=jc, tt=tt, psl=psl: e.activation(
                                    u0[:, jc * (TH + 3) + 3 + tt * TTL:
                                       jc * (TH + 3) + 3 + (tt + 1) * TTL],
                                    pt[psl][:, 0:TTL], ACT.Identity,
                                    bias=bin_sb[:, jc:jc + 1]))
                                c.setg(("ps", psl), "ss")
                            else:
                                ssl = (jc * NTT + tt) % 2
                                c.use("scalar", ("srstg", ssl))
                                c.s(lambda e, jc=jc, psl=psl, ssl=ssl: e.activation(
                                    srstg[:, ssl * TTL:(ssl + 1) * TTL],
                                    pt[psl][:, 0:TTL], ACT.Silu,
                                    bias=bin_sb[:, jc:jc + 1]))
                                c.setg(("ps", psl), "ss")
                                msl = c.mark("ss")
                                c.wait_marks("sync", msl)
                                lnr = "dr%d" % ssl
                                c.dma(sres_h[(jc - NKC) * 128:
                                             (jc - NKC + 1) * 128,
                                             t0 + tt * TTL:
                                             t0 + (tt + 1) * TTL],
                                      srstg[:, ssl * TTL:(ssl + 1) * TTL],
                                      lane=lnr)
                                c.guards[("srstg", ssl)] = {lnr: c.cnt[lnr]}
                        c.setg(("wint", wsl), "stn")
                    msu = c.mark("ss")
                    c.wait_marks("vector", {"ss": msu["ss"]})
                    if h == 0:
                        for kc in range(NKC):
                            c.v(lambda e, kc=kc: e.memset(u0k(kc)[:, 0:3], 0.0))
                    else:
                        for kc in range(NKC):
                            c.v(lambda e, kc=kc: e.tensor_copy(
                                u0k(kc)[:, 0:3], tails[:, kc * 3:kc * 3 + 3]))
                    c.selfw("vector")
                    for kc in range(NKC):
                        if h == 0:
                            c.v(lambda e, kc=kc: e.tensor_copy(
                                tails[:, kc * 3:kc * 3 + 3],
                                u0k(kc)[:, TH:TH + 3]))
                        c.v(lambda e, kc=kc: e.tensor_scalar(
                            acc2[:, 0:TH], u0k(kc)[:, 0:TH],
                            cw_sb[:, 0 * NKC + kc:0 * NKC + kc + 1],
                            cb_sb[:, kc:kc + 1], ALU.mult, ALU.add))
                        c.v(lambda e, kc=kc: e.scalar_tensor_tensor(
                            acc2[:, 0:TH], u0k(kc)[:, 1:TH + 1],
                            cw_sb[:, 1 * NKC + kc:1 * NKC + kc + 1],
                            acc2[:, 0:TH], ALU.mult, ALU.add))
                        c.v(lambda e, kc=kc: e.scalar_tensor_tensor(
                            acc2[:, 0:TH], u0k(kc)[:, 2:TH + 2],
                            cw_sb[:, 2 * NKC + kc:2 * NKC + kc + 1],
                            acc2[:, 0:TH], ALU.mult, ALU.add))
                        c.v(lambda e, kc=kc: e.scalar_tensor_tensor(
                            u0k(kc)[:, 3:TH + 3], u0k(kc)[:, 3:TH + 3],
                            cw_sb[:, 3 * NKC + kc:3 * NKC + kc + 1],
                            acc2[:, 0:TH], ALU.mult, ALU.add))
                        mv = c.mark("sv")
                        c.wait_marks("scalar", mv)
                        c.s(lambda e, kc=kc: e.activation(
                            u0k(kc)[:, 3:TH + 3], u0k(kc)[:, 3:TH + 3], ACT.Silu))
                    if taps:
                        c.wait_marks("gpsimd", c.mark("ss"))
                        for kc in range(NKC):
                            c.dma(tap["tap_u"][kc * 128:(kc + 1) * 128,
                                               t0:t0 + TH], u0k(kc)[:, 3:TH + 3],
                                  eng="gpsimd")
                    msu = c.mark("ss")
                    c.wait_marks("tensor", msu)
                    for tt in range(NTT):
                        psl = 4 + tt % 2
                        c.use("tensor", ("ps", psl))
                        for kc in range(NKC):
                            c.t(lambda e, kc=kc, tt=tt, psl=psl: e.matmul(
                                pt[psl][:, 0:TTL],
                                wx_bf[:, kc * 128:(kc + 1) * 128],
                                u0k(kc)[:, 3 + tt * TTL: 3 + (tt + 1) * TTL],
                                start=(kc == 0), stop=(kc == NKC - 1)))
                        mt = c.mark("stn")
                        c.wait_marks("vector", mt)
                        c.v(lambda e, tt=tt, psl=psl: e.tensor_copy(
                            dt_sb[0:64, tt * TTL:(tt + 1) * TTL],
                            pt[psl][0:64, 0:TTL]))
                        c.v(lambda e, tt=tt, psl=psl: e.tensor_scalar_mul(
                            bcB_sb[0:NS, tt * TTL:(tt + 1) * TTL],
                            pt[psl][64:64 + NS, 0:TTL], -1.0))
                        c.v(lambda e, tt=tt, psl=psl: e.tensor_copy(
                            bcC_sb[0:NS, tt * TTL:(tt + 1) * TTL],
                            pt[psl][96:96 + NS, 0:TTL]))
                        c.setg(("ps", psl), "sv")
                    c.wait_marks("sync", c.mark("sv"))
                    c.dma(bc_h[0:NS, t0:t0 + TH], bcB_sb[0:NS, :])
                    c.dma(bc_h[NS:2 * NS, t0:t0 + TH], bcC_sb[0:NS, :])
                    mdt = c.mark("sv")
                    c.wait_marks("tensor", mdt)
                    for jc in range(NKC):
                        for tt in range(NTT):
                            psl = (jc * NTT + tt) % 4
                            c.use("tensor", ("ps", psl))
                            c.t(lambda e, jc=jc, tt=tt, psl=psl: e.matmul(
                                pt[psl][:, 0:TTL],
                                wdt_bf[0:64, jc * 128:(jc + 1) * 128],
                                dt_sb[0:64, tt * TTL:(tt + 1) * TTL],
                                start=True, stop=True))
                            mt = c.mark("stn")
                            c.wait_marks("scalar", mt)
                            c.s(lambda e, jc=jc, tt=tt, psl=psl: e.activation(
                                delta[:, jc * TH + tt * TTL:
                                      jc * TH + (tt + 1) * TTL],
                                pt[psl][:, 0:TTL], ACT.Sigmoid,
                                bias=bdt_sb[:, jc:jc + 1], scale=-1.0))
                            c.setg(("ps", psl), "ss")
                    for jc in range(NKC):
                        c.s(lambda e, jc=jc: e.activation(
                            delta[:, jc * TH:(jc + 1) * TH],
                            delta[:, jc * TH:(jc + 1) * TH], ACT.Ln))
                    c.selfw("scalar")
                    c.s(lambda e: e.nop())
                    if taps:
                        c.wait_marks("gpsimd", c.mark("ss"))
                        for kc in range(NKC):
                            c.dma(tap["tap_delta"][kc * 128:(kc + 1) * 128,
                                                   t0:t0 + TH],
                                  delta[:, kc * TH:(kc + 1) * TH], eng="gpsimd")
                    if cut == 3:
                        mscope.close()
                        raise _Cut()
                    for kc in range(NKC):
                        c.use("sync", ("woutld", kc))
                        c.dma(wout_bf[:, kc * D:(kc + 1) * D],
                              wout_h[kc * 128:(kc + 1) * 128, :])
                    c.lanewait("sync", "dh", "dr0", "dr1")
                    mdl = c.mark("ss")
                    c.wait_marks("vector", mdl)
                    ibb = [0]
                    for kg in range(NKC // 4):
                        ks = [kg * 4 + j for j in range(4)]
                        for j, k in enumerate(ks[:2]):
                            c.use("sync", ("srkg", j % 2))
                            c.dma(srkg[:, (j % 2) * TH:(j % 2 + 1) * TH],
                                  sres_h[k * 128:(k + 1) * 128, t0:t0 + TH])
                        for j, k in enumerate(ks):
                            c.use("vector", ("ypsum", j))
                            c.v(lambda e, j=j, k=k: e.tensor_tensor(
                                gkt[:, j * TH:(j + 1) * TH],
                                delta[:, k * TH:(k + 1) * TH],
                                u0k(k)[:, 3:TH + 3], ALU.mult))
                            b0 = ibb[0] % 3
                            ibb[0] += 1
                            c.use("vector", ("bb", b0))
                            c.v(lambda e, k=k, b0=b0: e.tensor_scalar_mul(
                                bb[:, b0 * TH:(b0 + 1) * TH],
                                u0k(k)[:, 3:TH + 3], dv_sb[:, k:k + 1]))
                            mv = c.mark("sv")
                            c.wait_marks("tensor", mv)
                            for tb in range(YB):
                                c.t(lambda e, j=j, tb=tb, b0=b0: e.matmul(
                                    pt[2 * j + tb][:, 0:min(512, TH)], I_bf[:, :],
                                    bb[:, b0 * TH + tb * 512:
                                       b0 * TH + tb * 512 + min(512, TH)],
                                    start=True, stop=False))
                            c.setg(("bb", b0), "stn")
                        for n in range(NS):
                            bsl = n % 3
                            bln = ("db0", "db1", "ds0")[bsl]
                            c.use("sync", ("bcb", bsl))
                            c.dma(Bb[:, bsl * TH:(bsl + 1) * TH],
                                  bc_h[n:n + 1, t0:t0 + TH]
                                  .broadcast_to((128, TH)), lane=bln)
                            c.dma(Cb[:, bsl * TH:(bsl + 1) * TH],
                                  bc_h[NS + n:NS + n + 1, t0:t0 + TH]
                                  .broadcast_to((128, TH)), lane=bln)
                            for j, k in enumerate(ks):
                                asl = (n * 4 + j) % 3
                                hsl = (n * 4 + j) % 2
                                kn = k * NS + n
                                c.use("scalar", ("ab", asl))
                                c.s(lambda e, k=k, kn=kn, asl=asl: e.activation(
                                    ab[:, asl * TH:(asl + 1) * TH],
                                    delta[:, k * TH:(k + 1) * TH], ACT.Exp,
                                    scale=A_sb[:, kn:kn + 1]))
                                msa = c.mark("ss")
                                b1 = ibb[0] % 3
                                ibb[0] += 1
                                c.use("vector", ("bb", b1))
                                c.lanewait("vector", bln)
                                c.v(lambda e, j=j, bsl=bsl, b1=b1:
                                    e.tensor_tensor(
                                        bb[:, b1 * TH:(b1 + 1) * TH],
                                        gkt[:, j * TH:(j + 1) * TH],
                                        Bb[:, bsl * TH:(bsl + 1) * TH],
                                        ALU.mult))
                                c.wait_marks("vector", msa)
                                c.use("vector", ("hb", hsl))
                                c.v(lambda e, kn=kn, asl=asl, hsl=hsl, b1=b1:
                                    e.tensor_tensor_scan(
                                        hb[:, hsl * TH:(hsl + 1) * TH],
                                        ab[:, asl * TH:(asl + 1) * TH],
                                        bb[:, b1 * TH:(b1 + 1) * TH],
                                        state[:, kn:kn + 1], ALU.mult,
                                        ALU.add))
                                c.setg(("ab", asl), "sv")
                                c.setg(("bb", b1), "sv")
                                b2 = ibb[0] % 3
                                ibb[0] += 1
                                c.use("vector", ("bb", b2))
                                c.v(lambda e, hsl=hsl, bsl=bsl, b2=b2:
                                    e.tensor_tensor(
                                        bb[:, b2 * TH:(b2 + 1) * TH],
                                        hb[:, hsl * TH:(hsl + 1) * TH],
                                        Cb[:, bsl * TH:(bsl + 1) * TH],
                                        ALU.mult))
                                if h == 0:
                                    mv = c.mark("sv")
                                    c.wait_marks("gpsimd", mv)
                                    c.g(lambda e, kn=kn, hsl=hsl: e.tensor_copy(
                                        state[:, kn:kn + 1],
                                        hb[:, hsl * TH + TH - 1: hsl * TH + TH]))
                                    c.setg(("hb", hsl), "sv", "sg")
                                else:
                                    c.setg(("hb", hsl), "sv")
                                mv = c.mark("sv")
                                c.wait_marks("tensor", mv)
                                for tb in range(YB):
                                    c.t(lambda e, j=j, tb=tb, n=n, b2=b2:
                                        e.matmul(
                                            pt[2 * j + tb][:, 0:min(512, TH)],
                                            I_bf[:, :],
                                            bb[:, b2 * TH + tb * 512:
                                               b2 * TH + tb * 512 +
                                               min(512, TH)],
                                            start=False, stop=(n == NS - 1)))
                                c.setg(("bb", b2), "stn")
                            c.setg(("bcb", bsl), "sv")
                        mt = c.mark("stn")
                        c.wait_marks("vector", mt)
                        for j, k in enumerate(ks):
                            c.wait("vector", "dh")
                            for tb in range(YB):
                                w = min(512, TH)
                                c.v(lambda e, j=j, k=k, tb=tb, w=w: e.tensor_tensor(
                                    delta[:, k * TH + tb * 512:
                                          k * TH + tb * 512 + w],
                                    pt[2 * j + tb][:, 0:w],
                                    srkg[:, (j % 2) * TH + tb * 512:
                                         (j % 2) * TH + tb * 512 + w], ALU.mult))
                            c.setg(("ypsum", j), "sv")
                            c.setg(("srkg", j % 2), "sv")
                            if j + 2 < 4:
                                c.use("sync", ("srkg", j % 2))
                                c.dma(srkg[:, (j % 2) * TH:(j % 2 + 1) * TH],
                                      sres_h[ks[j + 2] * 128:
                                             (ks[j + 2] + 1) * 128, t0:t0 + TH])
                    if taps:
                        c.wait_marks("gpsimd", c.mark("sv"))
                        for kc in range(NKC):
                            c.dma(tap["tap_ymod"][kc * 128:(kc + 1) * 128,
                                                  t0:t0 + TH],
                                  delta[:, kc * TH:(kc + 1) * TH], eng="gpsimd")
                    if cut == 4:
                        mscope.close()
                        raise _Cut()
                    mym = c.mark("sv")
                    c.wait_marks("tensor", mym)
                    c.lanewait("tensor", "dh")
                    for kc in range(NKC):
                        c.setg(("woutld", kc), "stn")
                    for tcg in range(NTC // LNB):
                        for tc2 in range(LNB):
                            tc = tcg * LNB + tc2
                            c.use("sync", ("xbuf", tc2))
                            c.dma(xbuf[:, tc2 * D:(tc2 + 1) * D],
                                  xT[t0 + tc * 128: t0 + (tc + 1) * 128, :],
                                  lane="dx%d" % tc2)
                            c.lanewait("vector", "dx%d" % tc2)
                            c.use("vector", "xb2")
                            c.v(lambda e, tc2=tc2: e.scalar_tensor_tensor(
                                xb2[:, :], xbuf[:, tc2 * D:(tc2 + 1) * D],
                                2.0, bout_b[:, :], ALU.mult, ALU.add))
                            for hc in range(2):
                                psl = (tc * 2 + hc) % 4
                                c.use("tensor", ("ps", psl))
                                for kc in range(NKC):
                                    c.t(lambda e, kc=kc, tc=tc, hc=hc, psl=psl:
                                        e.matmul(
                                            pt[psl][:, 0:512],
                                            delta[:, kc * TH + tc * 128:
                                                  kc * TH + (tc + 1) * 128],
                                            wout_bf[:, kc * D + hc * 512:
                                                    kc * D + (hc + 1) * 512],
                                            start=(kc == 0),
                                            stop=(kc == NKC - 1)))
                                mt = c.mark("stn")
                                c.wait_marks("vector", mt)
                                c.v(lambda e, tc2=tc2, hc=hc, psl=psl:
                                    e.tensor_tensor(
                                        xbuf[:, tc2 * D + hc * 512:
                                             tc2 * D + hc * 512 + 512],
                                        pt[psl][:, 0:512],
                                        xb2[:, hc * 512: hc * 512 + 512],
                                        ALU.add))
                                c.setg(("ps", psl), "sv")
                            c.setg("xb2", "sv")
                            c.selfw("vector")
                            c.v(lambda e, tc2=tc2: e.tensor_reduce(
                                stat[:, tc2:tc2 + 1],
                                xbuf[:, tc2 * D:(tc2 + 1) * D], AX.X, ALU.add))
                            mv = c.mark("sv")
                            c.wait_marks("scalar", mv)
                            c.s(lambda e, tc2=tc2: e.activation(
                                sqd[:, :], xbuf[:, tc2 * D:(tc2 + 1) * D],
                                ACT.Square, accum_out=stat[:, 8 + tc2: 9 + tc2]))
                        ms = c.mark("ss")
                        c.wait_marks("vector", ms)
                        ln_rstd(LNB)
                        for tc2 in range(LNB):
                            tc = tcg * LNB + tc2
                            c.use("vector", "x1t")
                            c.use("vector", ("xnp", 0))
                            c.v(lambda e, tc2=tc2: e.tensor_scalar(
                                xnp[:, :], xbuf[:, tc2 * D:(tc2 + 1) * D],
                                stat[:, 16 + tc2:17 + tc2],
                                stat[:, 24 + tc2:25 + tc2],
                                ALU.subtract, ALU.mult))
                            c.setg(("xbuf", tc2), "sv")
                            c.v(lambda e, tc=tc, h=h: e.tensor_reduce(
                                mxu[:, h * NTC + tc: h * NTC + tc + 1],
                                xnp[:, :], AX.X, ALU.max,
                                apply_absolute_value=True))
                            mv = c.mark("sv")
                            c.wait_marks("scalar", mv)
                            c.dma(x1_h[t0 + tc * 128: t0 + (tc + 1) * 128, :],
                                  xnp[:, :], eng="scalar", lane="dst")
                            if taps:
                                c.dma(tap["tap_x1"][t0 + tc * 128:
                                                    t0 + (tc + 1) * 128, :],
                                      xnp[:, :], eng="scalar", lane="dst")
                            c.guards["x1t"] = {"dst": c.cnt["dst"]}
                    c.barrier()

                if cut == 5:
                    mscope.close()
                    raise _Cut()
                reduce_to_all(mxu[:, 0:2 * NTC], 0)
                collective(0, 1)
                c.v(lambda e: e.tensor_scalar(scb[:, 0:1], scg[:, 0:1],
                                              1.0 / QMAX, 1e-12, ALU.mult,
                                              ALU.max))
                c.selfw("vector")
                c.v(lambda e: e.reciprocal(scb[:, 1:2], scb[:, 0:1]))
                c.selfw("vector")
            c.barrier()
            if cut == 55:
                raise _Cut()

            fscope = ExitStack()
            with fscope:
                _e = fscope.enter_context
                xq = _e(nc.sbuf_tensor("xq", [128, NFC * L], BF16))
                xr = _e(nc.sbuf_tensor("xr", [128, 2 * D], F32))
                xqp = _e(nc.sbuf_tensor("xqp", [128, 2 * D], F32))
                fwt = _e(nc.sbuf_tensor("fwt", [128, 4 * 1024], BF16))
                stg = _e(nc.sbuf_tensor("stg", [128, 8 * 512], F32))

                for tc in range(L // 128):
                    xsl = tc % 2
                    c.use("sync", ("xr", xsl))
                    c.dma(xr[:, xsl * D:(xsl + 1) * D],
                          x1_h[tc * 128:(tc + 1) * 128, :],
                          lane="dx%d" % xsl)
                    c.lanewait("vector", "dx%d" % xsl)
                    c.use("vector", ("xqp", xsl))
                    c.v(lambda e, xsl=xsl: e.tensor_scalar(
                        xr[:, xsl * D:(xsl + 1) * D],
                        xr[:, xsl * D:(xsl + 1) * D],
                        scb[:, 1:2], MAGIC, ALU.mult, ALU.add))
                    c.selfw("vector")
                    c.v(lambda e, xsl=xsl: e.tensor_scalar(
                        xqp[:, xsl * D:(xsl + 1) * D],
                        xr[:, xsl * D:(xsl + 1) * D],
                        MAGIC, scb[:, 0:1], ALU.subtract, ALU.mult))
                    c.setg(("xr", xsl), "sv")
                    mv = c.mark("sv")
                    for fc in range(NFC):
                        psl = 6 + fc % 2
                        c.wait_marks("tensor", mv)
                        c.use("tensor", ("ps", psl))
                        c.t(lambda e, fc=fc, xsl=xsl, psl=psl: e.transpose(
                            pt[psl][:, 0:128],
                            xqp[:, xsl * D + fc * 128: xsl * D + (fc + 1) * 128],
                            I_f32[:, :]))
                        mt = c.mark("stn")
                        c.wait_marks("scalar", mt)
                        c.s(lambda e, fc=fc, tc=tc, psl=psl: e.activation(
                            xq[:, fc * L + tc * 128: fc * L + (tc + 1) * 128],
                            pt[psl][:, 0:128], ACT.Copy))
                        c.setg(("ps", psl), "ss")
                    c.setg(("xqp", xsl), "stn")

                if taps:
                    c.wait_marks("gpsimd", c.mark("ss"))
                    c.dma(xq_h[:, :], xq[:, :], eng="gpsimd")
                if cut == 6:
                    fscope.close()
                    raise _Cut()
                mxq = c.mark("ss")
                spc = [0]
                for ic in range(NIC):
                    wsl = ic % 2
                    c.use("sync", ("fwt", wsl))
                    c.dma(fwt[:, wsl * 1024: wsl * 1024 + 1024]
                          .rearrange("p (fc i) -> p fc i", fc=NFC),
                          wupT_h[:, ic * 128:(ic + 1) * 128]
                          .rearrange("(fc p) i -> p fc i", p=128),
                          lane="dw%d" % wsl)
                    c.dma(fwt[:, (2 + wsl) * 1024:
                              (2 + wsl) * 1024 + 1024]
                          .rearrange("p (fc i) -> p fc i", fc=NFC),
                          wgtT_h[:, ic * 128:(ic + 1) * 128]
                          .rearrange("(fc p) i -> p fc i", p=128),
                          lane="dw%d" % wsl)
                    for tt in range(NT2):
                        for gi, (spill, mx) in enumerate(
                                ((sp_up, mxu), (sp_gate, mxg))):
                            psl = (tt * 2 + gi) % 4
                            c.use("tensor", ("ps", psl))
                            c.lanewait("tensor", "dw%d" % wsl)
                            c.wait_marks("tensor", mxq)
                            for fc in range(NFC):
                                c.t(lambda e, fc=fc, tt=tt, gi=gi, wsl=wsl,
                                    psl=psl: e.matmul(
                                        pt[psl][:, 0:TL2],
                                        fwt[:, (2 * gi + wsl) * 1024 + fc * 128:
                                            (2 * gi + wsl) * 1024 + (fc + 1) * 128],
                                        xq[:, fc * L + tt * TL2:
                                           fc * L + (tt + 1) * TL2],
                                        start=(fc == 0), stop=(fc == NFC - 1)))
                            mt = c.mark("stn")
                            ssl = (ic * NT2 * 2 + tt * 2 + gi) % 4
                            c.wait_marks("scalar", mt)
                            c.use("scalar", ("stg", ssl))
                            c.s(lambda e, psl=psl, ssl=ssl: e.activation(
                                stg[:, ssl * 512: ssl * 512 + TL2],
                                pt[psl][:, 0:TL2], ACT.Copy))
                            c.setg(("ps", psl), "ss")
                            c.selfw("scalar")
                            c.s(lambda e: e.nop())
                            ms = c.mark("ss")
                            c.wait_marks("vector", ms)
                            c.v(lambda e, ic=ic, tt=tt, ssl=ssl, mx=mx:
                                e.tensor_reduce(
                                    mx[:, ic * NT2 + tt: ic * NT2 + tt + 1],
                                    stg[:, ssl * 512: ssl * 512 + TL2], AX.X,
                                    ALU.max, apply_absolute_value=True))
                            lnp = ("dp0", "dp1", "dt0", "dt1", "ds0", "ds1",
                                   "du0", "du1")[spc[0] % 8]
                            spc[0] += 1
                            c.dma(spill[ic * 128:(ic + 1) * 128,
                                        tt * TL2:(tt + 1) * TL2],
                                  stg[:, ssl * 512: ssl * 512 + TL2],
                                  eng="scalar", lane=lnp)
                            c.guards[("stg", ssl)] = {lnp: c.cnt[lnp],
                                                      "sv": c.cnt["sv"]}
                    c.setg(("fwt", wsl), "stn")

                reduce_to_all(mxu[:, :], 0)
                reduce_to_all(mxg[:, :], 1)
                collective(1, 2)
                c.v(lambda e: e.tensor_scalar(scb[:, 2:3], scg[:, 0:1],
                                              1.0 / QMAX, 1e-12, ALU.mult,
                                              ALU.max))
                c.v(lambda e: e.tensor_scalar(scb[:, 4:5], scg[:, 1:2],
                                              1.0 / QMAX, 1e-12, ALU.mult,
                                              ALU.max))
                c.selfw("vector")
                c.v(lambda e: e.reciprocal(scb[:, 3:4], scb[:, 2:3]))
                c.v(lambda e: e.reciprocal(scb[:, 5:6], scb[:, 4:5]))
                c.selfw("vector")
                mv = c.mark("sv")
                c.wait_marks("scalar", mv)
                c.wait_marks("gpsimd", mv)

                fscope.close()
                f34 = ExitStack()
                _e34 = f34.enter_context
                gld = _e34(nc.sbuf_tensor("gld", [128, 4 * L], F32))
                uld = _e34(nc.sbuf_tensor("uld", [128, 4 * L], F32))
                g2o = _e34(nc.sbuf_tensor("g2o", [128, 4 * L], F32))
                gint = _e34(nc.sbuf_tensor("gint", [128, 2 * L], F32))
                sig = _e34(nc.sbuf_tensor("sig", [128, 2 * L], F32))
                ldlanes = ("ds0", "ds1", "dr0", "dr1")
                uplanes = ("du0", "du1", "db0", "db1")
                splanes = ("dp0", "dp1", "dt0", "dt1")

                c.lanewait("sync", "dp0", "dp1", "dt0", "dt1",
                           "ds0", "ds1", "du0", "du1")
                for ic in range(NIC):
                    s4 = ic % 4
                    s2_ = ic % 2
                    c.use("sync", ("gld", s4))
                    c.dma(gld[:, s4 * L:(s4 + 1) * L],
                          sp_gate[ic * 128:(ic + 1) * 128, :],
                          lane=ldlanes[s4])
                    c.lanewait("vector", ldlanes[s4])
                    c.use("vector", ("gint", s2_))
                    c.v(lambda e, s4=s4, s2_=s2_: e.tensor_scalar(
                        gint[:, s2_ * L:(s2_ + 1) * L],
                        gld[:, s4 * L:(s4 + 1) * L],
                        scb[:, 5:6], MAGIC, ALU.mult, ALU.add))
                    c.setg(("gld", s4), "sv")
                    c.selfw("vector")
                    c.v(lambda e, s2_=s2_: e.tensor_single_scalar(
                        gint[:, s2_ * L:(s2_ + 1) * L],
                        gint[:, s2_ * L:(s2_ + 1) * L], MAGIC,
                        ALU.subtract))
                    mv = c.mark("sv")
                    c.wait_marks("scalar", mv)
                    c.use("scalar", ("sig", s2_))
                    c.s(lambda e, s2_=s2_: e.activation(
                        sig[:, s2_ * L:(s2_ + 1) * L],
                        gint[:, s2_ * L:(s2_ + 1) * L], ACT.Sigmoid,
                        scale=scb[:, 4:5]))
                    c.selfw("scalar")
                    c.s(lambda e: e.nop())
                    ms = c.mark("ss")
                    c.setg(("gint", s2_), "ss")
                    c.wait_marks("vector", ms)
                    c.v(lambda e, s2_=s2_: e.tensor_scalar(
                        sig[:, s2_ * L:(s2_ + 1) * L],
                        sig[:, s2_ * L:(s2_ + 1) * L],
                        65536.0, MAGIC, ALU.mult, ALU.add))
                    c.selfw("vector")
                    c.v(lambda e, s2_=s2_: e.tensor_scalar(
                        sig[:, s2_ * L:(s2_ + 1) * L],
                        sig[:, s2_ * L:(s2_ + 1) * L],
                        MAGIC, 65535.0, ALU.subtract, ALU.min))
                    c.selfw("vector")
                    c.use("vector", ("g2o", s4))
                    c.v(lambda e, s4=s4, s2_=s2_: e.tensor_tensor(
                        g2o[:, s4 * L:(s4 + 1) * L],
                        gint[:, s2_ * L:(s2_ + 1) * L],
                        sig[:, s2_ * L:(s2_ + 1) * L], ALU.mult))
                    c.setg(("sig", s2_), "sv")
                    c.selfw("vector")
                    c.v(lambda e, ic=ic, s4=s4: e.tensor_reduce(
                        mx4[:, ic:ic + 1],
                        g2o[:, s4 * L:(s4 + 1) * L], AX.X,
                        ALU.max, apply_absolute_value=True))
                    mv = c.mark("sv")
                    c.wait_marks("scalar", mv)
                    c.dma(sp_g2[ic * 128:(ic + 1) * 128, :],
                          g2o[:, s4 * L:(s4 + 1) * L], eng="scalar",
                          lane=splanes[s4])
                    c.guards[("g2o", s4)] = {splanes[s4]: c.cnt[splanes[s4]],
                                             "sv": c.cnt["sv"]}

                c.wait_marks("vector", c.mark("sg"))
                reduce_to_all(mx4[:, 0:NIC], 0)
                c.v(lambda e: e.tensor_tensor(sca[:, 0:1], sca[:, 0:1],
                                              scb[:, 4:5], ALU.mult))
                c.selfw("vector")
                c.v(lambda e: e.tensor_scalar_mul(sca[:, 0:1], sca[:, 0:1],
                                                  1.0 / 65536.0))
                c.selfw("vector")
                collective(2, 1)
                c.v(lambda e: e.tensor_scalar(sca[:, 2:3], scg[:, 0:1],
                                              1.0 / QMAX, 1e-12, ALU.mult,
                                              ALU.max))
                c.selfw("vector")
                c.v(lambda e: e.reciprocal(sca[:, 3:4], sca[:, 2:3]))
                c.selfw("vector")
                c.v(lambda e: e.tensor_tensor(scb[:, 6:7], scb[:, 4:5],
                                              sca[:, 3:4], ALU.mult))
                c.selfw("vector")
                c.v(lambda e: e.tensor_scalar_mul(scb[:, 6:7], scb[:, 6:7],
                                                  1.0 / 65536.0))
                c.selfw("vector")
                mv = c.mark("sv")
                c.wait_marks("gpsimd", mv)

                c.lanewait("sync", "dp0", "dp1", "dt0", "dt1")
                for ic in range(NIC):
                    s4 = ic % 4
                    s2_ = ic % 2
                    c.use("sync", ("gld", s4))
                    c.dma(gld[:, s4 * L:(s4 + 1) * L],
                          sp_g2[ic * 128:(ic + 1) * 128, :],
                          lane=ldlanes[s4])
                    c.use("sync", ("uld", s4))
                    c.dma(uld[:, s4 * L:(s4 + 1) * L],
                          sp_up[ic * 128:(ic + 1) * 128, :],
                          lane=uplanes[s4])
                    c.lanewait("vector", ldlanes[s4])
                    c.use("vector", ("gint", s2_))
                    c.v(lambda e, s4=s4, s2_=s2_: e.tensor_scalar(
                        gint[:, s2_ * L:(s2_ + 1) * L],
                        gld[:, s4 * L:(s4 + 1) * L],
                        scb[:, 6:7], MAGIC, ALU.mult, ALU.add))
                    c.setg(("gld", s4), "sv")
                    c.lanewait("vector", uplanes[s4])
                    c.use("vector", ("sig", s2_))
                    c.v(lambda e, s4=s4, s2_=s2_: e.tensor_scalar(
                        sig[:, s2_ * L:(s2_ + 1) * L],
                        uld[:, s4 * L:(s4 + 1) * L],
                        scb[:, 3:4], MAGIC, ALU.mult, ALU.add))
                    c.setg(("uld", s4), "sv")
                    c.selfw("vector")
                    c.v(lambda e, s2_=s2_: e.tensor_single_scalar(
                        gint[:, s2_ * L:(s2_ + 1) * L],
                        gint[:, s2_ * L:(s2_ + 1) * L], MAGIC,
                        ALU.subtract))
                    c.v(lambda e, s2_=s2_: e.tensor_single_scalar(
                        sig[:, s2_ * L:(s2_ + 1) * L],
                        sig[:, s2_ * L:(s2_ + 1) * L], MAGIC,
                        ALU.subtract))
                    c.selfw("vector")
                    c.use("vector", ("g2o", s4))
                    c.v(lambda e, s4=s4, s2_=s2_: e.tensor_tensor(
                        g2o[:, s4 * L:(s4 + 1) * L],
                        gint[:, s2_ * L:(s2_ + 1) * L],
                        sig[:, s2_ * L:(s2_ + 1) * L], ALU.mult))
                    c.setg(("gint", s2_), "sv")
                    c.setg(("sig", s2_), "sv")
                    c.selfw("vector")
                    c.v(lambda e, ic=ic, s4=s4: e.tensor_reduce(
                        mx5[:, ic:ic + 1],
                        g2o[:, s4 * L:(s4 + 1) * L], AX.X,
                        ALU.max, apply_absolute_value=True))
                    mv = c.mark("sv")
                    c.wait_marks("scalar", mv)
                    c.dma(sp_o[ic * 128:(ic + 1) * 128, :],
                          g2o[:, s4 * L:(s4 + 1) * L], eng="scalar",
                          lane=splanes[s4])
                    c.guards[("g2o", s4)] = {splanes[s4]: c.cnt[splanes[s4]],
                                             "sv": c.cnt["sv"]}

                c.wait_marks("vector", c.mark("sg"))
                reduce_to_all(mx5[:, 0:NIC], 0)
                c.v(lambda e: e.tensor_tensor(sca[:, 0:1], sca[:, 0:1],
                                              sca[:, 2:3], ALU.mult))
                c.selfw("vector")
                c.v(lambda e: e.tensor_tensor(sca[:, 0:1], sca[:, 0:1],
                                              scb[:, 2:3], ALU.mult))
                c.selfw("vector")
                collective(3, 1)
                c.v(lambda e: e.tensor_scalar(scb[:, 8:9], scg[:, 0:1],
                                              1.0 / QMAX, 1e-12, ALU.mult,
                                              ALU.max))
                c.selfw("vector")
                c.v(lambda e: e.reciprocal(sca[:, 4:5], scb[:, 8:9]))
                c.selfw("vector")
                c.v(lambda e: e.tensor_tensor(scb[:, 7:8], sca[:, 2:3],
                                              scb[:, 2:3], ALU.mult))
                c.selfw("vector")
                c.v(lambda e: e.tensor_tensor(scb[:, 7:8], scb[:, 7:8],
                                              sca[:, 4:5], ALU.mult))
                c.selfw("vector")
                if taps:
                    c.wait_marks("sync", c.mark("sv"))
                    c.dma(scb_h[:, :], scb[:, :])
                f34.close()
            c.barrier()

            if cut == 9:
                raise _Cut()
            f5scope = ExitStack()
            with f5scope:
                _e = f5scope.enter_context
                wdn = _e(nc.sbuf_tensor("wdn", [128, NIC * D], BF16))
                oqt = _e(nc.sbuf_tensor("oqt", [128, 2 * NIC * TL2], BF16))
                ot = _e(nc.sbuf_tensor("ot", [128, 4 * 512], F32))
                pre3 = _e(nc.sbuf_tensor("pre3", [128, (TL2 // 128) * D], F32))
                x1r = _e(nc.sbuf_tensor("x1r", [128, 2 * D], F32))
                outt = _e(nc.sbuf_tensor("outt", [128, 2 * D], F32))
                c.lanewait("sync", "dp0", "dp1", "dt0", "dt1")
                for ic in range(NIC):
                    c.dma(wdn[:, ic * D:(ic + 1) * D],
                          wdnT_h[ic * 128:(ic + 1) * 128, :])
                f5lanes = ("ds0", "ds1", "du0", "du1")
                for tt in range(NT2):
                    oqb = (tt % 2) * NIC * TL2
                    c.use("vector", ("oqt", tt % 2))
                    for ic in range(NIC):
                        osl = ic % 4
                        c.use("sync", ("ot", osl))
                        c.dma(ot[:, osl * 512: osl * 512 + TL2],
                              sp_o[ic * 128:(ic + 1) * 128,
                                   tt * TL2:(tt + 1) * TL2],
                              lane=f5lanes[osl])
                        c.lanewait("vector", f5lanes[osl])
                        c.v(lambda e, osl=osl: e.tensor_scalar(
                            ot[:, osl * 512: osl * 512 + TL2],
                            ot[:, osl * 512: osl * 512 + TL2],
                            scb[:, 7:8], MAGIC, ALU.mult, ALU.add))
                        c.selfw("vector")
                        c.v(lambda e, ic=ic, osl=osl, oqb=oqb: e.tensor_scalar(
                            oqt[:, oqb + ic * TL2: oqb + (ic + 1) * TL2],
                            ot[:, osl * 512: osl * 512 + TL2],
                            MAGIC, scb[:, 8:9], ALU.subtract, ALU.mult))
                        c.setg(("ot", osl), "sv")
                    moq = c.mark("sv")
                    c.wait_marks("tensor", moq)
                    for tcl in range(TL2 // 128):
                        tc = tt * (TL2 // 128) + tcl
                        xsl = tc % 2
                        c.use("sync", ("x1r", xsl))
                        c.dma(x1r[:, xsl * D:(xsl + 1) * D],
                              x1_h[tc * 128:(tc + 1) * 128, :],
                              lane="dx%d" % xsl)
                        for hc in range(2):
                            psl = (tcl * 2 + hc) % 4
                            c.use("tensor", ("ps", psl))
                            if tt == 0 and tcl == 0 and hc == 0:
                                c.lanewait("tensor", "dh")
                            for ic in range(NIC):
                                c.t(lambda e, ic=ic, tcl=tcl, hc=hc, psl=psl,
                                    oqb=oqb: e.matmul(
                                        pt[psl][:, 0:512],
                                        oqt[:, oqb + ic * TL2 + tcl * 128:
                                            oqb + ic * TL2 + (tcl + 1) * 128],
                                        wdn[:, ic * D + hc * 512:
                                            ic * D + (hc + 1) * 512],
                                        start=(ic == 0), stop=(ic == NIC - 1)))
                            mt = c.mark("stn")
                            c.wait_marks("vector", mt)
                            c.lanewait("vector", "dx%d" % xsl)
                            c.v(lambda e, tcl=tcl, hc=hc, psl=psl, xsl=xsl:
                                e.tensor_tensor(
                                    pre3[:, tcl * D + hc * 512:
                                         tcl * D + hc * 512 + 512],
                                    pt[psl][:, 0:512],
                                    x1r[:, xsl * D + hc * 512:
                                        xsl * D + hc * 512 + 512], ALU.add))
                            c.setg(("ps", psl), "sv")
                        c.setg(("x1r", xsl), "sv")
                        c.selfw("vector")
                        c.v(lambda e, tcl=tcl: e.tensor_reduce(
                            stat[:, tcl:tcl + 1], pre3[:, tcl * D:(tcl + 1) * D],
                            AX.X, ALU.add))
                        mv = c.mark("sv")
                        c.wait_marks("scalar", mv)
                        c.s(lambda e, tcl=tcl: e.activation(
                            sqd[:, :], pre3[:, tcl * D:(tcl + 1) * D], ACT.Square,
                            accum_out=stat[:, 8 + tcl: 9 + tcl]))
                    ms = c.mark("ss")
                    c.wait_marks("vector", ms)
                    ln_rstd(TL2 // 128)
                    for tcl in range(TL2 // 128):
                        tc = tt * (TL2 // 128) + tcl
                        osl = tcl % 2
                        c.use("vector", ("outt", osl))
                        c.v(lambda e, tcl=tcl, osl=osl: e.tensor_scalar(
                            outt[:, osl * D:(osl + 1) * D],
                            pre3[:, tcl * D:(tcl + 1) * D],
                            stat[:, 16 + tcl:17 + tcl],
                            stat[:, 24 + tcl:25 + tcl],
                            ALU.subtract, ALU.mult))
                        mv = c.mark("sv")
                        c.wait_marks("scalar", mv)
                        c.dma(outT[tc * 128:(tc + 1) * 128, :],
                              outt[:, osl * D:(osl + 1) * D], eng="scalar",
                              lane="dst")
                        c.guards[("outt", osl)] = {"dst": c.cnt["dst"]}
                    c.setg(("oqt", tt % 2), "stn")

        except _Cut:
            pass
        if cut < 99:
            c.barrier()
            for tc in range(L // 128):
                c.dma(outT[tc * 128:(tc + 1) * 128, :], sqd[:, :])

        c.barrier()
        for ln in c.dma_lanes:
            c.wait("sync", ln)
        c.wait("gpsimd", "dg")

        with nc.Block() as blk:
            for eng in ENGINES:
                ops = c.q[eng]

                def make(ops):
                    def body(e):
                        for f in ops:
                            f(e)
                    return body
                getattr(blk, eng)(make(ops))

    return nc


_NC_CACHE = {}


def _get_nc(L, n_cores, taps=False, cut=99):
    key = (L, n_cores, taps, cut)
    if key not in _NC_CACHE:
        _NC_CACHE[key] = build_nc(L, n_cores, taps, cut=cut)
    return _NC_CACHE[key]


def make_in_maps(inputs, n_cores=8):
    f = lambda k: np.ascontiguousarray(np.asarray(inputs[k], dtype=np.float32))
    x = f("L0_attn_x")
    base = {
        "ln_g": f("L0_attn_ln_gamma"), "ln_b": f("L0_attn_ln_beta"),
        "w_in": f("L0_attn_W_in"), "b_in": f("L0_attn_b_in"),
        "conv_w": f("L0_attn_conv_w"), "conv_b": f("L0_attn_conv_b"),
        "w_x": f("L0_attn_W_x"), "w_dt": f("L0_attn_W_dt"),
        "b_dt": f("L0_attn_b_dt"), "a_log": f("L0_attn_A_log"),
        "d_vec": f("L0_attn_D"), "w_out": f("L0_attn_W_out"),
        "b_out": f("L0_attn_b_out"), "w_gate": f("L0_ffn_w_gate"),
        "w_up": f("L0_ffn_w_up"), "w_down": f("L0_ffn_w_down"),
    }
    return [dict(base, x=np.ascontiguousarray(x[i])) for i in range(n_cores)]


def kernel(**inputs):
    n_cores = 8
    in_maps = make_in_maps(inputs, n_cores)
    nc = _get_nc(in_maps[0]["x"].shape[0], n_cores)
    res = run_bass_kernel_spmd(nc, in_maps, core_ids=list(range(n_cores)))
    return np.stack([res.results[i]["out"] for i in range(n_cores)], axis=0)


# version 21
# speedup vs baseline: 1.1801x; 1.0045x over previous
from contextlib import ExitStack

import numpy as np

import concourse.bass as bass
import concourse.mybir as mybir
from concourse.bass_utils import run_bass_kernel_spmd

F32 = mybir.dt.float32
BF16 = mybir.dt.bfloat16
ALU = mybir.AluOpType
ACT = mybir.ActivationFunctionType
AX = mybir.AxisListType

MAGIC = float(np.float32(1.5 * 2**23))
QMAX = 32767.0
EPS = 1e-5

D, DI, NS, RK, KC, HI = 1024, 2048, 16, 64, 4, 3072
NKC, NFC, NIC = DI // 128, D // 128, HI // 128

ENGINES = ("sync", "vector", "scalar", "tensor", "gpsimd")
DBG = {"no_collective": False, "no_r2s": False, "no_bcast": False}


class Ctx:

    def __init__(self, nc, stack):
        self.nc = nc
        self.q = {e: [] for e in ENGINES}
        self.sem = {}
        self.cnt = {}
        self.dma_lanes = ("dh", "dx0", "dx1", "dw0", "dw1", "db0", "db1",
                          "ds0", "ds1", "du0", "du1", "dr0", "dr1",
                          "dt0", "dt1", "dst", "dp0", "dp1", "dg")
        for name in self.dma_lanes + ("sv", "ss", "stn", "sg", "cc"):
            self.sem[name] = stack.enter_context(nc.semaphore(name))
            self.cnt[name] = 0
        self.guards = {}
        self.eng_sem = {"vector": "sv", "scalar": "ss", "tensor": "stn",
                        "gpsimd": "sg"}

    def wait(self, eng, name, val=None):
        if val is None:
            val = self.cnt[name]
        if val <= 0:
            return
        h = self.sem[name]
        self.q[eng].append(lambda e, h=h, v=val: e.wait_ge(h, v))

    def run(self, eng, fn, inc=None):
        if inc is None:
            self.q[eng].append(fn)
            return None
        step = 16 if inc in self.dma_lanes else 1
        self.cnt[inc] += step
        h = self.sem[inc]
        self.q[eng].append(lambda e, fn=fn, h=h, s=step: fn(e).then_inc(h, s))
        return self.cnt[inc]

    def v(self, fn):
        return self.run("vector", fn, "sv")

    def s(self, fn):
        return self.run("scalar", fn, "ss")

    def t(self, fn):
        return self.run("tensor", fn, "stn")

    def g(self, fn):
        return self.run("gpsimd", fn, "sg")

    def dma(self, out, in_, eng="sync", slow=False, lane=None):
        sem = lane or ("dg" if eng == "gpsimd" else "dh")
        return self.run(eng, lambda e, o=out, i=in_, s=slow: e.dma_start(
            out=o, in_=i, allow_slow_non_contiguous=s), sem)

    def mark(self, *names):
        if not names:
            names = ("sv", "ss", "stn", "sg", "dh", "dg")
        return {n: self.cnt[n] for n in names}

    def wait_marks(self, eng, marks):
        for name, val in marks.items():
            if val > 0:
                self.wait(eng, name, val)

    def use(self, eng, key):
        m = self.guards.get(key)
        if m:
            self.wait_marks(eng, m)

    def setg(self, key, *names):
        self.guards[key] = self.mark(*names)

    def selfw(self, eng):
        self.wait(eng, self.eng_sem[eng])

    def lanewait(self, eng, *lanes):
        for ln in lanes:
            self.wait(eng, ln)

    def barrier(self):
        for eng, sname in (("vector", "sv"), ("scalar", "ss"),
                           ("tensor", "stn"), ("gpsimd", "sg")):
            self.run(eng, lambda e: e.nop(), sname)
        m = {n: self.cnt[n] for n in
             ("sv", "ss", "stn", "sg") + self.dma_lanes}
        for eng in ENGINES:
            self.wait_marks(eng, m)


def build_nc(L=2048, n_cores=8, taps=False, cut=99):
    TH = L // 2
    NTC = TH // 128
    TTL = min(512, TH)
    NTT = TH // TTL
    TL2 = min(512, L)
    NT2 = L // TL2
    YB = max(1, TH // 512)
    LNB = min(2, NTC)

    nc = bass.Bass("TRN2", target_bir_lowering=False,
                   detect_race_conditions=False)

    xT = nc.dram_tensor("x", [L, D], F32, kind="ExternalInput").ap()
    g_ln = nc.dram_tensor("ln_g", [D], F32, kind="ExternalInput").ap()
    b_ln = nc.dram_tensor("ln_b", [D], F32, kind="ExternalInput").ap()
    w_in = nc.dram_tensor("w_in", [D, 2 * DI], F32, kind="ExternalInput").ap()
    b_in = nc.dram_tensor("b_in", [2 * DI], F32, kind="ExternalInput").ap()
    conv_w = nc.dram_tensor("conv_w", [KC, DI], F32, kind="ExternalInput").ap()
    conv_b = nc.dram_tensor("conv_b", [DI], F32, kind="ExternalInput").ap()
    w_x = nc.dram_tensor("w_x", [DI, RK + 2 * NS], F32, kind="ExternalInput").ap()
    w_dt = nc.dram_tensor("w_dt", [RK, DI], F32, kind="ExternalInput").ap()
    b_dt = nc.dram_tensor("b_dt", [DI], F32, kind="ExternalInput").ap()
    a_log = nc.dram_tensor("a_log", [DI, NS], F32, kind="ExternalInput").ap()
    d_vec = nc.dram_tensor("d_vec", [DI], F32, kind="ExternalInput").ap()
    w_out = nc.dram_tensor("w_out", [DI, D], F32, kind="ExternalInput").ap()
    b_out = nc.dram_tensor("b_out", [D], F32, kind="ExternalInput").ap()
    w_gate = nc.dram_tensor("w_gate", [HI, D], F32, kind="ExternalInput").ap()
    w_up = nc.dram_tensor("w_up", [HI, D], F32, kind="ExternalInput").ap()
    w_down = nc.dram_tensor("w_down", [D, HI], F32, kind="ExternalInput").ap()
    outT = nc.dram_tensor("out", [L, D], F32, kind="ExternalOutput").ap()

    wkind = "ExternalOutput" if taps else "Internal"
    win_h = nc.dram_tensor("win_h", [D, 2 * DI], BF16).ap()
    wout_h = nc.dram_tensor("wout_h", [DI, D], BF16).ap()
    wupT_h = nc.dram_tensor("wupT_h", [D, HI], BF16, kind=wkind).ap()
    wgtT_h = nc.dram_tensor("wgtT_h", [D, HI], BF16, kind=wkind).ap()
    wdnT_h = nc.dram_tensor("wdnT_h", [HI, D], BF16, kind=wkind).ap()
    sres_h = nc.dram_tensor("sres_h", [DI, L], BF16).ap()
    spkind = "ExternalOutput" if taps else "Internal"
    sp_up = nc.dram_tensor("sp_up", [HI, L], F32, kind=spkind).ap()
    sp_gate = nc.dram_tensor("sp_gate", [HI, L], F32, kind=spkind).ap()
    sp_g2 = nc.dram_tensor("sp_g2", [HI, L], F32, kind=spkind).ap()
    sp_o = nc.dram_tensor("sp_o", [HI, L], F32, kind=spkind).ap()
    x1_h = nc.dram_tensor("x1_h", [L, D], F32).ap()
    bc_h = nc.dram_tensor("bc_h", [2 * NS, L], BF16).ap()
    sc_h = nc.dram_tensor("sc_h", [4, 32], F32,
                          kind="ExternalOutput" if taps else "Internal").ap()
    r2s_h = nc.dram_tensor("r2s_h", [16, 128], F32,
                           kind="ExternalOutput" if taps else "Internal").ap()
    cc_bufs = [(nc.dram_tensor("cci%d" % i, [1, 8], F32).ap(),
                nc.dram_tensor("cco%d" % i, [1, 8], F32).ap())
               for i in range(4)]
    scb_h = (nc.dram_tensor("scb_h", [128, 12], F32, kind="ExternalOutput").ap()
             if taps else None)
    xq_h = (nc.dram_tensor("xq_h", [128, NFC * L], BF16,
                           kind="ExternalOutput").ap() if taps else None)
    tap = {}
    if taps:
        for nm, shp in (("tap_u", [DI, L]), ("tap_delta", [DI, L]),
                        ("tap_ymod", [DI, L]), ("tap_x1", [L, D])):
            tap[nm] = nc.dram_tensor(nm, shp, F32, kind="ExternalOutput").ap()

    stack = ExitStack()
    with stack:
        c = Ctx(nc, stack)

        def sb(name, shape, dt=F32):
            return stack.enter_context(nc.sbuf_tensor(name, shape, dt))

        pt = [stack.enter_context(nc.psum_tensor("ps%d" % i, [128, 512], F32))
              for i in range(8)]

        A_sb = sb("A_sb", [128, NKC * NS])
        cw_sb = sb("cw_sb", [128, KC * NKC])
        cb_sb = sb("cb_sb", [128, NKC])
        dv_sb = sb("dv_sb", [128, NKC])
        bdt_sb = sb("bdt_sb", [128, NKC])
        bin_sb = sb("bin_sb", [128, 2 * NKC])
        gam_sb = sb("gam_sb", [128, NFC])
        bet_sb = sb("bet_sb", [128, NFC])
        bout_b = sb("bout_b", [128, D], BF16)
        I_bf = sb("I_bf", [128, 128], BF16)
        I_f32 = sb("I_f32", [128, 128])
        state = sb("state", [128, NKC * NS])
        scal = sb("scal", [1, 32])
        scb = sb("scb", [128, 12])
        sca = sb("sca", [128, 8])
        scg = sb("scg", [128, 8])
        zero8 = sb("zero8", [1, 8])
        wx_bf = sb("wx_bf", [128, NKC * 128], BF16)
        wdt_bf = sb("wdt_bf", [64, DI], BF16)
        stat = sb("stat", [128, 64])
        sqd = sb("sqd", [128, D])
        mxu = sb("mxu", [128, max(16, NIC * NT2)])
        mxg = sb("mxg", [128, NIC * NT2])
        mx4 = sb("mx4", [128, NIC * NT2])
        mx5 = sb("mx5", [128, NIC * NT2])

        c.dma(A_sb[:, :].rearrange("p (k n) -> p k n", n=NS),
              a_log.rearrange("(k p) n -> p k n", p=128))
        c.dma(cw_sb[:, :].rearrange("p (i k) -> p i k", i=KC),
              conv_w.rearrange("i (k p) -> p i k", p=128), slow=True)
        c.dma(cb_sb[:, :], conv_b.rearrange("(k p) -> p k", p=128), slow=True)
        c.dma(dv_sb[:, :], d_vec.rearrange("(k p) -> p k", p=128), slow=True)
        c.dma(bdt_sb[:, :], b_dt.rearrange("(k p) -> p k", p=128), slow=True)
        c.dma(bin_sb[:, :], b_in.rearrange("(k p) -> p k", p=128), slow=True)
        c.dma(gam_sb[:, :], g_ln.rearrange("(k p) -> p k", p=128), slow=True)
        c.dma(bet_sb[:, :], b_ln.rearrange("(k p) -> p k", p=128), slow=True)
        c.dma(bout_b[:, :],
              b_out.rearrange("(o f) -> o f", o=1).broadcast_to((128, D)),
              eng="gpsimd")
        c.g(lambda e: e.memset(wx_bf[:, :], 0.0))
        c.wait("gpsimd", "sg")
        c.dma(wx_bf[:, :].rearrange("p (k j) -> p k j", j=128)[:, :, 0:RK],
              w_x.rearrange("(k p) j -> p k j", p=128)[:, :, 0:RK],
              eng="gpsimd")
        c.dma(wx_bf[:, :].rearrange("p (k j) -> p k j", j=128)[:, :, 64:64 + NS],
              w_x.rearrange("(k p) j -> p k j", p=128)[:, :, RK:RK + NS],
              eng="gpsimd")
        c.dma(wx_bf[:, :].rearrange("p (k j) -> p k j", j=128)[:, :, 96:96 + NS],
              w_x.rearrange("(k p) j -> p k j", p=128)[:, :, RK + NS:RK + 2 * NS],
              eng="gpsimd")
        c.dma(wdt_bf[0:64, :], w_dt[:, :], eng="gpsimd")
        c.wait("scalar", "dh")
        c.s(lambda e: e.activation(A_sb[:, :], A_sb[:, :], ACT.Exp))
        c.wait("vector", "ss")
        c.v(lambda e: e.tensor_scalar_mul(bdt_sb[:, :], bdt_sb[:, :], -1.0))
        c.g(lambda e: e.iota(I_f32[:, :], [[1, 128]], base=0,
                             channel_multiplier=0,
                             allow_small_or_imprecise_dtypes=True))
        c.g(lambda e: e.iota(scb[:, 9:10], [[0, 1]], base=0,
                             channel_multiplier=1,
                             allow_small_or_imprecise_dtypes=True))
        mg = c.mark("sg")
        c.wait_marks("vector", mg)
        c.v(lambda e: e.tensor_tensor(I_f32[:, :], I_f32[:, :],
                                      scb[:, 9:10].broadcast_to((128, 128)),
                                      ALU.is_equal))
        c.selfw("vector")
        c.v(lambda e: e.tensor_copy(I_bf[:, :], I_f32[:, :]))
        c.g(lambda e: e.memset(state[:, :], 0.0))
        c.g(lambda e: e.memset(zero8[:, :], 0.0))
        c.g(lambda e: e.memset(scal[:, :], 0.0))
        c.g(lambda e: e.memset(scb[:, 10:12], 0.0))
        c.g(lambda e: e.memset(sca[:, :], 0.0))
        c.g(lambda e: e.memset(scg[:, :], 0.0))
        c.barrier()

        r2s_idx = [0]

        def reduce_to_all(src_ap, dst_col):
            idx = r2s_idx[0] % 16
            r2s_idx[0] += 1
            c.selfw("vector")
            c.use("vector", "stat63")
            c.v(lambda e: e.tensor_reduce(stat[:, 63:64], src_ap, AX.X,
                                          ALU.max, apply_absolute_value=True))
            mv = c.mark("sv")
            c.wait_marks("sync", mv)
            c.dma(r2s_h[idx, :].rearrange("(p o) -> p o", o=1),
                  stat[:, 63:64])
            c.setg("stat63", "dh")
            c.lanewait("sync", "dh")
            c.dma(sqd[:, 0:128],
                  r2s_h[idx, :].rearrange("(o f) -> o f", o=1)
                  .broadcast_to((128, 128)))
            c.lanewait("vector", "dh")
            c.v(lambda e, d=dst_col: e.tensor_reduce(
                sca[:, d:d + 1], sqd[:, 0:128], AX.X, ALU.max,
                apply_absolute_value=True))
            c.selfw("vector")

        def collective(idx, n_in):
            cci, cco = cc_bufs[idx]
            if DBG["no_collective"]:
                c.v(lambda e, n=n_in: e.tensor_copy(scg[:, 0:n],
                                                    sca[:, 0:n]))
                c.selfw("vector")
                return
            mv = c.mark("sv")
            c.wait_marks("sync", mv)
            c.dma(cci[0:1, 0:8], zero8[0:1, 0:8])
            c.lanewait("sync", "dh")
            c.dma(cci[0:1, 0:n_in], sca[0:1, 0:n_in])
            c.lanewait("gpsimd", "dh")
            c.run("gpsimd", lambda e: e.collective_compute(
                "AllReduce", ALU.max,
                replica_groups=[list(range(n_cores))],
                ins=[cci.opt()], outs=[cco.opt()]), "cc")
            c.wait("sync", "cc")
            c.dma(scg[:, 0:n_in],
                  cco[0:1, 0:n_in].broadcast_to((128, n_in)))
            c.lanewait("vector", "dh")

        def ln_rstd(n):
            c.selfw("scalar")
            c.s(lambda e: e.nop())
            c.wait_marks("vector", c.mark("ss"))
            c.selfw("vector")
            c.v(lambda e: e.tensor_scalar_mul(stat[:, 16:16 + n],
                                              stat[:, 0:n], 1.0 / D))
            c.v(lambda e: e.tensor_scalar_mul(stat[:, 32:32 + n],
                                              stat[:, 8:8 + n], 1.0 / D))
            c.selfw("vector")
            c.v(lambda e: e.tensor_tensor(stat[:, 40:40 + n],
                                          stat[:, 16:16 + n],
                                          stat[:, 16:16 + n], ALU.mult))
            c.selfw("vector")
            c.v(lambda e: e.tensor_tensor(stat[:, 40:40 + n],
                                          stat[:, 32:32 + n],
                                          stat[:, 40:40 + n], ALU.subtract))
            c.selfw("vector")
            c.v(lambda e: e.tensor_scalar_add(stat[:, 40:40 + n],
                                              stat[:, 40:40 + n], EPS))
            mv = c.mark("sv")
            c.wait_marks("scalar", mv)
            c.s(lambda e: e.activation(stat[:, 48:48 + n],
                                       stat[:, 40:40 + n], ACT.Sqrt))
            c.selfw("scalar")
            c.s(lambda e: e.nop())
            ms = c.mark("ss")
            c.wait_marks("vector", ms)
            c.v(lambda e: e.reciprocal(stat[:, 24:24 + n], stat[:, 48:48 + n]))
            c.selfw("vector")
            c.v(lambda e: e.tensor_tensor(stat[:, 56:56 + n],
                                          stat[:, 40:40 + n],
                                          stat[:, 24:24 + n], ALU.mult))
            c.selfw("vector")
            c.v(lambda e: e.tensor_tensor(stat[:, 56:56 + n],
                                          stat[:, 56:56 + n],
                                          stat[:, 24:24 + n], ALU.mult))
            c.selfw("vector")
            c.v(lambda e: e.tensor_scalar(stat[:, 56:56 + n],
                                          stat[:, 56:56 + n], -0.5, 1.5,
                                          ALU.mult, ALU.add))
            c.selfw("vector")
            c.v(lambda e: e.tensor_tensor(stat[:, 24:24 + n],
                                          stat[:, 24:24 + n],
                                          stat[:, 56:56 + n], ALU.mult))
            c.selfw("vector")

        class _Cut(Exception):
            pass

        try:
            p0scope = ExitStack()
            with p0scope:
                _e = p0scope.enter_context
                raw = _e(nc.sbuf_tensor("raw", [128, 4 * 4096], F32))
                dqs = _e(nc.sbuf_tensor("dqs", [128, 4 * 128], F32))
                dqr = _e(nc.sbuf_tensor("dqr", [128, 4 * 128], F32))
                trb = _e(nc.sbuf_tensor("trb", [128, 8 * 128], BF16))
                cstb = _e(nc.sbuf_tensor("cstb", [128, 2 * 4096], BF16))
                p0lanes = ("dw0", "dw1", "dx0", "dx1",
                           "db0", "db1", "ds0", "ds1")
                casts = [("c", (w_in, win_h, rc, 2 * DI)) for rc in range(NFC)] + \
                        [("c", (w_out, wout_h, rc, D)) for rc in range(NKC)]
                groups = [("g", (w_up, wupT_h, 4 * g, 4, D)) for g in range(6)] + \
                         [("g", (w_gate, wgtT_h, 4 * g, 4, D)) for g in range(6)] + \
                         [("g", (w_down, wdnT_h, oc, 1, HI)) for oc in range(8)]
                seq = []
                while casts or groups:
                    if casts:
                        seq.append(casts.pop(0))
                    if groups:
                        seq.append(groups.pop(0))
                tri = 0
                for idx, (kind, item) in enumerate(seq):
                    s4 = idx % 4
                    ldl = p0lanes[idx % 8]
                    if kind == "c":
                        src, dst, rc, NCOL = item
                        sl = idx % 2
                        c.use("sync", ("p0raw", s4))
                        c.dma(raw[:, s4 * 4096: s4 * 4096 + NCOL],
                              src[rc * 128:(rc + 1) * 128, :], lane=ldl)
                        c.lanewait("vector", ldl)
                        c.use("vector", ("p0cst", sl))
                        c.v(lambda e, s4=s4, sl=sl, NCOL=NCOL: e.tensor_copy(
                            cstb[:, sl * 4096: sl * 4096 + NCOL],
                            raw[:, s4 * 4096: s4 * 4096 + NCOL]))
                        c.setg(("p0raw", s4), "sv")
                        mv = c.mark("sv")
                        c.wait_marks("scalar", mv)
                        ln = "dp%d" % sl
                        c.dma(dst[rc * 128:(rc + 1) * 128, :],
                              cstb[:, sl * 4096: sl * 4096 + NCOL],
                              eng="scalar", lane=ln)
                        c.guards[("p0cst", sl)] = {ln: c.cnt[ln]}
                        continue
                    wsrc, wdst, oc0, GRP, I = item
                    gw = GRP * I
                    NB = gw // 32
                    rawt = raw[:, s4 * 4096: s4 * 4096 + gw]
                    sct = dqs[:, s4 * 128: s4 * 128 + NB]
                    rct = dqr[:, s4 * 128: s4 * 128 + NB]
                    c.use("sync", ("p0raw", s4))
                    if GRP == 1:
                        c.dma(rawt, wsrc[oc0 * 128:(oc0 + 1) * 128, :],
                              lane=ldl)
                    else:
                        c.dma(rawt.rearrange("p (o i) -> p o i", o=GRP),
                              wsrc[oc0 * 128:(oc0 + GRP) * 128, :]
                              .rearrange("(o p) i -> p o i", p=128),
                              lane=ldl)
                    c.lanewait("vector", ldl)
                    c.use("vector", ("p0dq", s4))
                    c.v(lambda e, r=rawt, s=sct: e.tensor_reduce(
                        s, r.rearrange("p (b k) -> p b k", k=32), AX.X,
                        ALU.max, apply_absolute_value=True))
                    c.selfw("vector")
                    c.v(lambda e, s=sct: e.tensor_scalar(
                        s, s, 1.0 / 7.0, 1e-12, ALU.mult, ALU.max))
                    c.selfw("vector")
                    c.v(lambda e, s=sct, r=rct: e.reciprocal(r, s))
                    c.selfw("vector")
                    c.v(lambda e, r=rawt, rc_=rct, NB=NB: e.tensor_tensor(
                        r.rearrange("p (b k) -> p b k", k=32),
                        r.rearrange("p (b k) -> p b k", k=32),
                        rc_.rearrange("p (b o) -> p b o", o=1)
                        .broadcast_to((128, NB, 32)), ALU.mult))
                    c.v(lambda e, r=rawt: e.tensor_scalar(
                        r, r, MAGIC, MAGIC, ALU.add, ALU.subtract))
                    c.v(lambda e, r=rawt, s=sct, NB=NB: e.tensor_tensor(
                        r.rearrange("p (b k) -> p b k", k=32),
                        r.rearrange("p (b k) -> p b k", k=32),
                        s.rearrange("p (b o) -> p b o", o=1)
                        .broadcast_to((128, NB, 32)), ALU.mult))
                    mwq = c.mark("sv")
                    c.setg(("p0dq", s4), "sv")
                    NBT = I // 128
                    for b in range(gw // 128):
                        ocl, bb = divmod(b, NBT)
                        psl = tri % 8
                        tsl = tri % 8
                        c.wait_marks("tensor", mwq)
                        c.use("tensor", ("ps", psl))
                        c.t(lambda e, b=b, w=rawt, psl=psl: e.transpose(
                            pt[psl][:, 0:128], w[:, b * 128:(b + 1) * 128],
                            I_f32[:, :]))
                        mt = c.mark("stn")
                        c.wait_marks("scalar", mt)
                        c.use("scalar", ("p0trb", tsl))
                        c.s(lambda e, psl=psl, tsl=tsl: e.activation(
                            trb[:, tsl * 128:(tsl + 1) * 128],
                            pt[psl][:, 0:128], ACT.Copy))
                        c.setg(("ps", psl), "ss")
                        c.selfw("scalar")
                        c.s(lambda e: e.nop())
                        ln = ("dt0", "dt1", "dst", "dh")[tsl % 4]
                        c.dma(wdst[bb * 128:(bb + 1) * 128,
                                   (oc0 + ocl) * 128:(oc0 + ocl + 1) * 128],
                              trb[:, tsl * 128:(tsl + 1) * 128],
                              eng="scalar", lane=ln)
                        c.guards[("p0trb", tsl)] = {ln: c.cnt[ln]}
                        tri += 1
                    c.setg(("p0raw", s4), "stn")
            c.barrier()
            if cut == 2:
                raise _Cut()

            mscope = ExitStack()
            with mscope:
                _e = mscope.enter_context
                xbuf = _e(nc.sbuf_tensor("xbuf", [128, LNB * D], F32))
                xnp = _e(nc.sbuf_tensor("xnp", [128, D], F32))
                xn_fm = _e(nc.sbuf_tensor("xn_fm", [128, NFC * TH], BF16))
                u0 = _e(nc.sbuf_tensor("u0", [128, NKC * (TH + 3)], BF16))
                delta = _e(nc.sbuf_tensor("delta", [128, NKC * TH], BF16))
                dt_sb = _e(nc.sbuf_tensor("dt_sb", [64, TH], BF16))
                bcB_sb = _e(nc.sbuf_tensor("bcB_sb", [16, TH], BF16))
                bcC_sb = _e(nc.sbuf_tensor("bcC_sb", [16, TH], BF16))
                wint = _e(nc.sbuf_tensor("wint", [128, 2 * 1024], BF16))
                srstg = _e(nc.sbuf_tensor("srstg", [128, 2 * TTL], BF16))
                acc2 = _e(nc.sbuf_tensor("acc2", [128, TH], BF16))
                tails = _e(nc.sbuf_tensor("tails", [128, NKC * 3], BF16))
                gkt = _e(nc.sbuf_tensor("gkt", [128, 4 * TH], BF16))
                srkg = _e(nc.sbuf_tensor("srkg", [128, 2 * TH], BF16))
                Bb = _e(nc.sbuf_tensor("Bb", [128, 3 * TH], BF16))
                Cb = _e(nc.sbuf_tensor("Cb", [128, 3 * TH], BF16))
                ab = _e(nc.sbuf_tensor("ab", [128, 3 * TH], F32))
                bb = _e(nc.sbuf_tensor("bb", [128, 3 * TH], BF16))
                hb = _e(nc.sbuf_tensor("hb", [128, 2 * TH], BF16))
                wout_bf = _e(nc.sbuf_tensor("wout_bf", [128, NKC * D], BF16))
                xb2 = _e(nc.sbuf_tensor("xb2", [128, D], F32))

                def u0k(kc):
                    return u0[:, kc * (TH + 3): kc * (TH + 3) + TH + 3]

                for h in range(2):
                    t0 = h * TH
                    mxn_tt = {}
                    for tcg in range(NTC // LNB):
                        for tc2 in range(LNB):
                            tc = tcg * LNB + tc2
                            c.use("sync", ("xbuf", tc2))
                            c.dma(xbuf[:, tc2 * D:(tc2 + 1) * D],
                                  xT[t0 + tc * 128: t0 + (tc + 1) * 128, :],
                                  lane="dx%d" % tc2)
                            c.lanewait("vector", "dx%d" % tc2)
                            c.v(lambda e, tc2=tc2: e.tensor_reduce(
                                stat[:, tc2:tc2 + 1],
                                xbuf[:, tc2 * D:(tc2 + 1) * D], AX.X, ALU.add))
                            c.lanewait("scalar", "dx%d" % tc2)
                            c.s(lambda e, tc2=tc2: e.activation(
                                sqd[:, :], xbuf[:, tc2 * D:(tc2 + 1) * D],
                                ACT.Square, accum_out=stat[:, 8 + tc2: 9 + tc2]))
                        ms = c.mark("ss")
                        c.wait_marks("vector", ms)
                        ln_rstd(LNB)
                        for tc2 in range(LNB):
                            tc = tcg * LNB + tc2
                            c.use("vector", ("xnp", 0))
                            c.v(lambda e, tc2=tc2: e.tensor_scalar(
                                xnp[:, :],
                                xbuf[:, tc2 * D:(tc2 + 1) * D],
                                stat[:, 16 + tc2:17 + tc2],
                                stat[:, 24 + tc2:25 + tc2],
                                ALU.subtract, ALU.mult))
                            c.setg(("xbuf", tc2), "sv")
                            mv = c.mark("sv")
                            for fc in range(NFC):
                                psl = 6 + fc % 2
                                c.wait_marks("tensor", mv)
                                c.use("tensor", ("ps", psl))
                                c.t(lambda e, fc=fc, psl=psl: e.transpose(
                                    pt[psl][:, 0:128],
                                    xnp[:, fc * 128:(fc + 1) * 128],
                                    I_f32[:, :]))
                                mt = c.mark("stn")
                                c.wait_marks("vector", mt)
                                c.v(lambda e, fc=fc, tc=tc, psl=psl: e.tensor_scalar(
                                    xn_fm[:, fc * TH + tc * 128:
                                          fc * TH + (tc + 1) * 128],
                                    pt[psl][:, 0:128], gam_sb[:, fc:fc + 1],
                                    bet_sb[:, fc:fc + 1], ALU.mult, ALU.add))
                                c.setg(("ps", psl), "sv")
                            c.setg(("xnp", 0), "stn")
                        done = (tcg + 1) * LNB * 128
                        for tt in range(NTT):
                            if tt not in mxn_tt and done >= (tt + 1) * TTL:
                                mxn_tt[tt] = c.mark("sv")
                    for jc in range(2 * NKC):
                        wsl = jc % 2
                        c.use("sync", ("wint", wsl))
                        c.dma(wint[:, wsl * 1024:(wsl + 1) * 1024]
                              .rearrange("p (k j) -> p k j", k=NFC),
                              win_h[:, jc * 128:(jc + 1) * 128]
                              .rearrange("(k p) j -> p k j", p=128),
                              lane="dw%d" % wsl)
                        for tt in range(NTT):
                            psl = (jc * NTT + tt) % 4
                            c.use("tensor", ("ps", psl))
                            c.lanewait("tensor", "dw%d" % wsl)
                            c.wait_marks("tensor", mxn_tt[tt])
                            for kc in range(NFC):
                                c.t(lambda e, kc=kc, tt=tt, wsl=wsl, psl=psl:
                                    e.matmul(pt[psl][:, 0:TTL],
                                             wint[:, wsl * 1024 + kc * 128:
                                                  wsl * 1024 + (kc + 1) * 128],
                                             xn_fm[:, kc * TH + tt * TTL:
                                                   kc * TH + (tt + 1) * TTL],
                                             start=(kc == 0),
                                             stop=(kc == NFC - 1)))
                            mt = c.mark("stn")
                            c.wait_marks("scalar", mt)
                            if jc < NKC:
                                c.s(lambda e, jc=jc, tt=tt, psl=psl: e.activation(
                                    u0[:, jc * (TH + 3) + 3 + tt * TTL:
                                       jc * (TH + 3) + 3 + (tt + 1) * TTL],
                                    pt[psl][:, 0:TTL], ACT.Identity,
                                    bias=bin_sb[:, jc:jc + 1]))
                                c.setg(("ps", psl), "ss")
                            else:
                                ssl = (jc * NTT + tt) % 2
                                c.use("scalar", ("srstg", ssl))
                                c.s(lambda e, jc=jc, psl=psl, ssl=ssl: e.activation(
                                    srstg[:, ssl * TTL:(ssl + 1) * TTL],
                                    pt[psl][:, 0:TTL], ACT.Silu,
                                    bias=bin_sb[:, jc:jc + 1]))
                                c.setg(("ps", psl), "ss")
                                msl = c.mark("ss")
                                c.wait_marks("sync", msl)
                                lnr = "dr%d" % ssl
                                c.dma(sres_h[(jc - NKC) * 128:
                                             (jc - NKC + 1) * 128,
                                             t0 + tt * TTL:
                                             t0 + (tt + 1) * TTL],
                                      srstg[:, ssl * TTL:(ssl + 1) * TTL],
                                      lane=lnr)
                                c.guards[("srstg", ssl)] = {lnr: c.cnt[lnr]}
                        c.setg(("wint", wsl), "stn")
                    msu = c.mark("ss")
                    c.wait_marks("vector", {"ss": msu["ss"]})
                    if h == 0:
                        for kc in range(NKC):
                            c.v(lambda e, kc=kc: e.memset(u0k(kc)[:, 0:3], 0.0))
                    else:
                        for kc in range(NKC):
                            c.v(lambda e, kc=kc: e.tensor_copy(
                                u0k(kc)[:, 0:3], tails[:, kc * 3:kc * 3 + 3]))
                    c.selfw("vector")
                    for kc in range(NKC):
                        if h == 0:
                            c.v(lambda e, kc=kc: e.tensor_copy(
                                tails[:, kc * 3:kc * 3 + 3],
                                u0k(kc)[:, TH:TH + 3]))
                        c.v(lambda e, kc=kc: e.tensor_scalar(
                            acc2[:, 0:TH], u0k(kc)[:, 0:TH],
                            cw_sb[:, 0 * NKC + kc:0 * NKC + kc + 1],
                            cb_sb[:, kc:kc + 1], ALU.mult, ALU.add))
                        c.v(lambda e, kc=kc: e.scalar_tensor_tensor(
                            acc2[:, 0:TH], u0k(kc)[:, 1:TH + 1],
                            cw_sb[:, 1 * NKC + kc:1 * NKC + kc + 1],
                            acc2[:, 0:TH], ALU.mult, ALU.add))
                        c.v(lambda e, kc=kc: e.scalar_tensor_tensor(
                            acc2[:, 0:TH], u0k(kc)[:, 2:TH + 2],
                            cw_sb[:, 2 * NKC + kc:2 * NKC + kc + 1],
                            acc2[:, 0:TH], ALU.mult, ALU.add))
                        c.v(lambda e, kc=kc: e.scalar_tensor_tensor(
                            u0k(kc)[:, 3:TH + 3], u0k(kc)[:, 3:TH + 3],
                            cw_sb[:, 3 * NKC + kc:3 * NKC + kc + 1],
                            acc2[:, 0:TH], ALU.mult, ALU.add))
                        mv = c.mark("sv")
                        c.wait_marks("scalar", mv)
                        c.s(lambda e, kc=kc: e.activation(
                            u0k(kc)[:, 3:TH + 3], u0k(kc)[:, 3:TH + 3], ACT.Silu))
                    if taps:
                        c.wait_marks("gpsimd", c.mark("ss"))
                        for kc in range(NKC):
                            c.dma(tap["tap_u"][kc * 128:(kc + 1) * 128,
                                               t0:t0 + TH], u0k(kc)[:, 3:TH + 3],
                                  eng="gpsimd")
                    msu = c.mark("ss")
                    c.wait_marks("tensor", msu)
                    for tt in range(NTT):
                        psl = 4 + tt % 2
                        c.use("tensor", ("ps", psl))
                        for kc in range(NKC):
                            c.t(lambda e, kc=kc, tt=tt, psl=psl: e.matmul(
                                pt[psl][:, 0:TTL],
                                wx_bf[:, kc * 128:(kc + 1) * 128],
                                u0k(kc)[:, 3 + tt * TTL: 3 + (tt + 1) * TTL],
                                start=(kc == 0), stop=(kc == NKC - 1)))
                        mt = c.mark("stn")
                        c.wait_marks("vector", mt)
                        c.v(lambda e, tt=tt, psl=psl: e.tensor_copy(
                            dt_sb[0:64, tt * TTL:(tt + 1) * TTL],
                            pt[psl][0:64, 0:TTL]))
                        c.v(lambda e, tt=tt, psl=psl: e.tensor_scalar_mul(
                            bcB_sb[0:NS, tt * TTL:(tt + 1) * TTL],
                            pt[psl][64:64 + NS, 0:TTL], -1.0))
                        c.v(lambda e, tt=tt, psl=psl: e.tensor_copy(
                            bcC_sb[0:NS, tt * TTL:(tt + 1) * TTL],
                            pt[psl][96:96 + NS, 0:TTL]))
                        c.setg(("ps", psl), "sv")
                    c.wait_marks("sync", c.mark("sv"))
                    c.dma(bc_h[0:NS, t0:t0 + TH], bcB_sb[0:NS, :])
                    c.dma(bc_h[NS:2 * NS, t0:t0 + TH], bcC_sb[0:NS, :])
                    mdt = c.mark("sv")
                    c.wait_marks("tensor", mdt)
                    for jc in range(NKC):
                        for tt in range(NTT):
                            psl = (jc * NTT + tt) % 4
                            c.use("tensor", ("ps", psl))
                            c.t(lambda e, jc=jc, tt=tt, psl=psl: e.matmul(
                                pt[psl][:, 0:TTL],
                                wdt_bf[0:64, jc * 128:(jc + 1) * 128],
                                dt_sb[0:64, tt * TTL:(tt + 1) * TTL],
                                start=True, stop=True))
                            mt = c.mark("stn")
                            c.wait_marks("scalar", mt)
                            c.s(lambda e, jc=jc, tt=tt, psl=psl: e.activation(
                                delta[:, jc * TH + tt * TTL:
                                      jc * TH + (tt + 1) * TTL],
                                pt[psl][:, 0:TTL], ACT.Sigmoid,
                                bias=bdt_sb[:, jc:jc + 1], scale=-1.0))
                            c.setg(("ps", psl), "ss")
                    for jc in range(NKC):
                        c.s(lambda e, jc=jc: e.activation(
                            delta[:, jc * TH:(jc + 1) * TH],
                            delta[:, jc * TH:(jc + 1) * TH], ACT.Ln))
                    c.selfw("scalar")
                    c.s(lambda e: e.nop())
                    if taps:
                        c.wait_marks("gpsimd", c.mark("ss"))
                        for kc in range(NKC):
                            c.dma(tap["tap_delta"][kc * 128:(kc + 1) * 128,
                                                   t0:t0 + TH],
                                  delta[:, kc * TH:(kc + 1) * TH], eng="gpsimd")
                    if cut == 3:
                        mscope.close()
                        raise _Cut()
                    for kc in range(NKC):
                        c.use("sync", ("woutld", kc))
                        c.dma(wout_bf[:, kc * D:(kc + 1) * D],
                              wout_h[kc * 128:(kc + 1) * 128, :])
                    c.lanewait("sync", "dh", "dr0", "dr1")
                    mdl = c.mark("ss")
                    c.wait_marks("vector", mdl)
                    ibb = [0]
                    for kg in range(NKC // 4):
                        ks = [kg * 4 + j for j in range(4)]
                        for j, k in enumerate(ks[:2]):
                            c.use("sync", ("srkg", j % 2))
                            c.dma(srkg[:, (j % 2) * TH:(j % 2 + 1) * TH],
                                  sres_h[k * 128:(k + 1) * 128, t0:t0 + TH])
                        for j, k in enumerate(ks):
                            c.use("vector", ("ypsum", j))
                            c.v(lambda e, j=j, k=k: e.tensor_tensor(
                                gkt[:, j * TH:(j + 1) * TH],
                                delta[:, k * TH:(k + 1) * TH],
                                u0k(k)[:, 3:TH + 3], ALU.mult))
                            b0 = ibb[0] % 3
                            ibb[0] += 1
                            c.use("vector", ("bb", b0))
                            c.v(lambda e, k=k, b0=b0: e.tensor_scalar_mul(
                                bb[:, b0 * TH:(b0 + 1) * TH],
                                u0k(k)[:, 3:TH + 3], dv_sb[:, k:k + 1]))
                            mv = c.mark("sv")
                            c.wait_marks("tensor", mv)
                            for tb in range(YB):
                                c.t(lambda e, j=j, tb=tb, b0=b0: e.matmul(
                                    pt[2 * j + tb][:, 0:min(512, TH)], I_bf[:, :],
                                    bb[:, b0 * TH + tb * 512:
                                       b0 * TH + tb * 512 + min(512, TH)],
                                    start=True, stop=False))
                            c.setg(("bb", b0), "stn")
                        for n in range(NS):
                            bsl = n % 3
                            bln = ("db0", "db1", "ds0")[bsl]
                            c.use("sync", ("bcb", bsl))
                            c.dma(Bb[:, bsl * TH:(bsl + 1) * TH],
                                  bc_h[n:n + 1, t0:t0 + TH]
                                  .broadcast_to((128, TH)), lane=bln)
                            c.dma(Cb[:, bsl * TH:(bsl + 1) * TH],
                                  bc_h[NS + n:NS + n + 1, t0:t0 + TH]
                                  .broadcast_to((128, TH)), lane=bln)
                            for j, k in enumerate(ks):
                                asl = (n * 4 + j) % 3
                                hsl = (n * 4 + j) % 2
                                kn = k * NS + n
                                c.use("scalar", ("ab", asl))
                                c.s(lambda e, k=k, kn=kn, asl=asl: e.activation(
                                    ab[:, asl * TH:(asl + 1) * TH],
                                    delta[:, k * TH:(k + 1) * TH], ACT.Exp,
                                    scale=A_sb[:, kn:kn + 1]))
                                msa = c.mark("ss")
                                b1 = ibb[0] % 3
                                ibb[0] += 1
                                c.use("vector", ("bb", b1))
                                c.lanewait("vector", bln)
                                c.v(lambda e, j=j, bsl=bsl, b1=b1:
                                    e.tensor_tensor(
                                        bb[:, b1 * TH:(b1 + 1) * TH],
                                        gkt[:, j * TH:(j + 1) * TH],
                                        Bb[:, bsl * TH:(bsl + 1) * TH],
                                        ALU.mult))
                                c.wait_marks("vector", msa)
                                c.use("vector", ("hb", hsl))
                                c.v(lambda e, kn=kn, asl=asl, hsl=hsl, b1=b1:
                                    e.tensor_tensor_scan(
                                        hb[:, hsl * TH:(hsl + 1) * TH],
                                        ab[:, asl * TH:(asl + 1) * TH],
                                        bb[:, b1 * TH:(b1 + 1) * TH],
                                        state[:, kn:kn + 1], ALU.mult,
                                        ALU.add))
                                c.setg(("ab", asl), "sv")
                                c.setg(("bb", b1), "sv")
                                b2 = ibb[0] % 3
                                ibb[0] += 1
                                c.use("vector", ("bb", b2))
                                c.v(lambda e, hsl=hsl, bsl=bsl, b2=b2:
                                    e.tensor_tensor(
                                        bb[:, b2 * TH:(b2 + 1) * TH],
                                        hb[:, hsl * TH:(hsl + 1) * TH],
                                        Cb[:, bsl * TH:(bsl + 1) * TH],
                                        ALU.mult))
                                if h == 0:
                                    mv = c.mark("sv")
                                    c.wait_marks("gpsimd", mv)
                                    c.g(lambda e, kn=kn, hsl=hsl: e.tensor_copy(
                                        state[:, kn:kn + 1],
                                        hb[:, hsl * TH + TH - 1: hsl * TH + TH]))
                                    c.setg(("hb", hsl), "sv", "sg")
                                else:
                                    c.setg(("hb", hsl), "sv")
                                mv = c.mark("sv")
                                c.wait_marks("tensor", mv)
                                for tb in range(YB):
                                    c.t(lambda e, j=j, tb=tb, n=n, b2=b2:
                                        e.matmul(
                                            pt[2 * j + tb][:, 0:min(512, TH)],
                                            I_bf[:, :],
                                            bb[:, b2 * TH + tb * 512:
                                               b2 * TH + tb * 512 +
                                               min(512, TH)],
                                            start=False, stop=(n == NS - 1)))
                                c.setg(("bb", b2), "stn")
                            c.setg(("bcb", bsl), "sv")
                        mt = c.mark("stn")
                        c.wait_marks("vector", mt)
                        for j, k in enumerate(ks):
                            c.wait("vector", "dh")
                            for tb in range(YB):
                                w = min(512, TH)
                                c.v(lambda e, j=j, k=k, tb=tb, w=w: e.tensor_tensor(
                                    delta[:, k * TH + tb * 512:
                                          k * TH + tb * 512 + w],
                                    pt[2 * j + tb][:, 0:w],
                                    srkg[:, (j % 2) * TH + tb * 512:
                                         (j % 2) * TH + tb * 512 + w], ALU.mult))
                            c.setg(("ypsum", j), "sv")
                            c.setg(("srkg", j % 2), "sv")
                            if j + 2 < 4:
                                c.use("sync", ("srkg", j % 2))
                                c.dma(srkg[:, (j % 2) * TH:(j % 2 + 1) * TH],
                                      sres_h[ks[j + 2] * 128:
                                             (ks[j + 2] + 1) * 128, t0:t0 + TH])
                    if taps:
                        c.wait_marks("gpsimd", c.mark("sv"))
                        for kc in range(NKC):
                            c.dma(tap["tap_ymod"][kc * 128:(kc + 1) * 128,
                                                  t0:t0 + TH],
                                  delta[:, kc * TH:(kc + 1) * TH], eng="gpsimd")
                    if cut == 4:
                        mscope.close()
                        raise _Cut()
                    mym = c.mark("sv")
                    c.wait_marks("tensor", mym)
                    c.lanewait("tensor", "dh")
                    for kc in range(NKC):
                        c.setg(("woutld", kc), "stn")
                    for tcg in range(NTC // LNB):
                        for tc2 in range(LNB):
                            tc = tcg * LNB + tc2
                            c.use("sync", ("xbuf", tc2))
                            c.dma(xbuf[:, tc2 * D:(tc2 + 1) * D],
                                  xT[t0 + tc * 128: t0 + (tc + 1) * 128, :],
                                  lane="dx%d" % tc2)
                            c.lanewait("vector", "dx%d" % tc2)
                            c.use("vector", "xb2")
                            c.v(lambda e, tc2=tc2: e.scalar_tensor_tensor(
                                xb2[:, :], xbuf[:, tc2 * D:(tc2 + 1) * D],
                                2.0, bout_b[:, :], ALU.mult, ALU.add))
                            for hc in range(2):
                                psl = (tc * 2 + hc) % 4
                                c.use("tensor", ("ps", psl))
                                for kc in range(NKC):
                                    c.t(lambda e, kc=kc, tc=tc, hc=hc, psl=psl:
                                        e.matmul(
                                            pt[psl][:, 0:512],
                                            delta[:, kc * TH + tc * 128:
                                                  kc * TH + (tc + 1) * 128],
                                            wout_bf[:, kc * D + hc * 512:
                                                    kc * D + (hc + 1) * 512],
                                            start=(kc == 0),
                                            stop=(kc == NKC - 1)))
                                mt = c.mark("stn")
                                c.wait_marks("vector", mt)
                                c.v(lambda e, tc2=tc2, hc=hc, psl=psl:
                                    e.tensor_tensor(
                                        xbuf[:, tc2 * D + hc * 512:
                                             tc2 * D + hc * 512 + 512],
                                        pt[psl][:, 0:512],
                                        xb2[:, hc * 512: hc * 512 + 512],
                                        ALU.add))
                                c.setg(("ps", psl), "sv")
                            c.setg("xb2", "sv")
                            c.selfw("vector")
                            c.v(lambda e, tc2=tc2: e.tensor_reduce(
                                stat[:, tc2:tc2 + 1],
                                xbuf[:, tc2 * D:(tc2 + 1) * D], AX.X, ALU.add))
                            mv = c.mark("sv")
                            c.wait_marks("scalar", mv)
                            c.s(lambda e, tc2=tc2: e.activation(
                                sqd[:, :], xbuf[:, tc2 * D:(tc2 + 1) * D],
                                ACT.Square, accum_out=stat[:, 8 + tc2: 9 + tc2]))
                        ms = c.mark("ss")
                        c.wait_marks("vector", ms)
                        ln_rstd(LNB)
                        for tc2 in range(LNB):
                            tc = tcg * LNB + tc2
                            c.use("vector", "x1t")
                            c.use("vector", ("xnp", 0))
                            c.v(lambda e, tc2=tc2: e.tensor_scalar(
                                xnp[:, :], xbuf[:, tc2 * D:(tc2 + 1) * D],
                                stat[:, 16 + tc2:17 + tc2],
                                stat[:, 24 + tc2:25 + tc2],
                                ALU.subtract, ALU.mult))
                            c.setg(("xbuf", tc2), "sv")
                            c.v(lambda e, tc=tc, h=h: e.tensor_reduce(
                                mxu[:, h * NTC + tc: h * NTC + tc + 1],
                                xnp[:, :], AX.X, ALU.max,
                                apply_absolute_value=True))
                            mv = c.mark("sv")
                            c.wait_marks("scalar", mv)
                            c.dma(x1_h[t0 + tc * 128: t0 + (tc + 1) * 128, :],
                                  xnp[:, :], eng="scalar", lane="dst")
                            if taps:
                                c.dma(tap["tap_x1"][t0 + tc * 128:
                                                    t0 + (tc + 1) * 128, :],
                                      xnp[:, :], eng="scalar", lane="dst")
                            c.guards["x1t"] = {"dst": c.cnt["dst"]}
                    c.barrier()

                if cut == 5:
                    mscope.close()
                    raise _Cut()
                reduce_to_all(mxu[:, 0:2 * NTC], 0)
                collective(0, 1)
                c.v(lambda e: e.tensor_scalar(scb[:, 0:1], scg[:, 0:1],
                                              1.0 / QMAX, 1e-12, ALU.mult,
                                              ALU.max))
                c.selfw("vector")
                c.v(lambda e: e.reciprocal(scb[:, 1:2], scb[:, 0:1]))
                c.selfw("vector")
            c.barrier()
            if cut == 55:
                raise _Cut()

            fscope = ExitStack()
            with fscope:
                _e = fscope.enter_context
                xq = _e(nc.sbuf_tensor("xq", [128, NFC * L], BF16))
                xr = _e(nc.sbuf_tensor("xr", [128, 2 * D], F32))
                xqp = _e(nc.sbuf_tensor("xqp", [128, 2 * D], F32))
                fwt = _e(nc.sbuf_tensor("fwt", [128, 4 * 1024], BF16))
                stg = _e(nc.sbuf_tensor("stg", [128, 8 * 512], F32))

                for tc in range(L // 128):
                    xsl = tc % 2
                    c.use("sync", ("xr", xsl))
                    c.dma(xr[:, xsl * D:(xsl + 1) * D],
                          x1_h[tc * 128:(tc + 1) * 128, :],
                          lane="dx%d" % xsl)
                    c.lanewait("vector", "dx%d" % xsl)
                    c.use("vector", ("xqp", xsl))
                    c.v(lambda e, xsl=xsl: e.tensor_scalar(
                        xr[:, xsl * D:(xsl + 1) * D],
                        xr[:, xsl * D:(xsl + 1) * D],
                        scb[:, 1:2], MAGIC, ALU.mult, ALU.add))
                    c.selfw("vector")
                    c.v(lambda e, xsl=xsl: e.tensor_scalar(
                        xqp[:, xsl * D:(xsl + 1) * D],
                        xr[:, xsl * D:(xsl + 1) * D],
                        MAGIC, scb[:, 0:1], ALU.subtract, ALU.mult))
                    c.setg(("xr", xsl), "sv")
                    mv = c.mark("sv")
                    for fc in range(NFC):
                        psl = 6 + fc % 2
                        c.wait_marks("tensor", mv)
                        c.use("tensor", ("ps", psl))
                        c.t(lambda e, fc=fc, xsl=xsl, psl=psl: e.transpose(
                            pt[psl][:, 0:128],
                            xqp[:, xsl * D + fc * 128: xsl * D + (fc + 1) * 128],
                            I_f32[:, :]))
                        mt = c.mark("stn")
                        c.wait_marks("scalar", mt)
                        c.s(lambda e, fc=fc, tc=tc, psl=psl: e.activation(
                            xq[:, fc * L + tc * 128: fc * L + (tc + 1) * 128],
                            pt[psl][:, 0:128], ACT.Copy))
                        c.setg(("ps", psl), "ss")
                    c.setg(("xqp", xsl), "stn")

                if taps:
                    c.wait_marks("gpsimd", c.mark("ss"))
                    c.dma(xq_h[:, :], xq[:, :], eng="gpsimd")
                if cut == 6:
                    fscope.close()
                    raise _Cut()
                mxq = c.mark("ss")
                spc = [0]
                for ic in range(NIC):
                    wsl = ic % 2
                    c.use("sync", ("fwt", wsl))
                    c.dma(fwt[:, wsl * 1024: wsl * 1024 + 1024]
                          .rearrange("p (fc i) -> p fc i", fc=NFC),
                          wupT_h[:, ic * 128:(ic + 1) * 128]
                          .rearrange("(fc p) i -> p fc i", p=128),
                          lane="dw%d" % wsl)
                    c.dma(fwt[:, (2 + wsl) * 1024:
                              (2 + wsl) * 1024 + 1024]
                          .rearrange("p (fc i) -> p fc i", fc=NFC),
                          wgtT_h[:, ic * 128:(ic + 1) * 128]
                          .rearrange("(fc p) i -> p fc i", p=128),
                          lane="dw%d" % wsl)
                    for tt in range(NT2):
                        for gi, (spill, mx) in enumerate(
                                ((sp_up, mxu), (sp_gate, mxg))):
                            psl = (tt * 2 + gi) % 4
                            c.use("tensor", ("ps", psl))
                            c.lanewait("tensor", "dw%d" % wsl)
                            c.wait_marks("tensor", mxq)
                            for fc in range(NFC):
                                c.t(lambda e, fc=fc, tt=tt, gi=gi, wsl=wsl,
                                    psl=psl: e.matmul(
                                        pt[psl][:, 0:TL2],
                                        fwt[:, (2 * gi + wsl) * 1024 + fc * 128:
                                            (2 * gi + wsl) * 1024 + (fc + 1) * 128],
                                        xq[:, fc * L + tt * TL2:
                                           fc * L + (tt + 1) * TL2],
                                        start=(fc == 0), stop=(fc == NFC - 1)))
                            mt = c.mark("stn")
                            ssl = (ic * NT2 * 2 + tt * 2 + gi) % 4
                            c.wait_marks("scalar", mt)
                            c.use("scalar", ("stg", ssl))
                            c.s(lambda e, psl=psl, ssl=ssl: e.activation(
                                stg[:, ssl * 512: ssl * 512 + TL2],
                                pt[psl][:, 0:TL2], ACT.Copy))
                            c.setg(("ps", psl), "ss")
                            c.selfw("scalar")
                            c.s(lambda e: e.nop())
                            ms = c.mark("ss")
                            c.wait_marks("vector", ms)
                            c.v(lambda e, ic=ic, tt=tt, ssl=ssl, mx=mx:
                                e.tensor_reduce(
                                    mx[:, ic * NT2 + tt: ic * NT2 + tt + 1],
                                    stg[:, ssl * 512: ssl * 512 + TL2], AX.X,
                                    ALU.max, apply_absolute_value=True))
                            lnp = ("dp0", "dp1", "dt0", "dt1", "ds0", "ds1",
                                   "du0", "du1")[spc[0] % 8]
                            spc[0] += 1
                            c.dma(spill[ic * 128:(ic + 1) * 128,
                                        tt * TL2:(tt + 1) * TL2],
                                  stg[:, ssl * 512: ssl * 512 + TL2],
                                  eng="scalar", lane=lnp)
                            c.guards[("stg", ssl)] = {lnp: c.cnt[lnp],
                                                      "sv": c.cnt["sv"]}
                    c.setg(("fwt", wsl), "stn")

                reduce_to_all(mxu[:, :], 0)
                reduce_to_all(mxg[:, :], 1)
                collective(1, 2)
                c.v(lambda e: e.tensor_scalar(scb[:, 2:3], scg[:, 0:1],
                                              1.0 / QMAX, 1e-12, ALU.mult,
                                              ALU.max))
                c.v(lambda e: e.tensor_scalar(scb[:, 4:5], scg[:, 1:2],
                                              1.0 / QMAX, 1e-12, ALU.mult,
                                              ALU.max))
                c.selfw("vector")
                c.v(lambda e: e.reciprocal(scb[:, 3:4], scb[:, 2:3]))
                c.v(lambda e: e.reciprocal(scb[:, 5:6], scb[:, 4:5]))
                c.selfw("vector")
                mv = c.mark("sv")
                c.wait_marks("scalar", mv)
                c.wait_marks("gpsimd", mv)

                fscope.close()
                f34 = ExitStack()
                _e34 = f34.enter_context
                gld = _e34(nc.sbuf_tensor("gld", [128, 4 * L], F32))
                uld = _e34(nc.sbuf_tensor("uld", [128, 4 * L], F32))
                g2o = _e34(nc.sbuf_tensor("g2o", [128, 4 * L], F32))
                gint = _e34(nc.sbuf_tensor("gint", [128, 2 * L], F32))
                sig = _e34(nc.sbuf_tensor("sig", [128, 2 * L], F32))
                ldlanes = ("ds0", "ds1", "dr0", "dr1")
                uplanes = ("du0", "du1", "db0", "db1")
                splanes = ("dp0", "dp1", "dt0", "dt1")

                c.lanewait("sync", "dp0", "dp1", "dt0", "dt1",
                           "ds0", "ds1", "du0", "du1")
                for ic in range(NIC):
                    s4 = ic % 4
                    s2_ = ic % 2
                    c.use("sync", ("gld", s4))
                    c.dma(gld[:, s4 * L:(s4 + 1) * L],
                          sp_gate[ic * 128:(ic + 1) * 128, :],
                          lane=ldlanes[s4])
                    c.lanewait("vector", ldlanes[s4])
                    c.use("vector", ("gint", s2_))
                    c.v(lambda e, s4=s4, s2_=s2_: e.tensor_scalar(
                        gint[:, s2_ * L:(s2_ + 1) * L],
                        gld[:, s4 * L:(s4 + 1) * L],
                        scb[:, 5:6], MAGIC, ALU.mult, ALU.add))
                    c.setg(("gld", s4), "sv")
                    c.selfw("vector")
                    c.v(lambda e, s2_=s2_: e.tensor_single_scalar(
                        gint[:, s2_ * L:(s2_ + 1) * L],
                        gint[:, s2_ * L:(s2_ + 1) * L], MAGIC,
                        ALU.subtract))
                    mv = c.mark("sv")
                    c.wait_marks("scalar", mv)
                    c.use("scalar", ("sig", s2_))
                    c.s(lambda e, s2_=s2_: e.activation(
                        sig[:, s2_ * L:(s2_ + 1) * L],
                        gint[:, s2_ * L:(s2_ + 1) * L], ACT.Sigmoid,
                        scale=scb[:, 4:5]))
                    c.selfw("scalar")
                    c.s(lambda e: e.nop())
                    ms = c.mark("ss")
                    c.setg(("gint", s2_), "ss")
                    c.wait_marks("vector", ms)
                    c.v(lambda e, s2_=s2_: e.tensor_scalar(
                        sig[:, s2_ * L:(s2_ + 1) * L],
                        sig[:, s2_ * L:(s2_ + 1) * L],
                        65536.0, MAGIC, ALU.mult, ALU.add))
                    c.selfw("vector")
                    c.v(lambda e, s2_=s2_: e.tensor_scalar(
                        sig[:, s2_ * L:(s2_ + 1) * L],
                        sig[:, s2_ * L:(s2_ + 1) * L],
                        MAGIC, 65535.0, ALU.subtract, ALU.min))
                    c.selfw("vector")
                    c.use("vector", ("g2o", s4))
                    c.v(lambda e, s4=s4, s2_=s2_: e.tensor_tensor(
                        g2o[:, s4 * L:(s4 + 1) * L],
                        gint[:, s2_ * L:(s2_ + 1) * L],
                        sig[:, s2_ * L:(s2_ + 1) * L], ALU.mult))
                    c.setg(("sig", s2_), "sv")
                    c.selfw("vector")
                    c.v(lambda e, ic=ic, s4=s4: e.tensor_reduce(
                        mx4[:, ic:ic + 1],
                        g2o[:, s4 * L:(s4 + 1) * L], AX.X,
                        ALU.max, apply_absolute_value=True))
                    mv = c.mark("sv")
                    c.wait_marks("scalar", mv)
                    c.dma(sp_g2[ic * 128:(ic + 1) * 128, :],
                          g2o[:, s4 * L:(s4 + 1) * L], eng="scalar",
                          lane=splanes[s4])
                    c.guards[("g2o", s4)] = {splanes[s4]: c.cnt[splanes[s4]],
                                             "sv": c.cnt["sv"]}

                c.wait_marks("vector", c.mark("sg"))
                reduce_to_all(mx4[:, 0:NIC], 0)
                c.v(lambda e: e.tensor_tensor(sca[:, 0:1], sca[:, 0:1],
                                              scb[:, 4:5], ALU.mult))
                c.selfw("vector")
                c.v(lambda e: e.tensor_scalar_mul(sca[:, 0:1], sca[:, 0:1],
                                                  1.0 / 65536.0))
                c.selfw("vector")
                collective(2, 1)
                c.v(lambda e: e.tensor_scalar(sca[:, 2:3], scg[:, 0:1],
                                              1.0 / QMAX, 1e-12, ALU.mult,
                                              ALU.max))
                c.selfw("vector")
                c.v(lambda e: e.reciprocal(sca[:, 3:4], sca[:, 2:3]))
                c.selfw("vector")
                c.v(lambda e: e.tensor_tensor(scb[:, 6:7], scb[:, 4:5],
                                              sca[:, 3:4], ALU.mult))
                c.selfw("vector")
                c.v(lambda e: e.tensor_scalar_mul(scb[:, 6:7], scb[:, 6:7],
                                                  1.0 / 65536.0))
                c.selfw("vector")
                mv = c.mark("sv")
                c.wait_marks("gpsimd", mv)

                c.lanewait("sync", "dp0", "dp1", "dt0", "dt1")
                for ic in range(NIC):
                    s4 = ic % 4
                    s2_ = ic % 2
                    c.use("sync", ("gld", s4))
                    c.dma(gld[:, s4 * L:(s4 + 1) * L],
                          sp_g2[ic * 128:(ic + 1) * 128, :],
                          lane=ldlanes[s4])
                    c.use("sync", ("uld", s4))
                    c.dma(uld[:, s4 * L:(s4 + 1) * L],
                          sp_up[ic * 128:(ic + 1) * 128, :],
                          lane=uplanes[s4])
                    c.lanewait("vector", ldlanes[s4])
                    c.use("vector", ("gint", s2_))
                    c.v(lambda e, s4=s4, s2_=s2_: e.tensor_scalar(
                        gint[:, s2_ * L:(s2_ + 1) * L],
                        gld[:, s4 * L:(s4 + 1) * L],
                        scb[:, 6:7], MAGIC, ALU.mult, ALU.add))
                    c.setg(("gld", s4), "sv")
                    c.lanewait("vector", uplanes[s4])
                    c.use("vector", ("sig", s2_))
                    c.v(lambda e, s4=s4, s2_=s2_: e.tensor_scalar(
                        sig[:, s2_ * L:(s2_ + 1) * L],
                        uld[:, s4 * L:(s4 + 1) * L],
                        scb[:, 3:4], MAGIC, ALU.mult, ALU.add))
                    c.setg(("uld", s4), "sv")
                    c.selfw("vector")
                    c.v(lambda e, s2_=s2_: e.tensor_single_scalar(
                        gint[:, s2_ * L:(s2_ + 1) * L],
                        gint[:, s2_ * L:(s2_ + 1) * L], MAGIC,
                        ALU.subtract))
                    c.v(lambda e, s2_=s2_: e.tensor_single_scalar(
                        sig[:, s2_ * L:(s2_ + 1) * L],
                        sig[:, s2_ * L:(s2_ + 1) * L], MAGIC,
                        ALU.subtract))
                    c.selfw("vector")
                    c.use("vector", ("g2o", s4))
                    c.v(lambda e, s4=s4, s2_=s2_: e.tensor_tensor(
                        g2o[:, s4 * L:(s4 + 1) * L],
                        gint[:, s2_ * L:(s2_ + 1) * L],
                        sig[:, s2_ * L:(s2_ + 1) * L], ALU.mult))
                    c.setg(("gint", s2_), "sv")
                    c.setg(("sig", s2_), "sv")
                    c.selfw("vector")
                    c.v(lambda e, ic=ic, s4=s4: e.tensor_reduce(
                        mx5[:, ic:ic + 1],
                        g2o[:, s4 * L:(s4 + 1) * L], AX.X,
                        ALU.max, apply_absolute_value=True))
                    mv = c.mark("sv")
                    c.wait_marks("scalar", mv)
                    c.dma(sp_o[ic * 128:(ic + 1) * 128, :],
                          g2o[:, s4 * L:(s4 + 1) * L], eng="scalar",
                          lane=splanes[s4])
                    c.guards[("g2o", s4)] = {splanes[s4]: c.cnt[splanes[s4]],
                                             "sv": c.cnt["sv"]}

                c.wait_marks("vector", c.mark("sg"))
                reduce_to_all(mx5[:, 0:NIC], 0)
                c.v(lambda e: e.tensor_tensor(sca[:, 0:1], sca[:, 0:1],
                                              sca[:, 2:3], ALU.mult))
                c.selfw("vector")
                c.v(lambda e: e.tensor_tensor(sca[:, 0:1], sca[:, 0:1],
                                              scb[:, 2:3], ALU.mult))
                c.selfw("vector")
                collective(3, 1)
                c.v(lambda e: e.tensor_scalar(scb[:, 8:9], scg[:, 0:1],
                                              1.0 / QMAX, 1e-12, ALU.mult,
                                              ALU.max))
                c.selfw("vector")
                c.v(lambda e: e.reciprocal(sca[:, 4:5], scb[:, 8:9]))
                c.selfw("vector")
                c.v(lambda e: e.tensor_tensor(scb[:, 7:8], sca[:, 2:3],
                                              scb[:, 2:3], ALU.mult))
                c.selfw("vector")
                c.v(lambda e: e.tensor_tensor(scb[:, 7:8], scb[:, 7:8],
                                              sca[:, 4:5], ALU.mult))
                c.selfw("vector")
                if taps:
                    c.wait_marks("sync", c.mark("sv"))
                    c.dma(scb_h[:, :], scb[:, :])
                f34.close()
            c.barrier()

            if cut == 9:
                raise _Cut()
            f5scope = ExitStack()
            with f5scope:
                _e = f5scope.enter_context
                wdn = _e(nc.sbuf_tensor("wdn", [128, NIC * D], BF16))
                oqt = _e(nc.sbuf_tensor("oqt", [128, 2 * NIC * TL2], BF16))
                ot = _e(nc.sbuf_tensor("ot", [128, 4 * 512], F32))
                pre3 = _e(nc.sbuf_tensor("pre3", [128, (TL2 // 128) * D], F32))
                x1r = _e(nc.sbuf_tensor("x1r", [128, 2 * D], F32))
                outt = _e(nc.sbuf_tensor("outt", [128, 2 * D], F32))
                c.lanewait("sync", "dp0", "dp1", "dt0", "dt1")
                for ic in range(NIC):
                    c.dma(wdn[:, ic * D:(ic + 1) * D],
                          wdnT_h[ic * 128:(ic + 1) * 128, :])
                f5lanes = ("ds0", "ds1", "du0", "du1")
                for tt in range(NT2):
                    oqb = (tt % 2) * NIC * TL2
                    c.use("vector", ("oqt", tt % 2))
                    for ic in range(NIC):
                        osl = ic % 4
                        c.use("sync", ("ot", osl))
                        c.dma(ot[:, osl * 512: osl * 512 + TL2],
                              sp_o[ic * 128:(ic + 1) * 128,
                                   tt * TL2:(tt + 1) * TL2],
                              lane=f5lanes[osl])
                        c.lanewait("vector", f5lanes[osl])
                        c.v(lambda e, osl=osl: e.tensor_scalar(
                            ot[:, osl * 512: osl * 512 + TL2],
                            ot[:, osl * 512: osl * 512 + TL2],
                            scb[:, 7:8], MAGIC, ALU.mult, ALU.add))
                        c.selfw("vector")
                        c.v(lambda e, ic=ic, osl=osl, oqb=oqb: e.tensor_scalar(
                            oqt[:, oqb + ic * TL2: oqb + (ic + 1) * TL2],
                            ot[:, osl * 512: osl * 512 + TL2],
                            MAGIC, scb[:, 8:9], ALU.subtract, ALU.mult))
                        c.setg(("ot", osl), "sv")
                    moq = c.mark("sv")
                    c.wait_marks("tensor", moq)
                    for tcl in range(TL2 // 128):
                        tc = tt * (TL2 // 128) + tcl
                        xsl = tc % 2
                        c.use("sync", ("x1r", xsl))
                        c.dma(x1r[:, xsl * D:(xsl + 1) * D],
                              x1_h[tc * 128:(tc + 1) * 128, :],
                              lane="dx%d" % xsl)
                        for hc in range(2):
                            psl = (tcl * 2 + hc) % 4
                            c.use("tensor", ("ps", psl))
                            if tt == 0 and tcl == 0 and hc == 0:
                                c.lanewait("tensor", "dh")
                            for ic in range(NIC):
                                c.t(lambda e, ic=ic, tcl=tcl, hc=hc, psl=psl,
                                    oqb=oqb: e.matmul(
                                        pt[psl][:, 0:512],
                                        oqt[:, oqb + ic * TL2 + tcl * 128:
                                            oqb + ic * TL2 + (tcl + 1) * 128],
                                        wdn[:, ic * D + hc * 512:
                                            ic * D + (hc + 1) * 512],
                                        start=(ic == 0), stop=(ic == NIC - 1)))
                            mt = c.mark("stn")
                            c.wait_marks("vector", mt)
                            c.lanewait("vector", "dx%d" % xsl)
                            c.v(lambda e, tcl=tcl, hc=hc, psl=psl, xsl=xsl:
                                e.tensor_tensor(
                                    pre3[:, tcl * D + hc * 512:
                                         tcl * D + hc * 512 + 512],
                                    pt[psl][:, 0:512],
                                    x1r[:, xsl * D + hc * 512:
                                        xsl * D + hc * 512 + 512], ALU.add))
                            c.setg(("ps", psl), "sv")
                        c.setg(("x1r", xsl), "sv")
                        c.selfw("vector")
                        c.v(lambda e, tcl=tcl: e.tensor_reduce(
                            stat[:, tcl:tcl + 1], pre3[:, tcl * D:(tcl + 1) * D],
                            AX.X, ALU.add))
                        mv = c.mark("sv")
                        c.wait_marks("scalar", mv)
                        c.s(lambda e, tcl=tcl: e.activation(
                            sqd[:, :], pre3[:, tcl * D:(tcl + 1) * D], ACT.Square,
                            accum_out=stat[:, 8 + tcl: 9 + tcl]))
                    ms = c.mark("ss")
                    c.wait_marks("vector", ms)
                    ln_rstd(TL2 // 128)
                    for tcl in range(TL2 // 128):
                        tc = tt * (TL2 // 128) + tcl
                        osl = tcl % 2
                        c.use("vector", ("outt", osl))
                        c.v(lambda e, tcl=tcl, osl=osl: e.tensor_scalar(
                            outt[:, osl * D:(osl + 1) * D],
                            pre3[:, tcl * D:(tcl + 1) * D],
                            stat[:, 16 + tcl:17 + tcl],
                            stat[:, 24 + tcl:25 + tcl],
                            ALU.subtract, ALU.mult))
                        mv = c.mark("sv")
                        c.wait_marks("scalar", mv)
                        c.dma(outT[tc * 128:(tc + 1) * 128, :],
                              outt[:, osl * D:(osl + 1) * D], eng="scalar",
                              lane="dst")
                        c.guards[("outt", osl)] = {"dst": c.cnt["dst"]}
                    c.setg(("oqt", tt % 2), "stn")

        except _Cut:
            pass
        if cut < 99:
            c.barrier()
            for tc in range(L // 128):
                c.dma(outT[tc * 128:(tc + 1) * 128, :], sqd[:, :])

        c.barrier()
        for ln in c.dma_lanes:
            c.wait("sync", ln)
        c.wait("gpsimd", "dg")

        with nc.Block() as blk:
            for eng in ENGINES:
                ops = c.q[eng]

                def make(ops):
                    def body(e):
                        for f in ops:
                            f(e)
                    return body
                getattr(blk, eng)(make(ops))

    return nc


_NC_CACHE = {}


def _get_nc(L, n_cores, taps=False, cut=99):
    key = (L, n_cores, taps, cut)
    if key not in _NC_CACHE:
        _NC_CACHE[key] = build_nc(L, n_cores, taps, cut=cut)
    return _NC_CACHE[key]


def make_in_maps(inputs, n_cores=8):
    f = lambda k: np.ascontiguousarray(np.asarray(inputs[k], dtype=np.float32))
    x = f("L0_attn_x")
    base = {
        "ln_g": f("L0_attn_ln_gamma"), "ln_b": f("L0_attn_ln_beta"),
        "w_in": f("L0_attn_W_in"), "b_in": f("L0_attn_b_in"),
        "conv_w": f("L0_attn_conv_w"), "conv_b": f("L0_attn_conv_b"),
        "w_x": f("L0_attn_W_x"), "w_dt": f("L0_attn_W_dt"),
        "b_dt": f("L0_attn_b_dt"), "a_log": f("L0_attn_A_log"),
        "d_vec": f("L0_attn_D"), "w_out": f("L0_attn_W_out"),
        "b_out": f("L0_attn_b_out"), "w_gate": f("L0_ffn_w_gate"),
        "w_up": f("L0_ffn_w_up"), "w_down": f("L0_ffn_w_down"),
    }
    return [dict(base, x=np.ascontiguousarray(x[i])) for i in range(n_cores)]


def kernel(**inputs):
    n_cores = 8
    in_maps = make_in_maps(inputs, n_cores)
    nc = _get_nc(in_maps[0]["x"].shape[0], n_cores)
    res = run_bass_kernel_spmd(nc, in_maps, core_ids=list(range(n_cores)))
    return np.stack([res.results[i]["out"] for i in range(n_cores)], axis=0)
